# version 02
# baseline (speedup 1.0000x reference)
import numpy as np
import concourse.bass as bass
import concourse.mybir as mybir
from concourse.bass_utils import run_bass_kernel_spmd



F32 = mybir.dt.float32
BF16 = mybir.dt.bfloat16
I32 = mybir.dt.int32
AF = mybir.ActivationFunctionType
ALU = mybir.AluOpType
AX = mybir.AxisListType

ENGINES = ("pe", "act", "dve", "pool", "sp")
DMA_QUEUES = {"sp": "qsp", "act": "qact", "pool": "qpool"}
ENG_ATTR = {"pe": "tensor", "act": "scalar", "dve": "vector",
            "pool": "gpsimd", "sp": "sync"}


class Op:
    __slots__ = ("engine", "fn", "deps", "is_dma", "queue", "needed", "sem",
                 "value", "tag", "amt", "idx")

    def __init__(self, engine, fn, deps, is_dma, queue, tag, idx):
        self.engine = engine
        self.fn = fn
        self.deps = deps
        self.is_dma = is_dma
        self.queue = queue
        self.needed = False
        self.sem = None
        self.value = None
        self.tag = tag
        self.idx = idx


def _names(aps):
    out = []
    for ap in aps:
        if ap is None or isinstance(ap, (int, float)):
            continue
        out.append(ap.tensor.name)
    return out


class Prog:
    def __init__(self, nc):
        self.nc = nc
        self.ops = []
        self.last_writer = {}
        self.readers_since = {}
        self._consts = {}
        self._scopes = []
        self._barriers = []

    def const_tile(self, val, dtype=F32):
        key = (float(val), dtype)
        if key not in self._consts:
            g = self.nc.sbuf_tensor(f"const_{len(self._consts)}", [128, 1],
                                    dtype, side="right")
            t = g.__enter__()
            self.memset(t[:], float(val), tag=f"const={val}")
            self._consts[key] = t[:]
        return self._consts[key]

    def sbuf(self, name, shape, dtype, side=None):
        g = self.nc.sbuf_tensor(name, list(shape), dtype, side=side)
        t = g.__enter__()
        if self._scopes:
            self._scopes[-1][0].append((g, t.name))
        return t

    def psum(self, name, shape, dtype=F32):
        g = self.nc.psum_tensor(name, list(shape), dtype)
        t = g.__enter__()
        if self._scopes:
            self._scopes[-1][0].append((g, t.name))
        return t

    def fence(self, tag="fence"):
        deps = []
        seen = {}
        for op in reversed(self.ops):
            if op.engine not in seen:
                seen[op.engine] = op
        bar = self._rec("dve", lambda e: e.engine_nop(), [], [],
                        list(seen.values()), tag)
        self._barriers.append(bar)
        return bar

    def push_scope(self):
        self._scopes.append(([], len(self.ops)))

    def pop_scope(self):
        guards, start = self._scopes.pop()
        names = {n for _, n in guards}
        barrier_deps = []
        for n in names:
            barrier_deps.extend(self.last_writer.get(n, ()))
            barrier_deps.extend(self.readers_since.get(n, ()))
            self.last_writer.pop(n, None)
            self.readers_since.pop(n, None)
        for g, _ in reversed(guards):
            g.__exit__(None, None, None)
        if barrier_deps:
            bar = self._rec("dve", lambda e: e.engine_nop(), [], [],
                            barrier_deps, "scope_barrier")
            self._barriers.append(bar)

    def _rec(self, engine, fn, reads, writes, extra, tag, is_dma=False,
             pe_acc=False):
        op = Op(engine, fn, None, is_dma, DMA_QUEUES.get(engine) if is_dma else None,
                tag, len(self.ops))
        deps = set()
        for d in extra:
            if d is not None:
                deps.add(d)
        for b in self._barriers:
            deps.add(b)
        for n in _names(reads):
            for w in self.last_writer.get(n, ()):
                deps.add(w)
            self.readers_since.setdefault(n, []).append(op)
        for n in _names(writes):
            rs = self.readers_since.get(n)
            if rs:
                for r in rs:
                    deps.add(r)
                self.last_writer[n] = [op]
                self.readers_since[n] = []
            else:
                gen = self.last_writer.setdefault(n, [])
                if len(gen) > 200:
                    for w in gen:
                        deps.add(w)
                    self.last_writer[n] = [op]
                else:
                    if not is_dma:
                        for w in gen:
                            deps.add(w)
                    else:
                        for w in gen:
                            if not w.is_dma:
                                deps.add(w)
                    gen.append(op)
            if pe_acc:
                self.readers_since[n] = [op]
        deps.discard(op)
        op.deps = list(deps)
        self.ops.append(op)
        return op

    def ld(self, out, in_, extra=(), engine="sp", tag="ld", **kw):
        return self._rec(engine, lambda e: e.dma_start(out=out, in_=in_, **kw),
                         [in_], [out], extra, tag, is_dma=True)

    def ldT(self, out, in_, extra=(), engine="sp", tag="ldT"):
        return self._rec(engine, lambda e: e.dma_start_transpose(out=out, in_=in_),
                         [in_], [out], extra, tag, is_dma=True)

    def mm(self, out, lhsT, rhs, extra=(), start=True, stop=True,
           tile_position=None, tag="mm"):
        return self._rec(
            "pe",
            lambda e: e.matmul(out, lhsT, rhs, start=start, stop=stop,
                               tile_position=tile_position),
            [lhsT, rhs], [out], extra, tag, pe_acc=not stop)

    def transp(self, out, in_, identity, extra=(), tag="tr"):
        return self._rec("pe", lambda e: e.transpose(out, in_, identity),
                         [in_, identity], [out], extra, tag)

    def act(self, out, in_, func, extra=(), bias=0.0, scale=1.0, accum_out=None,
            tag="act", engine="act"):
        if isinstance(bias, (int, float)) and float(bias) != 0.0:
            np_ = in_.partition_size() if hasattr(in_, "partition_size") else 128
            bias = self.const_tile(bias)[0:np_, :]
        return self._rec(
            engine,
            lambda e: e.activation(out=out, in_=in_, func=func, bias=bias,
                                   scale=scale, accum_out=accum_out),
            [in_, bias if not isinstance(bias, float) else None,
             scale if not isinstance(scale, float) else None],
            [out] + ([accum_out] if accum_out is not None else []),
            extra, tag)

    def tt(self, out, in0, in1, op, extra=(), engine="dve", tag="tt"):
        return self._rec(engine,
                         lambda e: e.tensor_tensor(out=out, in0=in0, in1=in1, op=op),
                         [in0, in1], [out], extra, tag)

    def ts(self, out, in0, s1, op0, s2=None, op1=None, extra=(), accum_out=None,
           engine="dve", tag="ts"):
        def fn(e):
            if op1 is None:
                return e.tensor_scalar(out=out, in0=in0, scalar1=s1, scalar2=None,
                                       op0=op0, accum_out=accum_out)
            return e.tensor_scalar(out=out, in0=in0, scalar1=s1, scalar2=s2,
                                   op0=op0, op1=op1, accum_out=accum_out)
        return self._rec(engine, fn,
                         [in0, s1 if not isinstance(s1, float) else None,
                          s2 if not isinstance(s2, float) else None],
                         [out] + ([accum_out] if accum_out is not None else []),
                         extra, tag)

    def stt(self, out, in0, scalar, in1, op0, op1, extra=(), engine="dve",
            tag="stt"):
        return self._rec(
            engine,
            lambda e: e.scalar_tensor_tensor(out=out, in0=in0, scalar=scalar,
                                             in1=in1, op0=op0, op1=op1),
            [in0, in1, scalar if not isinstance(scalar, float) else None],
            [out], extra, tag)

    def red(self, out, in_, op, extra=(), negate=False, engine="dve", tag="red"):
        return self._rec(
            engine,
            lambda e: e.tensor_reduce(out=out, in_=in_, axis=AX.X, op=op,
                                      negate=negate),
            [in_], [out], extra, tag)

    def ttr(self, out, in0, in1, op0, op1, accum_out, scale=1.0, scalar=0.0,
            extra=(), tag="ttr"):
        return self._rec(
            "dve",
            lambda e: e.tensor_tensor_reduce(out=out, in0=in0, in1=in1,
                                             scale=scale, scalar=scalar,
                                             op0=op0, op1=op1,
                                             accum_out=accum_out),
            [in0, in1, scalar if not isinstance(scalar, float) else None],
            [out, accum_out], extra, tag)

    def recip(self, out, in_, extra=(), tag="recip"):
        return self._rec("dve", lambda e: e.reciprocal(out=out, in_=in_),
                         [in_], [out], extra, tag)

    def copy(self, out, in_, extra=(), engine="dve", tag="copy"):
        if engine == "act":
            return self._rec(engine, lambda e: e.copy(out=out, in_=in_),
                             [in_], [out], extra, tag)
        return self._rec(engine, lambda e: e.tensor_copy(out=out, in_=in_),
                         [in_], [out], extra, tag)

    def memset(self, ap, val, extra=(), engine="pool", tag="memset"):
        return self._rec(engine, lambda e: e.memset(ap, val), [], [ap], extra, tag)

    def iota(self, out, pattern, base=0, channel_multiplier=0, extra=(),
             tag="iota"):
        return self._rec("pool",
                         lambda e: e.iota(out, pattern, base=base,
                                          channel_multiplier=channel_multiplier),
                         [], [out], extra, tag)

    def affine_select(self, out, in_, pattern, compare_op, fill, base=0,
                      channel_multiplier=0, extra=(), tag="affsel"):
        return self._rec(
            "pool",
            lambda e: e.affine_select(out=out, in_=in_, pattern=pattern,
                                      compare_op=compare_op, fill=fill, base=base,
                                      channel_multiplier=channel_multiplier),
            [in_], [out], extra, tag)

    def ln_bwd_dx(self, out, dy, x_hat, mean_dyx, mean_dy, scale=1.0, extra=(),
                  tag="lnbwd"):
        return self._rec(
            "dve",
            lambda e: e.ln_bwd_dx(out=out, dy=dy, x_hat=x_hat, mean_dyx=mean_dyx,
                                  mean_dy=mean_dy, scale=scale),
            [dy, x_hat, mean_dyx, mean_dy], [out], extra, tag)

    def raw(self, engine, fn, reads, writes, extra=(), tag="raw", is_dma=False):
        return self._rec(engine, fn, reads, writes, extra, tag, is_dma=is_dma)

    def collective(self, kind, op, ins, outs, replica_groups, extra=(), tag="cc"):
        def fn(e):
            return e.collective_compute(kind, op, ins=ins, outs=outs,
                                        replica_groups=replica_groups)
        return self._rec("pool", fn, ins, outs, extra, tag, is_dma=True)

    def build(self, dma_slots=(("qsp", 48), ("qact", 24), ("qpool", 24))):
        nc = self.nc
        for op in self.ops:
            for d in op.deps:
                if (d.engine == "pe" and op.engine == "pe" and not d.is_dma):
                    continue
                d.needed = True
        dma_slots = dict(dma_slots)
        counters = {n: 0 for n in ENGINES}
        qcount = {q: 0 for q in DMA_QUEUES.values()}
        used = set()
        for op in self.ops:
            if op.is_dma:
                K = dma_slots[op.queue]
                i = qcount[op.queue]
                qcount[op.queue] += 1
                slot = i % K
                cycle = i // K
                op.sem = f"{op.queue}#{slot}"
                op.value = 16 * (cycle + 1)
                op.amt = 16
                used.add(op.sem)
            elif op.needed:
                op.sem = op.engine
                counters[op.engine] += 1
                op.value = counters[op.engine]
                op.amt = 1
                used.add(op.sem)
        sems = {s: nc.semaphore(f"sem_{s}").__enter__() for s in sorted(used)}
        eng_ops = {e: [] for e in ENGINES}
        for op in self.ops:
            eng_ops[op.engine].append(op)
        final_dma = {}
        for op in self.ops:
            if op.is_dma:
                final_dma[op.sem] = max(final_dma.get(op.sem, 0), op.value)

        with nc.Block() as block:
            for ename in ENGINES:
                ops = eng_ops[ename]
                if not ops and ename != "sp":
                    continue

                def make_body(ops=ops, ename=ename):
                    def body(eng):
                        waited = {}
                        for op in ops:
                            for d in sorted(op.deps, key=lambda o: o.idx):
                                if (d.engine == "pe" and ename == "pe"
                                        and not d.is_dma):
                                    continue
                                if d.sem is None:
                                    raise RuntimeError(
                                        f"dep of {op.tag} on un-inc'd {d.tag}")
                                if waited.get(d.sem, 0) < d.value:
                                    eng.wait_ge(sems[d.sem], d.value)
                                    waited[d.sem] = d.value
                            if op.is_dma and op.value > 16:
                                pv = op.value - 16
                                if waited.get(op.sem, 0) < pv:
                                    eng.wait_ge(sems[op.sem], pv)
                                    waited[op.sem] = pv
                            inst = op.fn(eng)
                            if op.sem is not None:
                                inst.then_inc(sems[op.sem], op.amt)
                        if ename == "sp":
                            for s, v in sorted(final_dma.items()):
                                if waited.get(s, 0) < v:
                                    eng.wait_ge(sems[s], v)
                    return body

                getattr(block, ENG_ATTR[ename])(make_body())
        return nc

    def stats(self):
        from collections import Counter
        c = Counter(op.engine for op in self.ops)
        return dict(c)



LN_EPS = 1e-5
FFN_EPS = 1e-7


def emit_ln(p, xt, ln_out, scratch, sc, D, tag="ln"):
    mu, ss, var, rstd = sc[:, 0:1], sc[:, 1:2], sc[:, 2:3], sc[:, 3:4]
    p.red(mu, xt, ALU.add, tag=f"{tag}:sum")
    p.act(scratch, xt, AF.Square, accum_out=ss, tag=f"{tag}:sumsq")
    p.ts(mu, mu, 1.0 / D, ALU.mult, tag=f"{tag}:mu")
    p.tt(var, mu, mu, ALU.mult, tag=f"{tag}:mu2")
    p.stt(var, ss, 1.0 / D, var, ALU.mult, ALU.subtract, tag=f"{tag}:var")
    p.act(rstd, var, AF.Sqrt, bias=LN_EPS, tag=f"{tag}:std")
    p.recip(rstd, rstd, tag=f"{tag}:rstd")
    return p.ts(ln_out, xt, mu, ALU.subtract, rstd, ALU.mult, tag=f"{tag}:norm")


def build_l1ffn(R=1024, D=2048, TOKT=512):
    nc = bass.Bass("TRN2", target_bir_lowering=False)
    x = nc.dram_tensor("x", [R, D], F32, kind="ExternalInput").ap()
    w = nc.dram_tensor("w", [D, D], F32, kind="ExternalInput").ap()
    scale = nc.dram_tensor("scale", [D], F32, kind="ExternalInput").ap()
    xo = nc.dram_tensor("xo", [R, D], F32, kind="ExternalOutput").ap()
    ln_d = nc.dram_tensor("ln_d", [R, D], BF16, kind="Internal").ap()
    otp_d = nc.dram_tensor("otp_d", [D, R], BF16, kind="Internal").ap()

    p = Prog(nc)
    nK = D // 128
    nM = D // 128
    nT = R // 128
    nN = R // TOKT

    xt = [p.sbuf(f"xt{i}", [128, D], F32) for i in range(nT)]
    lnt = p.sbuf("lnt", [128, D], BF16)
    trash = p.sbuf("trash", [128, D], BF16)
    sc_t = [p.sbuf(f"sc{i}", [128, 8], F32) for i in range(nT)]
    wt = [p.sbuf(f"wt{i}", [128, D], BF16) for i in range(nK)]
    fmt = [p.sbuf(f"fmt{i}", [128, R], BF16) for i in range(nK)]
    sct = p.sbuf("sct", [128, nM], F32)
    ot = [p.sbuf(f"ot{i}", [128, TOKT], BF16) for i in range(2)]
    pt = p.sbuf("pt", [128, D], F32)
    ps = [p.psum(f"ps{i}", [128, TOKT], F32) for i in range(2)]

    for i in range(nK):
        p.ld(wt[i][:], w[i * 128:(i + 1) * 128, :], engine="pool", tag=f"wld{i}")
    p.ld(sct[:], scale.rearrange("(a b) -> b a", b=128), engine="pool",
         tag="scld", allow_slow_non_contiguous=True)

    for t in range(nT):
        p.ld(xt[t][:], x[t * 128:(t + 1) * 128, :], tag=f"xld{t}")
        emit_ln(p, xt[t][:], lnt[:], trash[:], sc_t[t][:], D, tag=f"ln{t}")
        p.ld(ln_d[t * 128:(t + 1) * 128, :], lnt[:], engine="sp", tag=f"lnst{t}")

    for i in range(nK):
        p.ldT(fmt[i][:], ln_d[:, i * 128:(i + 1) * 128], engine="act",
              tag=f"fmld{i}")

    for m in range(nM):
        for n in range(nN):
            j = (m * nN + n) % 2
            for k in range(nK):
                p.mm(ps[j][:], wt[k][:, m * 128:(m + 1) * 128],
                     fmt[k][:, n * TOKT:(n + 1) * TOKT],
                     start=(k == 0), stop=(k == nK - 1), tag=f"mm{m}_{n}_{k}")
            p.ts(ot[j][:], ps[j][:], sct[:, m:m + 1], ALU.mult, tag=f"ev{m}_{n}")
            p.ld(otp_d[m * 128:(m + 1) * 128, n * TOKT:(n + 1) * TOKT],
                 ot[j][:], engine="sp", tag=f"ost{m}_{n}")

    for t in range(nT):
        s = sc_t[t]
        p.ldT(trash[:], otp_d[:, t * 128:(t + 1) * 128], engine="act",
              tag=f"pld{t}")
        o_bf = trash[:]
        ssq, vn, f1, tmp = s[:, 4:5], s[:, 5:6], s[:, 6:7], s[:, 7:8]
        p.act(pt[:], o_bf, AF.Square, accum_out=ssq, tag=f"po{t}:sq")
        p.act(vn, ssq, AF.Sqrt, tag=f"po{t}:norm")
        p.ts(vn, vn, FFN_EPS, ALU.max, tag=f"po{t}:vn")
        p.act(f1, vn, AF.Tanh, tag=f"po{t}:tanh")
        p.ts(tmp, vn, FFN_EPS, ALU.add, tag=f"po{t}:vne")
        p.recip(tmp, tmp, tag=f"po{t}:rec")
        p.tt(f1, f1, tmp, ALU.mult, tag=f"po{t}:f1")
        p.tt(tmp, f1, vn, ALU.mult, tag=f"po{t}:on")
        p.ts(tmp, tmp, FFN_EPS, ALU.max, tag=f"po{t}:onm")
        p.recip(tmp, tmp, tag=f"po{t}:rec2")
        p.ts(tmp, tmp, 0.99, ALU.mult, 1.0, ALU.min, tag=f"po{t}:f2")
        p.tt(f1, f1, tmp, ALU.mult, tag=f"po{t}:f")
        p.ts(pt[:], o_bf, f1, ALU.mult, tag=f"po{t}:scale")
        p.tt(pt[:], pt[:], xt[t][:], ALU.add, tag=f"po{t}:res")
        p.ld(xo[t * 128:(t + 1) * 128, :], pt[:], engine="sp", tag=f"xost{t}")

    p.build()
    return nc


def ref_l1ffn(x, w, scale):
    mu = x.mean(-1, keepdims=True)
    var = x.var(-1, keepdims=True)
    h = (x - mu) / np.sqrt(var + LN_EPS)
    out = h @ w * scale
    vn = np.maximum(np.linalg.norm(out, axis=-1, keepdims=True), FFN_EPS)
    out = out * (np.tanh(vn) / (vn + FFN_EPS))
    on = np.maximum(np.linalg.norm(out, axis=-1, keepdims=True), FFN_EPS)
    out = out * np.minimum(0.99 / on, 1.0)
    return x + out






def build_ln(R=1024, D=2048):
    nc = bass.Bass("TRN2", target_bir_lowering=False)
    x = nc.dram_tensor("x", [R, D], F32, kind="ExternalInput").ap()
    ln = nc.dram_tensor("ln", [R, D], BF16, kind="ExternalOutput").ap()
    p = Prog(nc)
    nT = R // 128
    xt = [p.sbuf(f"xt{i}", [128, D], F32) for i in range(2)]
    lnt = [p.sbuf(f"lnt{i}", [128, D], BF16) for i in range(2)]
    trash = p.sbuf("trash", [128, D], BF16)
    sc = p.sbuf("sc", [128, 8], F32)
    for t in range(nT):
        j = t % 2
        p.ld(xt[j][:], x[t * 128:(t + 1) * 128, :], tag=f"xld{t}")
        emit_ln(p, xt[j][:], lnt[j][:], trash[:], sc[:], D, tag=f"ln{t}")
        p.ld(ln[t * 128:(t + 1) * 128, :], lnt[j][:], engine="sp", tag=f"lnst{t}")
    p.build()
    return nc


def build_l0ffn(R=1024, D=2048, I=2048, TOKT=512):
    nc = bass.Bass("TRN2", target_bir_lowering=False)
    ln = nc.dram_tensor("ln", [R, D], BF16, kind="ExternalInput").ap()
    x = nc.dram_tensor("x", [R, D], F32, kind="ExternalInput").ap()
    gup = nc.dram_tensor("gup", [D, 2 * I], F32, kind="ExternalInput").ap()
    down = nc.dram_tensor("down", [I, D], F32, kind="ExternalInput").ap()
    xo = nc.dram_tensor("xo", [R, D], F32, kind="ExternalOutput").ap()
    otp_d = nc.dram_tensor("otp_d", [D, R], BF16, kind="Internal").ap()

    p = Prog(nc)
    nK = D // 128
    nI = I // 128
    nT = R // 128
    nN = R // TOKT

    fmt = [p.sbuf(f"fmt{i}", [128, R], BF16) for i in range(nK)]
    for i in range(nK):
        p.ldT(fmt[i][:], ln[:, i * 128:(i + 1) * 128], engine="act", tag=f"fmld{i}")

    gwp = [p.sbuf(f"gwp{b}", [128, nK * 256], BF16) for b in range(2)]
    ps = [p.psum(f"ps{i}", [128, TOKT], F32) for i in range(4)]
    sg = [p.sbuf(f"sg{i}", [128, TOKT], F32) for i in range(2)]
    inter = [p.sbuf(f"it{i}", [128, R], BF16) for i in range(nI)]

    for m in range(nI):
        gb = gwp[m % 2]
        for k in range(nK):
            p.ld(gb[:, k * 256:k * 256 + 128],
                 gup[k * 128:(k + 1) * 128, m * 128:(m + 1) * 128],
                 engine="pool", tag=f"gwl{m}_{k}")
            p.ld(gb[:, k * 256 + 128:k * 256 + 256],
                 gup[k * 128:(k + 1) * 128, I + m * 128:I + (m + 1) * 128],
                 engine="pool", tag=f"uwl{m}_{k}")
        for n in range(nN):
            jg = (2 * (m * nN + n)) % 4
            ju = jg + 1
            for k in range(nK):
                p.mm(ps[jg][:], gb[:, k * 256:k * 256 + 128],
                     fmt[k][:, n * TOKT:(n + 1) * TOKT],
                     start=(k == 0), stop=(k == nK - 1), tag=f"gmm{m}_{n}_{k}")
                p.mm(ps[ju][:], gb[:, k * 256 + 128:k * 256 + 256],
                     fmt[k][:, n * TOKT:(n + 1) * TOKT],
                     start=(k == 0), stop=(k == nK - 1), tag=f"umm{m}_{n}_{k}")
            p.act(sg[n % 2][:], ps[jg][:], AF.Silu, tag=f"silu{m}_{n}")
            p.tt(inter[m][:, n * TOKT:(n + 1) * TOKT], sg[n % 2][:],
                 ps[ju][:], ALU.mult, tag=f"gx{m}_{n}")

    dwp = [p.sbuf(f"dwp{b}", [128, nI * 128], BF16) for b in range(2)]
    ot = [p.sbuf(f"ot{i}", [128, TOKT], BF16) for i in range(2)]
    nM = D // 128
    for m in range(nM):
        db = dwp[m % 2]
        for k in range(nI):
            p.ld(db[:, k * 128:(k + 1) * 128],
                 down[k * 128:(k + 1) * 128, m * 128:(m + 1) * 128],
                 engine="pool", tag=f"dwl{m}_{k}")
        for n in range(nN):
            j = (m * nN + n) % 2
            for k in range(nI):
                p.mm(ps[j][:], db[:, k * 128:(k + 1) * 128],
                     inter[k][:, n * TOKT:(n + 1) * TOKT],
                     start=(k == 0), stop=(k == nI - 1), tag=f"dmm{m}_{n}_{k}")
            p.copy(ot[j][:], ps[j][:], tag=f"dev{m}_{n}")
            p.ld(otp_d[m * 128:(m + 1) * 128, n * TOKT:(n + 1) * TOKT],
                 ot[j][:], engine="sp", tag=f"ost{m}_{n}")

    xt = p.sbuf("xt", [128, D], F32)
    pt = p.sbuf("pt", [128, D], F32)
    ob = p.sbuf("ob", [128, D], BF16)
    for t in range(nT):
        p.ldT(ob[:], otp_d[:, t * 128:(t + 1) * 128], engine="act", tag=f"pld{t}")
        p.ld(xt[:], x[t * 128:(t + 1) * 128, :], tag=f"xld{t}")
        p.tt(pt[:], ob[:], xt[:], ALU.add, tag=f"res{t}")
        p.ld(xo[t * 128:(t + 1) * 128, :], pt[:], engine="sp", tag=f"xost{t}")

    p.build()
    return nc


def ref_l0ffn(ln, x, gup, down):
    gu = ln.astype(np.float32) @ gup
    I = gup.shape[1] // 2
    gate, up = gu[:, :I], gu[:, I:]
    inter = gate / (1 + np.exp(-gate)) * up
    return x + inter @ down





NH_CORE = 2
HD, RD = 128, 64
QR, KVR = 768, 512
LT = 1408


def rope_tables(S, dtype=np.float32):
    d = RD
    inv = 1.0 / 10000.0 ** (np.arange(0, d, 2, dtype=np.float64) / d)
    pos = np.arange(S, dtype=np.float64)
    fr = np.outer(pos, inv)
    cos = np.concatenate([np.cos(fr), np.cos(fr)], -1)
    sin = np.sin(fr)
    ssin = np.concatenate([-sin, sin], -1)
    return cos.T.astype(dtype).copy(), ssin.T.astype(dtype).copy()


def build_mla_lat(R=1024, D=2048):
    nc = bass.Bass("TRN2", target_bir_lowering=False)
    x = nc.dram_tensor("x", [R, D], F32, kind="ExternalInput").ap()
    wlat = nc.dram_tensor("wlat", [D, LT], F32, kind="ExternalInput").ap()
    latT = nc.dram_tensor("latT", [LT, R], BF16, kind="ExternalOutput").ap()
    ln_d = nc.dram_tensor("ln_d", [R, D], BF16, kind="Internal").ap()
    p = Prog(nc)
    nK, nT, TOKT = D // 128, R // 128, 512
    nN = R // TOKT
    nM = LT // 128
    xt = [p.sbuf(f"xt{i}", [128, D], F32) for i in range(2)]
    lnt = [p.sbuf(f"lnt{i}", [128, D], BF16) for i in range(2)]
    trash = p.sbuf("trash", [128, D], BF16)
    sc = p.sbuf("sc", [128, 8], F32)
    for t in range(nT):
        j = t % 2
        p.ld(xt[j][:], x[t * 128:(t + 1) * 128, :], tag=f"xld{t}")
        emit_ln(p, xt[j][:], lnt[j][:], trash[:], sc[:], D, tag=f"ln{t}")
        p.ld(ln_d[t * 128:(t + 1) * 128, :], lnt[j][:], engine="sp", tag=f"lnst{t}")
    fmt = [p.sbuf(f"fmt{i}", [128, R], BF16) for i in range(nK)]
    for i in range(nK):
        p.ldT(fmt[i][:], ln_d[:, i * 128:(i + 1) * 128], engine="act",
              tag=f"fmld{i}")
    wt = [p.sbuf(f"wt{i}", [128, LT], BF16) for i in range(nK)]
    for i in range(nK):
        p.ld(wt[i][:], wlat[i * 128:(i + 1) * 128, :], engine="pool",
             tag=f"wld{i}")
    ps = [p.psum(f"ps{i}", [128, TOKT], F32) for i in range(2)]
    ot = [p.sbuf(f"ot{i}", [128, TOKT], BF16) for i in range(2)]
    for m in range(nM):
        for n in range(nN):
            j = (m * nN + n) % 2
            for k in range(nK):
                p.mm(ps[j][:], wt[k][:, m * 128:(m + 1) * 128],
                     fmt[k][:, n * TOKT:(n + 1) * TOKT],
                     start=(k == 0), stop=(k == nK - 1), tag=f"mm{m}_{n}_{k}")
            p.copy(ot[j][:], ps[j][:], tag=f"ev{m}_{n}")
            p.ld(latT[m * 128:(m + 1) * 128, n * TOKT:(n + 1) * TOKT],
                 ot[j][:], engine="sp", tag=f"ost{m}_{n}")
    p.build()
    return nc


def build_mla(S=8192, QT=512):
    nc = bass.Bass("TRN2", target_bir_lowering=False)
    latT = nc.dram_tensor("latT", [LT, S], BF16, kind="ExternalInput").ap()
    wqu = nc.dram_tensor("wqu", [QR, NH_CORE * HD], F32, kind="ExternalInput").ap()
    wqr = nc.dram_tensor("wqr", [QR, NH_CORE * RD], F32, kind="ExternalInput").ap()
    wqrp = nc.dram_tensor("wqrp", [QR, NH_CORE * RD], F32, kind="ExternalInput").ap()
    wku = nc.dram_tensor("wku", [KVR, NH_CORE * HD], F32, kind="ExternalInput").ap()
    wvu = nc.dram_tensor("wvu", [KVR, NH_CORE * HD], F32, kind="ExternalInput").ap()
    cosT_d = nc.dram_tensor("cosT", [RD, S], F32, kind="ExternalInput").ap()
    ssinT_d = nc.dram_tensor("ssinT", [RD, S], F32, kind="ExternalInput").ap()
    oT = nc.dram_tensor("oT", [NH_CORE * HD, S], BF16, kind="ExternalOutput").ap()
    vb_d = nc.dram_tensor("vb_d", [HD, S], BF16, kind="Internal").ap()
    qn_d = nc.dram_tensor("qn_d", [1, S], F32, kind="Internal").ap()

    p = Prog(nc)
    SCALE = 1.0 / float(np.sqrt(HD + RD))
    nS = S // 128
    nQ = S // QT
    kq = QT // 128
    nP = S // 512
    RC = min(1024, S)

    nQR, nKV = QR // 128, KVR // 128
    wqu_t = [p.sbuf(f"wqu{i}", [128, NH_CORE * HD], BF16) for i in range(nQR)]
    wqr_t = [p.sbuf(f"wqr{i}", [128, NH_CORE * RD], BF16) for i in range(nQR)]
    wqrp_t = [p.sbuf(f"wqrp{i}", [128, NH_CORE * RD], BF16) for i in range(nQR)]
    wku_t = [p.sbuf(f"wku{i}", [128, NH_CORE * HD], BF16) for i in range(nKV)]
    wvu_t = [p.sbuf(f"wvu{i}", [128, NH_CORE * HD], BF16) for i in range(nKV)]
    for i in range(nQR):
        p.ld(wqu_t[i][:], wqu[i * 128:(i + 1) * 128, :], engine="pool", tag=f"wq{i}")
        p.ld(wqr_t[i][:], wqr[i * 128:(i + 1) * 128, :], engine="pool", tag=f"wr{i}")
        p.ld(wqrp_t[i][:], wqrp[i * 128:(i + 1) * 128, :], engine="pool", tag=f"wp{i}")
    for i in range(nKV):
        p.ld(wku_t[i][:], wku[i * 128:(i + 1) * 128, :], engine="pool", tag=f"wk{i}")
        p.ld(wvu_t[i][:], wvu[i * 128:(i + 1) * 128, :], engine="pool", tag=f"wv{i}")

    masks = [p.sbuf(f"mask{d}", [128, QT], BF16) for d in range(kq)]
    for d in range(kq):
        p.memset(masks[d][:], 1.0, tag=f"mask{d}:set")
        p.affine_select(masks[d][:], masks[d][:], pattern=[[1, QT]],
                        compare_op=ALU.is_ge, fill=0.0, base=-128 * d,
                        channel_multiplier=-1, tag=f"mask{d}:sel")

    qc1 = p.sbuf("qc1", [128, S], BF16)
    qc2 = p.sbuf("qc2", [128, S], BF16)
    kc1 = p.sbuf("kc1", [128, S], BF16)
    kc2 = p.sbuf("kc2", [128, S], BF16)
    vtk = p.sbuf("vtk", [128, S], BF16)
    sq = p.sbuf("sq", [128, S], BF16)
    rpre = p.sbuf("rpre", [64, S], BF16)
    rrot = p.sbuf("rrot", [64, S], BF16)
    latp = [[p.sbuf(f"latp{b}_{i}", [128, 512], BF16) for i in range(nQR)]
            for b in range(2)]
    kvp = [[p.sbuf(f"kvp{b}_{i}", [128, 512], BF16) for i in range(nKV)]
           for b in range(2)]
    cosc = p.sbuf("cosc", [RD, RC], F32)
    ssinc = p.sbuf("ssinc", [RD, RC], F32)
    rfp = p.sbuf("rfp", [RD, RC], F32)
    rfp2 = p.sbuf("rfp2", [RD, RC], F32)
    rowbuf = p.sbuf("rowbuf", [1, 16], F32)
    qn = p.sbuf("qn", [1, 512], F32)
    qnb = p.sbuf("qnb", [1, 512], BF16)
    ones_k = p.sbuf("ones_k", [128, 1], BF16)
    ones_r = p.sbuf("ones_r", [1, 128], F32)
    pT = [p.sbuf(f"pT{i}", [128, QT], BF16) for i in range(2)]
    obuf = [p.sbuf(f"obuf{i}", [128, QT], BF16) for i in range(2)]
    rs = p.sbuf("rs", [1, QT], F32)
    rb = p.sbuf("rb", [128, QT], F32)

    p.memset(ones_k[:], 1.0, tag="ones_k")
    p.memset(ones_r[:], 1.0, tag="ones_r")

    psA = p.psum("psA", [128, 512], F32)
    psB = p.psum("psB", [128, 512], F32)
    psS = [p.psum(f"psS{i}", [128, QT], F32) for i in range(2)]
    psO = p.psum("psO", [128, QT], F32)
    psR = p.psum("psR", [1, QT], F32)
    psN = p.psum("psN", [1, 512], F32)

    def rope_combine(dst, pre, rot, tag):
        for c in range(S // RC):
            sl = slice(c * RC, (c + 1) * RC)
            p.ld(cosc[:], cosT_d[:, sl], tag=f"{tag}cos{c}")
            p.ld(ssinc[:], ssinT_d[:, sl], tag=f"{tag}sin{c}")
            p.tt(rfp[:], pre[:, sl], cosc[:], ALU.mult, tag=f"{tag}a{c}")
            p.tt(rfp2[:], rot[:, sl], ssinc[:], ALU.mult, tag=f"{tag}b{c}")
            p.tt(dst[0:64, sl], rfp[:], rfp2[:], ALU.add, tag=f"{tag}c{c}")

    p.ld(rpre[:], latT[1280:1344, :], tag="krpld")
    p.ld(rrot[:], latT[1344:1408, :], tag="krrld")
    rope_combine(kc2, rpre[:], rrot[:], "kR")
    p.memset(kc2[64:65, 0:S], 1.0, tag="kc2one")

    for h in range(NH_CORE):
        hHD = slice(h * HD, (h + 1) * HD)
        hRD = slice(h * RD, (h + 1) * RD)
        for n in range(nP):
            b = n % 2
            sl = slice(n * 512, (n + 1) * 512)
            for i in range(nKV):
                p.ld(kvp[b][i][:], latT[QR + i * 128:QR + (i + 1) * 128, sl],
                     tag=f"h{h}kvp{n}_{i}")
            for k in range(nKV):
                p.mm(psA[:], wku_t[k][:, hHD], kvp[b][k][:],
                     start=(k == 0), stop=(k == nKV - 1), tag=f"h{h}kc{n}_{k}")
            p.copy(kc1[:, sl], psA[:], tag=f"h{h}kcev{n}")
            for k in range(nKV):
                p.mm(psB[:], wvu_t[k][:, hHD], kvp[b][k][:],
                     start=(k == 0), stop=(k == nKV - 1), tag=f"h{h}v{n}_{k}")
            p.copy(sq[:, sl], psB[:], tag=f"h{h}vev{n}")
            p.ld(vb_d[:, sl], sq[:, sl], engine="sp", tag=f"h{h}vst{n}")
        for t in range(nS):
            p.ldT(vtk[:, t * 128:(t + 1) * 128], vb_d[:, t * 128:(t + 1) * 128],
                  engine="act", tag=f"h{h}vld{t}")
        for n in range(nP):
            b = n % 2
            sl = slice(n * 512, (n + 1) * 512)
            for i in range(nQR):
                p.ld(latp[b][i][:], latT[i * 128:(i + 1) * 128, sl],
                     tag=f"h{h}lp{n}_{i}")
            for k in range(nQR):
                p.mm(psA[:], wqu_t[k][:, hHD], latp[b][k][:],
                     start=(k == 0), stop=(k == nQR - 1), tag=f"h{h}qc{n}_{k}")
            p.copy(qc1[:, sl], psA[:], tag=f"h{h}qcev{n}")
            for k in range(nQR):
                p.mm(psB[0:64, :], wqr_t[k][:, hRD], latp[b][k][:],
                     start=(k == 0), stop=(k == nQR - 1), tag=f"h{h}qr{n}_{k}")
            for k in range(nQR):
                p.mm(psB[64:128, :], wqrp_t[k][:, hRD], latp[b][k][:],
                     start=(k == 0), stop=(k == nQR - 1), tag=f"h{h}qp{n}_{k}")
            p.copy(rpre[:, sl], psB[0:64, :], tag=f"h{h}qrev{n}")
            p.copy(rrot[:, sl], psB[64:128, :], tag=f"h{h}qpev{n}")
        rope_combine(qc2, rpre[:], rrot[:], f"h{h}qR")
        p.act(sq[:, 0:S], qc1[:], AF.Square, tag=f"h{h}sqq1")
        for n in range(nP):
            sl = slice(n * 512, (n + 1) * 512)
            p.mm(psN[:], ones_k[:], sq[:, sl], start=True, stop=False,
                 tag=f"h{h}qn1_{n}")
            p.act(sq[0:64, sl], qc2[0:64, sl], AF.Square, tag=f"h{h}sqq2_{n}")
            p.mm(psN[:], ones_k[0:64, :], sq[0:64, sl], start=False, stop=True,
                 tag=f"h{h}qn2_{n}")
            p.act(qn[:], psN[:], AF.Sqrt, tag=f"h{h}qnrm{n}")
            p.ld(qn_d[0:1, sl], qn[:], engine="sp", tag=f"h{h}qns{n}")
        p.act(sq[:, 0:S], kc1[:], AF.Square, tag=f"h{h}sqk1")
        for n in range(nP):
            sl = slice(n * 512, (n + 1) * 512)
            p.mm(psN[:], ones_k[:], sq[:, sl], start=True, stop=False,
                 tag=f"h{h}kn1_{n}")
            p.act(sq[0:64, sl], kc2[0:64, sl], AF.Square, tag=f"h{h}sqk2_{n}")
            p.mm(psN[:], ones_k[0:64, :], sq[0:64, sl], start=False, stop=True,
                 tag=f"h{h}kn2_{n}")
            if n == 0:
                p.red(rowbuf[:, 0:1], psN[:], ALU.max, tag=f"h{h}km{n}")
            else:
                p.red(rowbuf[:, 1:2], psN[:], ALU.max, tag=f"h{h}km{n}")
                p.tt(rowbuf[:, 0:1], rowbuf[:, 0:1], rowbuf[:, 1:2], ALU.max,
                     tag=f"h{h}kmx{n}")
        p.act(rowbuf[:, 2:3], rowbuf[:, 0:1], AF.Sqrt, tag=f"h{h}mk")
        for n in range(nP):
            sl = slice(n * 512, (n + 1) * 512)
            p.ld(qn[:], qn_d[0:1, sl], tag=f"h{h}qnl{n}")
            p.ts(qn[:], qn[:], rowbuf[:, 2:3], ALU.mult, -1.0, ALU.mult,
                 tag=f"h{h}brow{n}")
            p.copy(qnb[:], qn[:], tag=f"h{h}brow16{n}")
            p.ld(qc2[64:65, sl], qnb[:], engine="sp", tag=f"h{h}bmv{n}")

        for qt in range(nQ):
            nk = kq * qt + kq
            qsl = slice(qt * QT, (qt + 1) * QT)
            for kt in range(nk):
                j = kt % 2
                ksl = slice(kt * 128, (kt + 1) * 128)
                p.mm(psS[j][:], kc1[:, ksl], qc1[:, qsl], start=True, stop=False,
                     tag=f"h{h}s{qt}_{kt}a")
                p.mm(psS[j][:], kc2[0:65, ksl], qc2[0:65, qsl], start=False,
                     stop=True, tag=f"h{h}s{qt}_{kt}b")
                p.act(pT[j][:], psS[j][:], AF.Exp, scale=SCALE,
                      tag=f"h{h}e{qt}_{kt}")
                d = kt - kq * qt
                if d >= 0:
                    p.tt(pT[j][:], pT[j][:], masks[d][:], ALU.mult,
                         tag=f"h{h}m{qt}_{kt}")
                p.mm(psR[:], ones_k[:], pT[j][:], start=(kt == 0),
                     stop=(kt == nk - 1), tag=f"h{h}r{qt}_{kt}")
                p.mm(psO[:], vtk[:, ksl], pT[j][:], start=(kt == 0),
                     stop=(kt == nk - 1), tag=f"h{h}o{qt}_{kt}")
            jo = qt % 2
            p.copy(rs[:], psR[:], tag=f"h{h}rse{qt}")
            p.recip(rs[:], rs[:], tag=f"h{h}rr{qt}")
            p.mm(psS[0][:], ones_r[:], rs[:], start=True, stop=True,
                 tag=f"h{h}rb{qt}")
            p.copy(rb[:], psS[0][:], tag=f"h{h}rbe{qt}")
            p.tt(obuf[jo][:], psO[:], rb[:], ALU.mult, tag=f"h{h}on{qt}")
            p.ld(oT[h * HD:(h + 1) * HD, qsl], obuf[jo][:], engine="sp",
                 tag=f"h{h}ost{qt}")

    p.build()
    return nc


def ref_mla_full(x, wq_down, wq_up, wq_rope, wkv_down, wk_up, wv_up, wk_rope,
                 NH=16):
    import math
    S, D = x.shape
    mu = x.mean(-1, keepdims=True)
    h = (x - mu) / np.sqrt(x.var(-1, keepdims=True) + 1e-5)
    ql = h @ wq_down
    qc = (ql @ wq_up).reshape(S, NH, HD)
    qr = (ql @ wq_rope).reshape(S, NH, RD)
    cos, ssin = rope_tables(S)

    def rope(v):
        d = RD
        inv = 1.0 / 10000.0 ** (np.arange(0, d, 2) / d)
        fr = np.outer(np.arange(S), inv)
        c = np.concatenate([np.cos(fr), np.cos(fr)], -1)
        s = np.concatenate([np.sin(fr), np.sin(fr)], -1)
        rot = np.concatenate([-v[..., d // 2:], v[..., :d // 2]], -1)
        if v.ndim == 3:
            return v * c[:, None, :] + rot * s[:, None, :]
        return v * c + rot * s

    qr = rope(qr)
    kvl = h @ wkv_down
    kc = (kvl @ wk_up).reshape(S, NH, HD)
    v = (kvl @ wv_up).reshape(S, NH, HD)
    kr = rope(h @ wk_rope)
    kr = np.broadcast_to(kr[:, None, :], (S, NH, RD))
    q = np.concatenate([qc, qr], -1)
    k = np.concatenate([kc, kr], -1)
    scale = 1.0 / math.sqrt(HD + RD)
    out = np.zeros((S, NH, HD), np.float32)
    for hh in range(NH):
        sc = q[:, hh] @ k[:, hh].T * scale
        mask = np.triu(np.ones((S, S), bool), 1)
        sc = np.where(mask, -np.inf, sc)
        sc = sc - sc.max(-1, keepdims=True)
        e = np.exp(sc)
        a = e / e.sum(-1, keepdims=True)
        out[:, hh] = a @ v[:, hh]
    return out.reshape(S, NH * HD)






def build_oproj(R=1024, C=4096, D=2048, norm="none", TOKT=512, ln_eps=1e-6,
                rms_eps=1e-5):
    nc = bass.Bass("TRN2", target_bir_lowering=False)
    o = nc.dram_tensor("o", [R, C], BF16, kind="ExternalInput").ap()
    wT = nc.dram_tensor("wT", [C, D], F32, kind="ExternalInput").ap()
    x = nc.dram_tensor("x", [R, D], F32, kind="ExternalInput").ap()
    if norm in ("lnwb", "rmsw"):
        g = nc.dram_tensor("g", [C], F32, kind="ExternalInput").ap()
    if norm == "lnwb":
        bvec = nc.dram_tensor("b", [C], F32, kind="ExternalInput").ap()
    xo = nc.dram_tensor("xo", [R, D], F32, kind="ExternalOutput").ap()
    on_d = nc.dram_tensor("on_d", [R, C], BF16, kind="Internal").ap()

    p = Prog(nc)
    nT = R // 128
    nC = C // 128
    nN = R // TOKT
    nD = D // 512

    psA = p.psum("psA", [128, 512], F32)
    if norm in ("lnwb", "rmsw"):
        grow = p.sbuf("grow", [1, C], F32)
        grep = p.sbuf("grep", [128, C], F32)
        ones1 = p.sbuf("ones1", [1, 128], F32)
        p.memset(ones1[:], 1.0, tag="ones1")
        p.ld(grow[:], g.rearrange("(a c) -> a c", a=1), tag="gld")
        for i in range(C // 512):
            p.mm(psA[:], ones1[:], grow[:, i * 512:(i + 1) * 512], tag=f"gb{i}")
            p.copy(grep[:, i * 512:(i + 1) * 512], psA[:], tag=f"gbe{i}")
    if norm == "lnwb":
        brow = p.sbuf("brow", [1, C], F32)
        brep = p.sbuf("brep", [128, C], F32)
        p.ld(brow[:], bvec.rearrange("(a c) -> a c", a=1), tag="bld")
        for i in range(C // 512):
            p.mm(psA[:], ones1[:], brow[:, i * 512:(i + 1) * 512], tag=f"bb{i}")
            p.copy(brep[:, i * 512:(i + 1) * 512], psA[:], tag=f"bbe{i}")

    if norm != "none":
        ot_ = p.sbuf("ot_", [128, C], BF16)
        nt_ = p.sbuf("nt_", [128, C], F32)
        nb_ = p.sbuf("nb_", [128, C], BF16)
        scs = p.sbuf("scs", [128, 8], F32)
        for t in range(nT):
            p.ld(ot_[:], o[t * 128:(t + 1) * 128, :], tag=f"old{t}")
            mu, ss, var, rstd = scs[:, 0:1], scs[:, 1:2], scs[:, 2:3], scs[:, 3:4]
            if norm == "lnwb":
                p.red(mu, ot_[:], ALU.add, tag=f"n{t}:sum")
                p.act(nt_[:], ot_[:], AF.Square, accum_out=ss, tag=f"n{t}:sq")
                p.ts(mu, mu, 1.0 / C, ALU.mult, tag=f"n{t}:mu")
                p.tt(var, mu, mu, ALU.mult, tag=f"n{t}:mu2")
                p.stt(var, ss, 1.0 / C, var, ALU.mult, ALU.subtract, tag=f"n{t}:var")
                p.act(rstd, var, AF.Sqrt, bias=ln_eps, tag=f"n{t}:std")
                p.recip(rstd, rstd, tag=f"n{t}:rstd")
                p.ts(nt_[:], ot_[:], mu, ALU.subtract, rstd, ALU.mult,
                     tag=f"n{t}:xh")
                p.tt(nt_[:], nt_[:], grep[:], ALU.mult, tag=f"n{t}:g")
                p.tt(nb_[:], nt_[:], brep[:], ALU.add, tag=f"n{t}:b")
            else:
                p.act(nt_[:], ot_[:], AF.Square, accum_out=ss, tag=f"n{t}:sq")
                p.ts(ss, ss, 1.0 / C, ALU.mult, tag=f"n{t}:ms")
                p.act(rstd, ss, AF.Sqrt, bias=rms_eps, tag=f"n{t}:std")
                p.recip(rstd, rstd, tag=f"n{t}:rstd")
                p.ts(nt_[:], ot_[:], rstd, ALU.mult, tag=f"n{t}:xh")
                p.tt(nb_[:], nt_[:], grep[:], ALU.mult, tag=f"n{t}:g")
            p.ld(on_d[t * 128:(t + 1) * 128, :], nb_[:], engine="sp",
                 tag=f"nst{t}")
        osrc = on_d
    else:
        osrc = o

    ofm = [p.sbuf(f"ofm{i}", [128, R], BF16) for i in range(nC)]
    for i in range(nC):
        p.ldT(ofm[i][:], osrc[:, i * 128:(i + 1) * 128], engine="act",
              tag=f"ofm{i}")

    wtg = [p.sbuf(f"wtg{b}", [128, nC * 512], BF16) for b in range(2)]
    xt = [p.sbuf(f"xt{i}", [128, 512], F32) for i in range(2)]
    acc = [p.sbuf(f"acc{i}", [128, 512], F32) for i in range(2)]
    ps = [p.psum(f"ps{i}", [128, 512], F32) for i in range(2)]

    for n in range(nD):
        b = n % 2
        nsl = slice(n * 512, (n + 1) * 512)
        for k in range(nC):
            p.ld(wtg[b][:, k * 512:(k + 1) * 512], wT[k * 128:(k + 1) * 128, nsl],
                 engine="pool", tag=f"wld{n}_{k}")
        for t in range(nT):
            j = t % 2
            p.ld(xt[j][:], x[t * 128:(t + 1) * 128, nsl], tag=f"xld{n}_{t}")
            for k in range(nC):
                p.mm(ps[j][:], ofm[k][:, t * 128:(t + 1) * 128],
                     wtg[b][:, k * 512:(k + 1) * 512],
                     start=(k == 0), stop=(k == nC - 1), tag=f"mm{n}_{t}_{k}")
            p.tt(acc[j][:], ps[j][:], xt[j][:], ALU.add, tag=f"ev{n}_{t}")
            p.ld(xo[t * 128:(t + 1) * 128, nsl], acc[j][:], engine="sp",
                 tag=f"xst{n}_{t}")

    p.build()
    return nc



BIG = 1e9
LOG128 = float(np.log(128.0))


def make_selm():
    selm = np.zeros((16, 12 * 128), np.float32)
    for h in range(4):
        qh = h // 2
        c = lambda v: slice((v * 4 + h) * 128, (v * 4 + h + 1) * 128)
        selm[h, c(0)] = -1.0
        selm[6 + qh, c(0)] = -0.5
        selm[h, c(1)] = 1.0
        selm[6 + qh, c(1)] = -0.5
        selm[8 + h, c(1)] = 1.0
        selm[h, c(2)] = 1.0
        selm[4 + qh, c(2)] = -0.5
        selm[12, c(2)] = -0.5 * LOG128
    return selm


def pack_wproj(W_qkv, W_z, W_b, W_a, core):
    q = W_qkv[256 * core:256 * (core + 1)]
    k = W_qkv[2048 + 256 * core:2048 + 256 * (core + 1)]
    v = W_qkv[4096 + 512 * core:4096 + 512 * (core + 1)]
    z = W_z[512 * core:512 * (core + 1)]
    b = W_b[4 * core:4 * (core + 1)]
    a = W_a[4 * core:4 * (core + 1)]
    pad = np.zeros((28, 2048), np.float32)
    Wall = np.concatenate([q, k, v, z, b, pad, a, pad], 0)
    return np.ascontiguousarray(Wall.T)


def pack_convw(conv_w, core):
    cw = conv_w[:, 0, :] if conv_w.ndim == 3 else conv_w
    q = cw[256 * core:256 * (core + 1)]
    k = cw[2048 + 256 * core:2048 + 256 * (core + 1)]
    v = cw[4096 + 512 * core:4096 + 512 * (core + 1)]
    return np.ascontiguousarray(np.concatenate([q, k, v], 0))


def build_l0a(S=8192, D=2048):
    nc = bass.Bass("TRN2", target_bir_lowering=False)
    ln = nc.dram_tensor("ln", [S, D], BF16, kind="ExternalInput").ap()
    wproj = nc.dram_tensor("wproj", [D, 1600], F32, kind="ExternalInput").ap()
    convw = nc.dram_tensor("convw", [1024, 4], F32, kind="ExternalInput").ap()
    dtb = nc.dram_tensor("dtb", [4], F32, kind="ExternalInput").ap()
    alog = nc.dram_tensor("alog", [4], F32, kind="ExternalInput").ap()
    normw = nc.dram_tensor("normw", [128], F32, kind="ExternalInput").ap()
    selm_d = nc.dram_tensor("selm", [16, 12 * 128], F32, kind="ExternalInput").ap()
    o = nc.dram_tensor("o", [S, 512], BF16, kind="ExternalOutput").ap()
    pre_d = nc.dram_tensor("pre_d", [1024, S], BF16, kind="Internal").ap()
    qkvc_d = nc.dram_tensor("qkvc_d", [1024, S], BF16, kind="Internal").ap()
    zT_d = nc.dram_tensor("zT_d", [512, S], BF16, kind="Internal").ap()
    core_d = nc.dram_tensor("core_d", [S, 512], BF16, kind="Internal").ap()
    b_d = nc.dram_tensor("b_d", [4, S], F32, kind="Internal").ap()
    a_d = nc.dram_tensor("a_d", [4, S], F32, kind="Internal").ap()
    rows_d = nc.dram_tensor("rows_d", [16, S], F32, kind="Internal").ap()

    p = Prog(nc)
    nT = S // 128
    nP = S // 512
    BL = min(2048, S)
    NB = S // BL

    tmv = p.sbuf("tmv", [128, nT * 16], F32)
    i1k = p.sbuf("i1k", [128, nT * 4], F32)
    i1q = p.sbuf("i1q", [128, nT * 4], F32)
    j1t = p.sbuf("j1t", [128, nT * 4], F32)
    nci = p.sbuf("nci", [128, nT * 4], F32)
    ceq = p.sbuf("ceq", [128, nT * 4], F32)
    ckd = p.sbuf("ckd", [128, nT * 4], F32)
    ecsr = p.sbuf("ecsr", [4, nT * 2], F32)
    selm = p.sbuf("selm_s", [16, 12 * 128], F32)
    p.ld(selm[:], selm_d[:], engine="pool", tag="selmld")
    identT = p.sbuf("identT", [128, 128], F32)
    p.memset(identT[:], 1.0, tag="idT0")
    p.affine_select(identT[:], identT[:], pattern=[[-1, 128]],
                    compare_op=ALU.is_equal, fill=0.0, base=0,
                    channel_multiplier=1, tag="idT1")
    identb = p.sbuf("identb", [128, 128], BF16)
    p.copy(identb[:], identT[:], tag="idb")
    bdU = p.sbuf("bdU", [128, 128], BF16)
    p.memset(bdU[:], 1.0, tag="bdU0")
    p.affine_select(bdU[:], bdU[:], pattern=[[1, 128]], compare_op=ALU.is_ge,
                    fill=0.0, base=0, channel_multiplier=-1, tag="bdU1")
    p.memset(bdU[0:64, 64:128], 0.0, tag="bdU2")
    bdA = p.sbuf("bdA", [128, 128], BF16)
    p.memset(bdA[:], 1.0, tag="bdA0")
    p.memset(bdA[0:64, 64:128], 0.0, tag="bdA1")
    p.memset(bdA[64:128, 0:64], 0.0, tag="bdA2")
    PMc = p.sbuf("PMc", [128, 128], BF16)
    p.memset(PMc[:], 0.0, tag="PM0")
    p.affine_select(PMc[:], PMc[:], pattern=[[-1, 128]], compare_op=ALU.is_gt,
                    fill=-BIG, base=0, channel_multiplier=1, tag="PM1")
    p.memset(PMc[64:128, 0:64], -BIG, tag="PM2")
    PNs = p.sbuf("PNs", [128, 128], BF16)
    p.memset(PNs[:], 0.0, tag="PNs0")
    p.affine_select(PNs[:], PNs[:], pattern=[[1, 128]], compare_op=ALU.is_gt,
                    fill=-BIG, base=0, channel_multiplier=-1, tag="PNs1")
    p.memset(PNs[0:64, 64:128], -BIG, tag="PNs2")
    PNi = p.sbuf("PNi", [128, 128], BF16)
    p.memset(PNi[:], 0.0, tag="PNi0")
    p.affine_select(PNi[:], PNi[:], pattern=[[1, 128]], compare_op=ALU.is_ge,
                    fill=-BIG, base=0, channel_multiplier=-1, tag="PNi1")
    p.memset(PNi[0:64, 64:128], -BIG, tag="PNi2")
    ones128 = p.sbuf("ones128", [128, 1], BF16)
    p.memset(ones128[:], 1.0, tag="ones128")
    onesr = p.sbuf("onesr", [1, 128], F32)
    p.memset(onesr[:], 1.0, tag="onesr")
    ind4 = p.sbuf("ind4", [4, 512], F32)
    p.memset(ind4[:], 1.0, tag="ind40")
    p.affine_select(ind4[:].rearrange("p (b f) -> p b f", b=4),
                    ind4[:].rearrange("p (b f) -> p b f", b=4),
                    pattern=[[-1, 4], [0, 128]], compare_op=ALU.is_equal,
                    fill=0.0, base=0, channel_multiplier=1, tag="ind41")
    ind4h = [ind4[:, h * 128:(h + 1) * 128] for h in range(4)]

    p.push_scope()
    lnp = [[p.sbuf(f"lnp{b}_{i}", [128, 512], BF16) for i in range(16)]
           for b in range(2)]
    wt = [p.sbuf(f"wt{i}", [128, 1600], BF16) for i in range(16)]
    for i in range(16):
        p.ld(wt[i][:], wproj[i * 128:(i + 1) * 128, :], engine="pool",
             tag=f"wld{i}")
    psE = [p.psum(f"psE{i}", [128, 512], F32) for i in range(2)]
    evb = [p.sbuf(f"evb{i}", [128, 512], BF16) for i in range(2)]
    stb = p.sbuf("stb", [4, 512], F32)
    sta = p.sbuf("sta", [36, 512], F32)
    for n in range(nP):
        b = n % 2
        sl = slice(n * 512, (n + 1) * 512)
        for i in range(16):
            p.ldT(lnp[b][i][:], ln[sl, i * 128:(i + 1) * 128], engine="act",
                  tag=f"lnp{n}_{i}")
        for m in range(13):
            M0, M1 = m * 128, min(1600, (m + 1) * 128)
            j = m % 2
            for k in range(16):
                p.mm(psE[j][0:M1 - M0, :], wt[k][:, M0:M1], lnp[b][k][:],
                     start=(k == 0), stop=(k == 15), tag=f"pj{n}_{m}_{k}")
            if m < 12:
                p.copy(evb[j][:], psE[j][:], tag=f"pje{n}_{m}")
                dst = pre_d if m < 8 else zT_d
                r0 = M0 if m < 8 else M0 - 1024
                p.ld(dst[r0:r0 + 128, sl], evb[j][:], engine="sp",
                     tag=f"pjs{n}_{m}")
            else:
                p.copy(stb[:], psE[j][0:4, :], tag=f"pjb{n}")
                p.copy(sta[32:36, :], psE[j][32:36, :], tag=f"pja{n}")
                p.ld(b_d[:, sl], stb[:], engine="sp", tag=f"pbs{n}")
                p.ld(a_d[:, sl], sta[32:36, :], engine="sp", tag=f"pas{n}")
    p.pop_scope()

    p.push_scope()
    cw = p.sbuf("cw", [128, 4 * 8], F32)
    for t in range(8):
        p.ld(cw[:, t * 4:(t + 1) * 4], convw[t * 128:(t + 1) * 128, :],
             engine="pool", tag=f"cwld{t}")
    cin = p.sbuf("cin", [128, S + 3], BF16)
    cacc = [p.sbuf(f"cacc{i}", [128, S], F32) for i in range(2)]
    cout = p.sbuf("cout", [128, S], BF16)
    p.memset(cin[:, 0:3], 0.0, tag="cpad")
    for t in range(8):
        p.ld(cin[:, 3:S + 3], pre_d[t * 128:(t + 1) * 128, :], tag=f"cld{t}")
        p.ts(cacc[0][:], cin[:, 0:S], cw[:, t * 4:t * 4 + 1], ALU.mult,
             tag=f"cv{t}_0")
        p.stt(cacc[1][:], cin[:, 1:S + 1], cw[:, t * 4 + 1:t * 4 + 2],
              cacc[0][:], ALU.mult, ALU.add, tag=f"cv{t}_1")
        p.stt(cacc[0][:], cin[:, 2:S + 2], cw[:, t * 4 + 2:t * 4 + 3],
              cacc[1][:], ALU.mult, ALU.add, tag=f"cv{t}_2")
        p.stt(cacc[1][:], cin[:, 3:S + 3], cw[:, t * 4 + 3:t * 4 + 4],
              cacc[0][:], ALU.mult, ALU.add, tag=f"cv{t}_3")
        p.act(cout[:], cacc[1][:], AF.Silu, tag=f"cvs{t}")
        p.ld(qkvc_d[t * 128:(t + 1) * 128, :], cout[:], engine="sp",
             tag=f"cst{t}")
    p.pop_scope()

    gcb = p.sbuf("gcb", [4, BL], F32, side="right")
    lnb = p.sbuf("lnb", [4, BL], F32, side="right")
    lnsq = [p.sbuf(f"lnsq{r}", [1, BL], F32, side="right") for r in range(4)]
    onesbl = p.sbuf("onesbl", [1, BL], F32, side="right")
    g_r = p.sbuf("g_r", [4, BL], F32, side="right")
    bco = p.sbuf("bco", [4, BL], F32, side="right")
    p.push_scope()
    braw = p.sbuf("braw", [4, BL], F32)
    araw = p.sbuf("araw", [4, BL], F32)
    tr1 = p.sbuf("tr1", [4, BL], F32)
    p.memset(onesbl[:], 1.0, tag="onesbl")
    rtp = p.sbuf("rtp", [16, BL], F32)
    sqs = p.sbuf("sqs", [128, 512], BF16)
    psN1 = p.psum("psN1", [1, 512], F32)
    psT3 = p.psum("psT3", [128, 16], F32)
    psC3 = p.psum("psC3", [128, 8], F32)
    psR4 = p.psum("psR4", [4, 128], F32)
    dtbt = p.sbuf("dtbt", [4, 1], F32)
    alogt = p.sbuf("alogt", [4, 1], F32)
    p.ld(dtbt[:], dtb.rearrange("(a b) -> a b", b=1), tag="dtbld")
    p.ld(alogt[:], alog.rearrange("(a b) -> a b", b=1), tag="alogld")
    p.act(alogt[:], alogt[:], AF.Exp, tag="sg:ea")
    p.ts(alogt[:], alogt[:], -1.0, ALU.mult, tag="sg:nega")

    def softplus_rows(dst, src, tag):
        p.act(tr1[:], src, AF.Abs, tag=f"{tag}:abs")
        p.act(tr1[:], tr1[:], AF.Exp, scale=-1.0, tag=f"{tag}:exp")
        p.act(tr1[:], tr1[:], AF.Ln, bias=1.0, tag=f"{tag}:ln")
        p.ts(dst, src, 0.0, ALU.max, tag=f"{tag}:max")
        p.tt(dst, dst, tr1[:], ALU.add, tag=f"{tag}:add")

    for blk in range(NB):
        bsl = slice(blk * BL, (blk + 1) * BL)
        nt0 = blk * (BL // 128)
        p.ld(braw[:], b_d[:, bsl], tag=f"b3b{blk}")
        p.ld(araw[:], a_d[:, bsl], tag=f"b3a{blk}")
        p.ts(g_r[:], braw[:], -1.0, ALU.mult, tag=f"b3neg{blk}")
        softplus_rows(lnb[:], g_r[:], f"b3sb{blk}")
        p.ts(lnb[:], lnb[:], -1.0, ALU.mult, tag=f"b3lnb{blk}")
        p.act(bco[:], braw[:], AF.Sigmoid, tag=f"b3sig{blk}")
        p.ts(braw[:], araw[:], dtbt[:], ALU.add, tag=f"b3adt{blk}")
        softplus_rows(g_r[:], braw[:], f"b3sg{blk}")
        p.ts(g_r[:], g_r[:], alogt[:], ALU.mult, tag=f"b3g{blk}")
        for r in range(4):
            src_row = r * 128 if r < 2 else 256 + (r - 2) * 128
            for n2 in range(BL // 512):
                gsl = slice(blk * BL + n2 * 512, blk * BL + (n2 + 1) * 512)
                p.ld(sqs[:], qkvc_d[src_row:src_row + 128, gsl],
                     tag=f"b3sq{blk}_{r}_{n2}")
                p.act(sqs[:], sqs[:], AF.Square, tag=f"b3sa{blk}_{r}_{n2}")
                p.mm(psN1[:], ones128[:], sqs[:], tag=f"b3sm{blk}_{r}_{n2}")
                p.act(lnsq[r][0:1, n2 * 512:(n2 + 1) * 512], psN1[:], AF.Ln,
                      bias=1e-6, tag=f"b3sl{blk}_{r}_{n2}")
        for r in range(4):
            p.ld(rtp[r:r + 1, :], lnsq[r][0:1, :], engine="pool",
                 tag=f"b3p{blk}_{r}")
        p.ld(rtp[4:8, :], lnb[:], engine="pool", tag=f"b3pl{blk}")
        p.ld(rtp[8:12, :], g_r[:], engine="pool", tag=f"b3pg{blk}")
        p.ld(rtp[12:16, :], bco[:], engine="pool", tag=f"b3pb{blk}")
        for tt_ in range(BL // 128):
            t = nt0 + tt_
            lsl = slice(tt_ * 128, (tt_ + 1) * 128)
            p.transp(psT3[:], rtp[:, lsl], identT[0:16, 0:16],
                     tag=f"b3t{blk}_{tt_}")
            p.copy(tmv[:, t * 16:(t + 1) * 16], psT3[:], tag=f"b3te{blk}_{tt_}")
        for tt_ in range(BL // 128):
            t = nt0 + tt_
            lsl = slice(tt_ * 128, (tt_ + 1) * 128)
            gslice = tmv[:, t * 16 + 8:t * 16 + 12]
            p.copy(sqs[:, 0:4], gslice, tag=f"b3gb{blk}_{tt_}")
            p.mm(psC3[:, 0:4], bdU[:], sqs[:, 0:4], tag=f"b3c1_{blk}_{tt_}")
            p.mm(psC3[:, 4:8], bdA[:], sqs[:, 0:4], tag=f"b3c2_{blk}_{tt_}")
            p.copy(i1k[:, t * 4:(t + 1) * 4], psC3[:, 0:4], tag=f"b3ce{blk}_{tt_}")
            p.copy(ckd[:, t * 4:(t + 1) * 4], psC3[:, 4:8], tag=f"b3cf{blk}_{tt_}")
            p.mm(psR4[:], sqs[:, 0:4], bdU[:], tag=f"b3gt{blk}_{tt_}")
            p.copy(gcb[:, lsl], psR4[:], tag=f"b3gte{blk}_{tt_}")
            p.mm(psR4[:], sqs[:, 0:4], bdA[:], tag=f"b3cs{blk}_{tt_}")
            p.copy(ecsr[:, t * 2:t * 2 + 2],
                   psR4[:].rearrange("p (a c) -> p a c", a=2)[:, :, 63],
                   tag=f"b3cse{blk}_{tt_}")
        p.fence(tag=f"b3f{blk}")
        p.ld(rows_d[0:4, bsl], gcb[:], engine="sp", tag=f"b3r0{blk}")
        for r in range(4):
            p.ld(rows_d[4 + r:5 + r, bsl], lnsq[r][0:1, :], engine="sp",
                 tag=f"b3r1{blk}_{r}")
        p.ld(rows_d[8:12, bsl], lnb[:], engine="sp", tag=f"b3r2{blk}")
        for rz in range(12, 16):
            p.ld(rows_d[rz:rz + 1, bsl], onesbl[:], engine="sp",
                 tag=f"b3r3{blk}_{rz}")
    p.act(ecsr[:], ecsr[:], AF.Exp, tag="ecse")
    tm16 = tmv[:].rearrange("p (t k) -> p t k", k=16)
    gc4 = i1k[:].rearrange("p (t k) -> p t k", k=4)
    cs4 = ckd[:].rearrange("p (t k) -> p t k", k=4)
    lnb_t = tm16[:, :, 4:8]
    ssqq3 = tm16[:, :, 0:2]
    ssqk3 = tm16[:, :, 2:4]

    def rsl4(ap, rr):
        return ap.rearrange("p (t k) -> p t k", k=4)[:, :, rr::2]

    for rr in range(2):
        p.stt(rsl4(j1t[:], rr), ssqk3, -0.5, rsl4(i1k[:], rr), ALU.mult,
              ALU.subtract, tag=f"cj1_{rr}")
        p.stt(rsl4(i1q[:], rr), ssqq3, -0.5, rsl4(i1k[:], rr), ALU.mult,
              ALU.add, tag=f"ciq1_{rr}")
    p.ts(i1q[:], i1q[:], -0.5 * LOG128, ALU.add, tag="ciq2")
    p.tt(ckd[:].rearrange("p (t k) -> p t k", k=4), cs4, gc4,
         ALU.subtract, tag="ckd1")
    for rr in range(2):
        p.stt(rsl4(ckd[:], rr), ssqk3, -0.5, rsl4(ckd[:], rr), ALU.mult,
              ALU.add, tag=f"ckd2_{rr}")
    p.tt(gc4, gc4, lnb_t, ALU.add, tag="ci1k1")
    for rr in range(2):
        p.stt(rsl4(i1k[:], rr), ssqk3, -0.5, rsl4(i1k[:], rr), ALU.mult,
              ALU.add, tag=f"ci1k2_{rr}")
    p.act(nci[:], i1k[:], AF.Exp, tag="cnci")
    p.ts(nci[:], nci[:], -1.0, ALU.mult, tag="cnci2")
    p.act(ceq[:], i1q[:], AF.Exp, tag="cceq")
    p.act(ckd[:], ckd[:], AF.Exp, tag="cckd")
    p.pop_scope()

    p.push_scope()
    kfm = p.sbuf("kfm", [128, S], BF16)
    qfm = p.sbuf("qfm", [128, S], BF16)
    ktm = p.sbuf("ktm", [128, S], BF16)
    vtm = p.sbuf("vtm", [128, S], BF16)
    Zb = p.sbuf("Zb", [128, S], BF16)
    atb = p.sbuf("atb", [128, S], BF16)
    rstg = [p.sbuf(f"rstg{i}", [16, 1024], F32) for i in range(2)]
    expw = p.sbuf("expw", [128, 128], BF16)
    Mt = p.sbuf("Mt", [128, 128], BF16)
    Nt = p.sbuf("Nt", [128, 128], BF16)
    pw = [p.sbuf(f"pw{i}", [128, 128], BF16) for i in range(2)]
    Tb = [p.sbuf(f"Tb{i}", [128, 128], BF16) for i in range(2)]
    Ssb = p.sbuf("Ssb", [128, 128], BF16)
    vnb = p.sbuf("vnb", [128, 128], BF16)
    kds = p.sbuf("kds", [128, 128], BF16)
    oas = p.sbuf("oas", [128, 128], BF16)
    gcol = p.sbuf("gcol", [128, 1], F32)
    otmp = p.sbuf("otmp", [128, 128], F32)
    psG = p.psum("psG", [128, 128], F32)
    psP = p.psum("psP", [128, 128], F32)
    psP2 = p.psum("psP2", [128, 128], F32)
    psW = p.psum("psW", [128, 128], F32)
    psSt = p.psum("psSt", [128, 128], F32)
    psQ = p.psum("psQ", [128, 128], F32)
    psA2 = p.psum("psA2", [128, 128], F32)
    psg1 = p.psum("psg1", [128, 1], F32)
    tm16b = tmv[:].rearrange("p (t k) -> p t k", k=16)

    for h in range(4):
        qk = h // 2
        qrow, krow, vrow = 128 * qk, 256 + 128 * qk, 512 + 128 * h
        for n in range(nP):
            sl = slice(n * 512, (n + 1) * 512)
            p.ld(kfm[:, sl], qkvc_d[krow:krow + 128, sl], tag=f"h{h}kf{n}")
            p.ld(qfm[:, sl], qkvc_d[qrow:qrow + 128, sl], tag=f"h{h}qf{n}")
        for t in range(nT):
            tsl = slice(t * 128, (t + 1) * 128)
            p.ldT(ktm[:, tsl], qkvc_d[krow:krow + 128, tsl], engine="act",
                  tag=f"h{h}kt{t}")
            p.ldT(vtm[:, tsl], qkvc_d[vrow:vrow + 128, tsl], engine="act",
                  tag=f"h{h}vt{t}")
        selM = selm[:, (0 * 4 + h) * 128:(0 * 4 + h + 1) * 128]
        selN = selm[:, (1 * 4 + h) * 128:(1 * 4 + h + 1) * 128]
        selA = selm[:, (2 * 4 + h) * 128:(2 * 4 + h + 1) * 128]
        for t in range(nT):
            tsl = slice(t * 128, (t + 1) * 128)
            RG = min(8, nT)
            if t % RG == 0:
                rb = rstg[(t // RG) % 2]
                p.ld(rb[:, 0:RG * 128], rows_d[:, t * 128:(t + RG) * 128],
                     engine="pool", tag=f"h{h}rl{t}")
            rsl = rstg[(t // RG) % 2][:, (t % RG) * 128:(t % RG + 1) * 128]
            ck = i1k[:, t * 4 + h:t * 4 + h + 1]
            cj = j1t[:, t * 4 + h:t * 4 + h + 1]
            cb = tm16b[:, t, 12 + h:13 + h]
            p.mm(psG[:], kfm[:, tsl], kfm[:, tsl], tag=f"h{h}G{t}")
            p.mm(psP[:], selM, rsl, start=True, stop=False, tag=f"h{h}pm{t}")
            p.mm(psP[:], identb[:], PMc[:], start=False, stop=True,
                 tag=f"h{h}pm2{t}")
            p.act(expw[:], psP[:], AF.Exp, bias=ck, tag=f"h{h}em{t}")
            p.tt(Mt[:], psG[:], expw[:], ALU.mult, tag=f"h{h}M{t}")
            p.mm(psP2[:], selN, rsl, start=True, stop=False, tag=f"h{h}pn{t}")
            p.mm(psP2[:], identb[:], PNs[:], start=False, stop=True,
                 tag=f"h{h}pn2{t}")
            p.act(expw[:], psP2[:], AF.Exp, bias=cj, tag=f"h{h}en{t}")
            p.tt(Nt[:], psG[:], expw[:], ALU.mult, tag=f"h{h}N{t}")
            p.ts(Tb[0][:], vtm[:, tsl], cb, ALU.mult, tag=f"h{h}T0{t}")
            curM, curN = Mt, Nt
            src = 0
            for lev in range(5):
                p.mm(psP[:], curM[:], curN[:], tag=f"h{h}pwn{t}_{lev}")
                p.mm(psP2[:], curN[:], curM[:], tag=f"h{h}pwm{t}_{lev}")
                pm, pn = pw[0], pw[1]
                p.copy(pm[:], psP2[:], tag=f"h{h}pwem{t}_{lev}")
                p.copy(pn[:], psP[:], engine="act", tag=f"h{h}pwen{t}_{lev}")
                p.mm(psP[:], pn[:], Tb[src][:], tag=f"h{h}ap{t}_{lev}")
                p.tt(Tb[1 - src][:], psP[:], Tb[src][:], ALU.add,
                     tag=f"h{h}ape{t}_{lev}")
                src = 1 - src
                curM, curN = pm, pn
            p.mm(psP[:], Nt[:], Tb[src][:], tag=f"h{h}apf{t}")
            p.raw("dve",
                  lambda e, o_=Zb[:, tsl], a=Tb[src][:], b=psP[:]:
                  e.tensor_tensor(out=o_, in0=a, in1=b, op=ALU.subtract),
                  [psP[:], Tb[src][:]], [Zb[:, tsl]], tag=f"h{h}Z{t}")
            p.mm(psG[:], kfm[:, tsl], qfm[:, tsl], tag=f"h{h}Gq{t}")
            p.mm(psP2[:], selA, rsl, start=True, stop=False, tag=f"h{h}pa{t}")
            p.mm(psP2[:], identb[:], PNi[:], start=False, stop=True,
                 tag=f"h{h}pa2{t}")
            p.act(expw[:], psP2[:], AF.Exp, bias=cj, tag=f"h{h}ea{t}")
            p.tt(atb[:, tsl], psG[:], expw[:], ALU.mult, tag=f"h{h}A{t}")
        p.memset(Ssb[:], 0.0, tag=f"h{h}S0")
        for c in range(nT * 2):
            t, half = c // 2, c % 2
            csl = slice(c * 64, (c + 1) * 64)
            hsl = slice(half * 64, half * 64 + 64)
            p.mm(psW[hsl, :], kfm[:, csl], Ssb[:], tag=f"h{h}W{c}")
            ncol = nci[hsl, t * 4 + h:t * 4 + h + 1]
            p.raw("dve",
                  lambda e, o_=vnb[hsl, :], w=psW[hsl, :], s=ncol,
                  z=Zb[hsl, t * 128:(t + 1) * 128]:
                  e.scalar_tensor_tensor(out=o_, in0=w, scalar=s, in1=z,
                                         op0=ALU.mult, op1=ALU.add),
                  [psW[:], ncol, Zb[hsl, t * 128:(t + 1) * 128]], [vnb[:]],
                  tag=f"h{h}vn{c}")
            p.mm(psQ[hsl, :], qfm[:, csl], Ssb[:], tag=f"h{h}q{c}")
            p.mm(psA2[hsl, :],
                 atb[hsl, t * 128 + half * 64:t * 128 + half * 64 + 64],
                 vnb[hsl, :], tag=f"h{h}av{c}")
            qcol = ceq[hsl, t * 4 + h:t * 4 + h + 1]
            p.act(otmp[hsl, :], psQ[hsl, :], AF.Copy, scale=qcol,
                  engine="act", tag=f"h{h}oe{c}")
            p.tt(oas[hsl, :], otmp[hsl, :], psA2[hsl, :], ALU.add,
                 tag=f"h{h}oa{c}")
            kcol = ckd[hsl, t * 4 + h:t * 4 + h + 1]
            p.ts(kds[hsl, :], ktm[hsl, t * 128:(t + 1) * 128], kcol, ALU.mult,
                 tag=f"h{h}kd{c}")
            p.mm(psSt[:], kds[hsl, :], vnb[hsl, :], tag=f"h{h}st{c}")
            p.mm(psg1[:], ind4h[h], ecsr[:, c:c + 1], tag=f"h{h}g{c}")
            p.copy(gcol[:], psg1[:], tag=f"h{h}ge{c}")
            p.raw("dve",
                  lambda e, o_=Ssb[:], s=gcol[:], ps=psSt[:]:
                  e.scalar_tensor_tensor(out=o_, in0=o_, scalar=s, in1=ps,
                                         op0=ALU.mult, op1=ALU.add),
                  [Ssb[:], gcol[:], psSt[:]], [Ssb[:]], tag=f"h{h}su{c}")
            if half == 1:
                p.ld(core_d[t * 128:(t + 1) * 128, h * 128:(h + 1) * 128],
                     oas[:], engine="sp", tag=f"h{h}os{t}")
    p.pop_scope()

    p.push_scope()
    ztm = p.sbuf("ztm", [128, 512], BF16)
    ctm = p.sbuf("ctm", [128, 512], BF16)
    wrep = p.sbuf("wrep", [128, 512], F32)
    nrow = p.sbuf("nrow", [1, 512], F32)
    sred = p.sbuf("sred", [128, 4], F32)
    g1 = p.sbuf("g1", [128, 512], F32)
    g2 = p.sbuf("g2", [128, 512], F32)
    ob = p.sbuf("ob", [128, 512], BF16)
    psB5 = p.psum("psB5", [128, 512], F32)
    p.ld(nrow[:, 0:128], normw.rearrange("(a b) -> a b", a=1), tag="nwld")
    p.copy(nrow[:, 128:256], nrow[:, 0:128], tag="nw2")
    p.copy(nrow[:, 256:512], nrow[:, 0:256], tag="nw3")
    p.mm(psB5[:], onesr[:], nrow[:], tag="nwb")
    p.copy(wrep[:], psB5[:], tag="nwbe")
    for t in range(nT):
        tsl = slice(t * 128, (t + 1) * 128)
        p.ld(ctm[:], core_d[tsl, :], tag=f"g5c{t}")
        for i in range(4):
            p.ldT(ztm[:, i * 128:(i + 1) * 128],
                  zT_d[i * 128:(i + 1) * 128, tsl], engine="act",
                  tag=f"g5z{t}_{i}")
        p.act(g1[:], ctm[:], AF.Square, tag=f"g5sa{t}")
        p.red(sred[:], g1[:].rearrange("p (h d) -> p h d", h=4), ALU.add,
              tag=f"g5sq{t}")
        p.ts(sred[:], sred[:], 1.0 / 128, ALU.mult, tag=f"g5m{t}")
        p.act(sred[:], sred[:], AF.Sqrt, bias=1e-6, tag=f"g5s{t}")
        p.recip(sred[:], sred[:], tag=f"g5rc{t}")
        srb = sred[:].broadcast_to([128, 4, 128])
        c3 = ctm[:].rearrange("p (h d) -> p h d", h=4)
        p.tt(g1[:].rearrange("p (h d) -> p h d", h=4), c3, srb, ALU.mult,
             tag=f"g5n{t}")
        p.act(g2[:], ztm[:], AF.Silu, tag=f"g5si{t}")
        p.tt(g2[:], g2[:], wrep[:], ALU.mult, engine="pool", tag=f"g5w{t}")
        p.tt(ob[:], g1[:], g2[:], ALU.mult, tag=f"g5o{t}")
        p.ld(o[tsl, :], ob[:], engine="sp", tag=f"g5st{t}")
    p.pop_scope()
    p.build()
    return nc



REC = 840


def pack_wproj_ttt(q_w, k_w, v_w, lr_w, core):
    q = q_w[256 * core:256 * (core + 1)]
    k = k_w[256 * core:256 * (core + 1)]
    v = v_w[256 * core:256 * (core + 1)]
    lr = lr_w[4 * core:4 * (core + 1), 0, :]
    pad = np.zeros((28, 2048), np.float32)
    return np.ascontiguousarray(np.concatenate([q, k, v, lr, pad], 0).T)


def build_l0b(S=8192, D=2048, cut=9):
    nc = bass.Bass("TRN2", target_bir_lowering=False)
    ln = nc.dram_tensor("ln", [S, D], BF16, kind="ExternalInput").ap()
    wproj = nc.dram_tensor("wproj", [D, 800], F32, kind="ExternalInput").ap()
    w1in = nc.dram_tensor("w1in", [256, 64], F32, kind="ExternalInput").ap()
    b1in = nc.dram_tensor("b1in", [1, 256], F32, kind="ExternalInput").ap()
    tnw = nc.dram_tensor("tnw", [1, 256], F32, kind="ExternalInput").ap()
    tnb = nc.dram_tensor("tnb", [1, 256], F32, kind="ExternalInput").ap()
    tib = nc.dram_tensor("tib", [1, 16], F32, kind="ExternalInput").ap()
    tilearn = nc.dram_tensor("tilearn", [1, 16], F32, kind="ExternalInput").ap()
    lrb = nc.dram_tensor("lrb", [4], F32, kind="ExternalInput").ap()
    o = nc.dram_tensor("o", [S, 256], BF16, kind="ExternalOutput").ap()
    qf_d = nc.dram_tensor("qf_d", [256, S], BF16, kind="Internal").ap()
    kf_d = nc.dram_tensor("kf_d", [256, S], BF16, kind="Internal").ap()
    vf_d = nc.dram_tensor("vf_d", [256, S], BF16, kind="Internal").ap()
    st_d = nc.dram_tensor("st_d", [S // 16, 16, REC], BF16, kind="Internal").ap()
    zb_d = nc.dram_tensor("zb_d", [S // 16, 16, 256], BF16, kind="Internal").ap()
    xqa_d = nc.dram_tensor("xqa_d", [S, 256], BF16, kind="Internal").ap()

    p = Prog(nc)
    nT = S // 128
    nP = S // 512
    NS = S // 16

    onesr = p.sbuf("onesr", [1, 128], F32)
    p.memset(onesr[:], 1.0, tag="onesr")
    onescb = p.sbuf("onescb", [16, 1], BF16)
    p.memset(onescb[:], 1.0, tag="onescb")
    ones1b = p.sbuf("ones1b", [1, 16], BF16)
    p.memset(ones1b[:], 1.0, tag="ones1b")

    p.push_scope()
    lnp = [[p.sbuf(f"lnp{b}_{i}", [128, 512], BF16) for i in range(16)]
           for b in range(2)]
    wt = [p.sbuf(f"wt{i}", [128, 800], BF16) for i in range(16)]
    for i in range(16):
        p.ld(wt[i][:], wproj[i * 128:(i + 1) * 128, :], engine="pool",
             tag=f"wld{i}")
    psE = [p.psum(f"psE{i}", [128, 512], F32) for i in range(2)]
    evb = [p.sbuf(f"evb{i}", [128, 512], BF16) for i in range(2)]
    lrr = p.sbuf("lrr", [4, S], F32, side="right")
    for n in range(nP):
        b = n % 2
        sl = slice(n * 512, (n + 1) * 512)
        for i in range(16):
            p.ldT(lnp[b][i][:], ln[sl, i * 128:(i + 1) * 128], engine="act",
                  tag=f"lnp{n}_{i}")
        for m in range(7):
            M0, M1 = m * 128, min(800, (m + 1) * 128)
            j = m % 2
            for k in range(16):
                p.mm(psE[j][0:M1 - M0, :], wt[k][:, M0:M1], lnp[b][k][:],
                     start=(k == 0), stop=(k == 15), tag=f"pj{n}_{m}_{k}")
            if m < 6:
                p.copy(evb[j][:], psE[j][:], tag=f"pje{n}_{m}")
                dst = [qf_d, qf_d, kf_d, kf_d, vf_d, vf_d][m]
                r0 = (m % 2) * 128
                p.ld(dst[r0:r0 + 128, sl], evb[j][:], engine="sp",
                     tag=f"pjs{n}_{m}")
            else:
                p.copy(lrr[:, sl], psE[j][0:4, :], tag=f"pjl{n}")
    p.pop_scope()

    grow = p.sbuf("grow", [1, 256], F32)
    browf = p.sbuf("browf", [1, 256], F32)
    trow = p.sbuf("trow", [1, 16], F32)
    trow512 = p.sbuf("trow512", [1, 512], F32)
    ti15 = p.sbuf("ti15", [16, 1], F32)
    G2R16 = p.sbuf("G2R16", [16, 256], F32)
    gam16 = p.sbuf("gam16", [16, 256], F32)
    bet16 = p.sbuf("bet16", [16, 256], F32)
    TIMbig = p.sbuf("TIMbig", [16, 512], BF16)
    psW = p.psum("psW", [128, 512], F32)
    p.ld(grow[:], tnw[:], tag="gld")
    p.ld(browf[:], tnb[:], tag="bld")
    p.mm(psW[0:16, 0:256], onesr[:, 0:16], grow[:], tag="gb1")
    p.copy(gam16[:], psW[0:16, 0:256], tag="gb2")
    p.act(G2R16[:], gam16[:], AF.Square, tag="gb3")
    p.mm(psW[0:16, 0:256], onesr[:, 0:16], browf[:], tag="bb1")
    p.copy(bet16[:], psW[0:16, 0:256], tag="bb2")
    tl2 = p.sbuf("tl2", [1, 16], F32)
    p.ld(trow[:], tib[:], tag="tld")
    p.ld(tl2[:], tilearn[:], tag="tl2")
    p.tt(trow[:], trow[:], tl2[:], ALU.add, tag="tadd")
    p.act(trow[:], trow[:], AF.Relu, tag="trelu")
    p.raw("dve",
          lambda e: e.tensor_copy(
              out=trow512[:].rearrange("a (s c) -> a s c", c=16),
              in_=trow[:].rearrange("a c -> a c").broadcast_to([1, 16, 32])
              .transpose([0, 2, 1])),
          [trow[:]], [trow512[:]], tag="t512")
    p.mm(psW[0:16, 0:512], onesr[:, 0:16], trow512[:], tag="tb1")
    p.memset(TIMbig[:], -1.0, tag="tim0")
    p.affine_select(TIMbig[:].rearrange("p (s c) -> p s c", c=16),
                    TIMbig[:].rearrange("p (s c) -> p s c", c=16),
                    pattern=[[0, 32], [1, 16]], compare_op=ALU.is_ge,
                    fill=0.0, base=0, channel_multiplier=-1, tag="tim1")
    p.raw("dve", lambda e: e.tensor_tensor(out=TIMbig[:], in0=TIMbig[:],
                                           in1=psW[0:16, 0:512], op=ALU.mult),
          [TIMbig[:], psW[:]], [TIMbig[:]], tag="tim2")
    p.mm(psW[0:16, 0:1], onesr[:, 0:16], trow[:, 15:16], tag="ti151")
    p.copy(ti15[:], psW[0:16, 0:1], tag="ti152")
    lrbt = p.sbuf("lrbt", [4, 1], F32)
    p.ld(lrbt[:], lrb.rearrange("(a b) -> a b", b=1), tag="lrbld")
    p.ts(lrr[:], lrr[:], lrbt[:], ALU.add, tag="slr0")
    p.act(lrr[:], lrr[:], AF.Sigmoid, tag="slr1")
    p.ts(lrr[:], lrr[:], 1.0 / 64, ALU.mult, tag="slr2")

    p.push_scope()
    xq_t = p.sbuf("xq_t", [128, 256], BF16)
    xk_t = p.sbuf("xk_t", [128, 256], BF16)
    xv_t = p.sbuf("xv_t", [128, 256], BF16)
    tg_t = p.sbuf("tg_t", [128, 256], BF16)
    t1_t = p.sbuf("t1_t", [128, 256], F32)
    brep = p.sbuf("brep", [128, 256], F32)
    grep = p.sbuf("grep", [128, 256], F32)
    slr_b = p.sbuf("slr_b", [128, 8], BF16)
    psT = p.psum("psT", [128, 16], F32)
    identT = p.sbuf("identT", [16, 16], F32)
    p.memset(identT[:], 1.0, tag="idT0")
    p.affine_select(identT[:], identT[:], pattern=[[-1, 16]],
                    compare_op=ALU.is_equal, fill=0.0, base=0,
                    channel_multiplier=1, tag="idT1")
    p.mm(psW[:, 0:256], onesr[:], browf[:], tag="brep1")
    p.copy(brep[:], psW[:, 0:256], tag="brep2")
    p.mm(psW[:, 0:256], onesr[:], grow[:], tag="grep1")
    p.copy(grep[:], psW[:, 0:256], tag="grep2")
    for t in range(nT):
        tsl = slice(t * 128, (t + 1) * 128)
        for pr in range(2):
            psl = slice(pr * 128, (pr + 1) * 128)
            p.ldT(xq_t[:, psl], qf_d[psl, tsl], engine="act", tag=f"q2t{t}_{pr}")
            p.ldT(xk_t[:, psl], kf_d[psl, tsl], engine="act", tag=f"k2t{t}_{pr}")
            p.ldT(xv_t[:, psl], vf_d[psl, tsl], engine="act", tag=f"v2t{t}_{pr}")
        p.raw("dve", lambda e, o_=t1_t[:], a=xk_t[:], b=xv_t[:]:
              e.tensor_tensor(out=o_, in0=a, in1=b, op=ALU.subtract),
              [xk_t[:], xv_t[:]], [t1_t[:]], tag=f"tg1{t}")
        p.tt(t1_t[:], t1_t[:], brep[:], ALU.add, engine="pool", tag=f"tg2{t}")
        p.tt(tg_t[:], t1_t[:], grep[:], ALU.mult, tag=f"tg3{t}")
        p.transp(psT[:, 0:4], lrr[:, tsl], identT[0:4, 0:4], tag=f"sl1{t}")
        if t == 0:
            p.memset(slr_b[:, 4:8], 0.0, tag="slrz")
        p.copy(slr_b[:, 0:4], psT[0:128, 0:4], tag=f"sl2{t}")
        t0 = t * 8
        p.ld(st_d[t0:t0 + 8, :, 0:256].rearrange("s p c -> (s p) c"),
             xq_t[:], engine="sp", tag=f"sw1{t}")
        p.ld(st_d[t0:t0 + 8, :, 256:512].rearrange("s p c -> (s p) c"),
             xk_t[:], engine="sp", tag=f"sw2{t}")
        p.ld(st_d[t0:t0 + 8, :, 512:768].rearrange("s p c -> (s p) c"),
             tg_t[:], engine="sp", tag=f"sw3{t}")
        p.ld(st_d[t0:t0 + 8, :, 832:840].rearrange("s p c -> (s p) c"),
             slr_b[:], engine="pool", tag=f"sw4{t}")
        p.ld(xqa_d[tsl, :], xq_t[:], engine="pool", tag=f"sw5{t}")
    p.pop_scope()

    p.push_scope()
    kh = [p.sbuf(f"kh{h}", [64, 128], BF16) for h in range(4)]
    qh = [p.sbuf(f"qh{h}", [64, 128], BF16) for h in range(4)]
    g2f = p.sbuf("g2f", [16, 512], F32)
    g2s = p.sbuf("g2s", [16, 512], BF16)
    psA = p.psum("psA", [16, 512], F32)
    for grp in range(NS // 8):
        ssl = slice(grp * 128, (grp + 1) * 128)
        for h in range(4):
            p.ld(kh[h][:], kf_d[64 * h:64 * h + 64, ssl], tag=f"gk{grp}_{h}")
            p.ld(qh[h][:], qf_d[64 * h:64 * h + 64, ssl], tag=f"gq{grp}_{h}")
        for sl_ in range(8):
            for h in range(4):
                tsl2 = slice(sl_ * 16, sl_ * 16 + 16)
                p.mm(psA[0:16, 16 * (4 * sl_ + h):16 * (4 * sl_ + h) + 16],
                     kh[h][:, tsl2], qh[h][:, tsl2],
                     tag=f"a1_{grp}_{sl_}_{h}")
        p.ts(g2f[:], psA[:], 1.0, ALU.add, tag=f"g2a{grp}")
        p.tt(g2s[:], g2f[:], TIMbig[:], ALU.mult, tag=f"g2b{grp}")
        p.ld(st_d[grp * 8:grp * 8 + 8, :, 768:832].rearrange("s p c -> p s c"),
             g2s[:].rearrange("p (s c) -> p s c", s=8), engine="sp",
             tag=f"g2w{grp}")
    p.pop_scope()

    p.push_scope()
    rc = [p.sbuf(f"rc{i}", [16, REC], BF16) for i in range(2)]
    fmq = [p.sbuf(f"fmq{i}", [128, 512], BF16) for i in range(4)]
    fmk = [p.sbuf(f"fmk{i}", [128, 512], BF16) for i in range(4)]
    W1p = [p.sbuf(f"W1p{i}", [128, 128], BF16) for i in range(2)]
    b1rowp = [p.sbuf(f"b1rowp{i}", [1, 128], F32) for i in range(2)]
    b1rbp = [p.sbuf(f"b1rbp{i}", [1, 128], BF16) for i in range(2)]
    sqsp = [p.sbuf(f"sqsp{i}", [16, 128], F32) for i in range(2)]
    zcp = [p.sbuf(f"zcp{i}", [16, 128], F32) for i in range(2)]
    xhp = [p.sbuf(f"xhp{i}", [16, 128], F32) for i in range(2)]
    gxhp = [p.sbuf(f"gxhp{i}", [16, 128], F32) for i in range(2)]
    prodp = [p.sbuf(f"prodp{i}", [16, 128], F32) for i in range(2)]
    tvecp = [p.sbuf(f"tvecp{i}", [16, 128], F32) for i in range(2)]
    grdp = [p.sbuf(f"grdp{i}", [16, 128], F32) for i in range(2)]
    grad2p = [p.sbuf(f"grad2p{i}", [16, 128], BF16) for i in range(2)]
    gradLp = [p.sbuf(f"gradLp{i}", [16, 128], BF16) for i in range(2)]
    zbbp = [p.sbuf(f"zbbp{i}", [16, 128], BF16) for i in range(2)]
    psZrp = [p.sbuf(f"psZrp{i}", [16, 128], F32) for i in range(2)]
    sredp = [p.sbuf(f"sredp{i}", [16, 6], F32) for i in range(2)]
    psZp = [p.psum(f"psZp{i}", [16, 128], F32) for i in range(2)]
    psBp = [p.psum(f"psBp{i}", [33, 128], F32) for i in range(2)]
    psUp = [p.psum(f"psUp{i}", [128, 128], F32) for i in range(2)]
    for pr in range(2):
        p.memset(W1p[pr][:], 0.0, tag=f"w1z{pr}")
        for j in range(2):
            h = 2 * pr + j
            p.ld(W1p[pr][64 * j:64 * j + 64, 64 * j:64 * j + 64],
                 w1in[64 * h:64 * (h + 1), :], engine="pool", tag=f"w1i{h}")
        p.ld(b1rowp[pr][:], b1in[:, 128 * pr:128 * pr + 128], tag=f"b1i{pr}")
        p.copy(b1rbp[pr][:], b1rowp[pr][:], tag=f"b1ib{pr}")

    def bv2(ap):
        return ap.broadcast_to([16, 2, 64])

    def h3(ap):
        return ap.rearrange("p (h d) -> p h d", h=2)

    for s in range(NS):
        rb = rc[s % 2]
        p.ld(rb[:], st_d[s, :, :], tag=f"s{s}rc")
        if s % 32 == 0:
            g = (s // 32) % 2
            ssl = slice(s * 16, s * 16 + 512)
            p.ld(fmq[2 * g][:], qf_d[0:128, ssl], engine="pool", tag=f"s{s}fq0")
            p.ld(fmq[2 * g + 1][:], qf_d[128:256, ssl], engine="pool",
                 tag=f"s{s}fq1")
            p.ld(fmk[2 * g][:], kf_d[0:128, ssl], engine="pool", tag=f"s{s}fk0")
            p.ld(fmk[2 * g + 1][:], kf_d[128:256, ssl], engine="pool",
                 tag=f"s{s}fk1")
        g = (s // 32) % 2
        lsl = slice((s % 32) * 16, (s % 32) * 16 + 16)
        C0 = [0, 128]
        for pr in range(2):
            p.mm(psZp[pr][0:16, :], fmk[2 * g + pr][:, lsl], W1p[pr][:],
                 start=True, stop=False, tag=f"s{s}z{pr}")
            p.mm(psZp[pr][0:16, :], ones1b[:], b1rbp[pr][:],
                 start=False, stop=True, tag=f"s{s}zb{pr}")
        for pr in range(2):
            mu = sredp[pr][:, 0:2]
            p.red(mu, h3(psZp[pr][0:16, :]), ALU.add, tag=f"s{s}mu{pr}")
        for pr in range(2):
            p.copy(psZrp[pr][:], psZp[pr][0:16, :], tag=f"s{s}ev{pr}")
        for pr in range(2):
            p.tt(sqsp[pr][:], psZrp[pr][:], psZrp[pr][:], ALU.mult,
                 tag=f"s{s}sq{pr}")
        for pr in range(2):
            p.red(sredp[pr][:, 2:4], h3(sqsp[pr][:]), ALU.add,
                  tag=f"s{s}ss{pr}")
        for pr in range(2):
            mu = sredp[pr][:, 0:2]
            p.ts(mu, mu, 1.0 / 64, ALU.mult, tag=f"s{s}mu2{pr}")
        for pr in range(2):
            mu = sredp[pr][:, 0:2]
            p.tt(sredp[pr][:, 4:6], mu, mu, ALU.mult, tag=f"s{s}m2{pr}")
        for pr in range(2):
            ssq, m2 = sredp[pr][:, 2:4], sredp[pr][:, 4:6]
            p.raw("dve", lambda e, a=ssq, b=m2, v=ssq:
                  e.scalar_tensor_tensor(out=v, in0=a, scalar=1.0 / 64, in1=b,
                                         op0=ALU.mult, op1=ALU.subtract),
                  [ssq, m2], [ssq], tag=f"s{s}var{pr}")
        for pr in range(2):
            var, rstd = sredp[pr][:, 2:4], sredp[pr][:, 4:6]
            p.act(rstd, var, AF.Sqrt, bias=1e-6, tag=f"s{s}sd{pr}")
        for pr in range(2):
            rstd = sredp[pr][:, 4:6]
            p.recip(rstd, rstd, tag=f"s{s}rs{pr}")
        for pr in range(2):
            mu = sredp[pr][:, 0:2]
            p.raw("dve", lambda e, z=zcp[pr][:], ps=psZp[pr][0:16, :],
                  m=bv2(mu):
                  e.tensor_tensor(out=z, in0=ps, in1=m, op=ALU.subtract),
                  [psZp[pr][:], mu], [zcp[pr][:]], tag=f"s{s}zc{pr}")
        for pr in range(2):
            rstd = sredp[pr][:, 4:6]
            p.tt(h3(xhp[pr][:]), h3(zcp[pr][:]), bv2(rstd), ALU.mult,
                 engine="pool", tag=f"s{s}xh{pr}")
        for pr in range(2):
            p.tt(gxhp[pr][:], xhp[pr][:], G2R16[:, C0[pr]:C0[pr] + 128],
                 ALU.mult, tag=f"s{s}g1{pr}")
        for pr in range(2):
            p.tt(gxhp[pr][:], gxhp[pr][:],
                 rb[:, 512 + C0[pr]:512 + C0[pr] + 128], ALU.add,
                 engine="pool", tag=f"s{s}g2{pr}")
        for pr in range(2):
            p.red(sredp[pr][:, 0:2], h3(gxhp[pr][:]), ALU.add,
                  tag=f"s{s}sg{pr}")
        for pr in range(2):
            p.tt(prodp[pr][:], gxhp[pr][:], xhp[pr][:], ALU.mult,
                 engine="pool", tag=f"s{s}pr{pr}")
        for pr in range(2):
            p.red(sredp[pr][:, 2:4], h3(prodp[pr][:]), ALU.add,
                  tag=f"s{s}sgx{pr}")
        for pr in range(2):
            p.ts(sredp[pr][:, 0:4], sredp[pr][:, 0:4], 1.0 / 64, ALU.mult,
                 tag=f"s{s}m14{pr}")
        for pr in range(2):
            p.tt(h3(tvecp[pr][:]), h3(xhp[pr][:]), bv2(sredp[pr][:, 2:4]),
                 ALU.mult, tag=f"s{s}t1{pr}")
        for pr in range(2):
            p.tt(tvecp[pr][:], gxhp[pr][:], tvecp[pr][:], ALU.subtract,
                 engine="pool", tag=f"s{s}t2{pr}")
        for pr in range(2):
            p.raw("dve", lambda e, o_=h3(tvecp[pr][:]), a=h3(tvecp[pr][:]),
                  b=bv2(sredp[pr][:, 0:2]):
                  e.tensor_tensor(out=o_, in0=a, in1=b, op=ALU.subtract),
                  [tvecp[pr][:], sredp[pr][:, 0:2]], [tvecp[pr][:]],
                  tag=f"s{s}t3{pr}")
        for pr in range(2):
            p.tt(h3(grdp[pr][:]), h3(tvecp[pr][:]), bv2(sredp[pr][:, 4:6]),
                 ALU.mult, engine="pool", tag=f"s{s}gr{pr}")
        for pr in range(2):
            p.tt(h3(grad2p[pr][:]), h3(grdp[pr][:]),
                 rb[:, 832 + 2 * pr:832 + 2 * pr + 2].broadcast_to([16, 2, 64]),
                 ALU.mult, tag=f"s{s}g2b{pr}")
        for pr in range(2):
            p.ts(gradLp[pr][:], grad2p[pr][:], ti15[:], ALU.mult,
                 tag=f"s{s}gL{pr}")
        for pr in range(2):
            p.mm(psBp[pr][0:16, :], fmq[2 * g + pr][:, lsl], W1p[pr][:],
                 start=True, stop=False, tag=f"s{s}q{pr}")
        for pr in range(2):
            for j in range(2):
                h = 2 * pr + j
                p.mm(psBp[pr][0:16, 64 * j:64 * j + 64],
                     rb[:, 768 + 16 * h:768 + 16 * h + 16],
                     grad2p[pr][:, 64 * j:64 * j + 64],
                     start=False, stop=False, tag=f"s{s}G{h}")
        for pr in range(2):
            p.mm(psBp[pr][0:16, :], ones1b[:], b1rbp[pr][:],
                 start=False, stop=True, tag=f"s{s}bb{pr}")
            p.mm(psBp[pr][32:33, :], onescb[:], gradLp[pr][:],
                 start=True, stop=True, tag=f"s{s}b1{pr}")
        for pr in range(2):
            p.copy(zbbp[pr][:], psBp[pr][0:16, :], tag=f"s{s}ze{pr}")
        for pr in range(2):
            p.ld(zb_d[s, :, C0[pr]:C0[pr] + 128], zbbp[pr][:], engine="act",
                 tag=f"s{s}zw{pr}")
        for pr in range(2):
            for j in range(2):
                h = 2 * pr + j
                qs = slice(64 * j, 64 * j + 64)
                p.mm(psUp[pr][qs, qs],
                     rb[:, 256 + 64 * h:256 + 64 * h + 64],
                     gradLp[pr][:, 64 * j:64 * j + 64],
                     start=True, stop=True, tag=f"s{s}w{h}")
        for pr in range(2):
            for j in range(2):
                qs = slice(64 * j, 64 * j + 64)
                p.raw("dve", lambda e, w=W1p[pr][qs, qs], ps=psUp[pr][qs, qs]:
                      e.tensor_tensor(out=w, in0=w, in1=ps, op=ALU.subtract),
                      [W1p[pr][:], psUp[pr][:]], [W1p[pr][:]],
                      tag=f"s{s}wu{pr}_{j}")
        for pr in range(2):
            p.raw("dve", lambda e, b=b1rowp[pr][:], ps=psBp[pr][32:33, :]:
                  e.tensor_tensor(out=b, in0=b, in1=ps, op=ALU.subtract),
                  [b1rowp[pr][:], psBp[pr][:]], [b1rowp[pr][:]],
                  tag=f"s{s}bu{pr}")
        for pr in range(2):
            p.copy(b1rbp[pr][:], b1rowp[pr][:], engine="pool",
                   tag=f"s{s}bc{pr}")
    p.pop_scope()

    p.push_scope()
    zt = p.sbuf("zt", [128, 256], BF16)
    xqt = p.sbuf("xqt", [128, 256], BF16)
    sq6 = p.sbuf("sq6", [128, 256], F32)
    n6 = p.sbuf("n6", [128, 256], F32)
    ob6 = p.sbuf("ob6", [128, 256], BF16)
    sr6 = p.sbuf("sr6", [128, 12], F32)
    grep2 = p.sbuf("grep2", [128, 256], F32)
    brep2 = p.sbuf("brep2", [128, 256], F32)
    p.mm(psW[:, 0:256], onesr[:], grow[:], tag="g6a")
    p.copy(grep2[:], psW[:, 0:256], tag="g6b")
    p.mm(psW[:, 0:256], onesr[:], browf[:], tag="g6c")
    p.copy(brep2[:], psW[:, 0:256], tag="g6d")
    p.memset(ob6[:], 0.0, tag="ob6z")
    for t in range(nT):
        tsl = slice(t * 128, (t + 1) * 128)
        p.ld(o[tsl, :], ob6[:], engine="pool", tag=f"dd{t}st")
        p.ld(zt[:], zb_d.rearrange("s p c -> (s p) c")[tsl, :], tag=f"d{t}z")
        p.ld(xqt[:], xqa_d[tsl, :], tag=f"d{t}x")
        mu6, ss6, rs6 = sr6[:, 0:4], sr6[:, 4:8], sr6[:, 8:12]
        if cut < 7.2:
            continue
        p.red(mu6, zt[:].rearrange("p (h d) -> p h d", h=4), ALU.add,
              tag=f"d{t}mu")
        p.act(sq6[:], zt[:], AF.Square, tag=f"d{t}sq")
        p.red(ss6, sq6[:].rearrange("p (h d) -> p h d", h=4), ALU.add,
              tag=f"d{t}ss")
        p.ts(mu6, mu6, 1.0 / 64, ALU.mult, tag=f"d{t}m")
        p.tt(sr6[:, 8:12], mu6, mu6, ALU.mult, tag=f"d{t}m2")
        p.raw("dve", lambda e, a=ss6, b=sr6[:, 8:12], v=ss6:
              e.scalar_tensor_tensor(out=v, in0=a, scalar=1.0 / 64, in1=b,
                                     op0=ALU.mult, op1=ALU.subtract),
              [ss6, sr6[:, 8:12]], [ss6], tag=f"d{t}v")
        if cut < 7.4:
            continue
        p.act(rs6, ss6, AF.Sqrt, bias=1e-6, tag=f"d{t}sd")
        p.recip(rs6, rs6, tag=f"d{t}rs")
        if cut < 7.6:
            continue
        bv6 = lambda ap: ap.broadcast_to([128, 4, 64])
        p.raw("dve", lambda e, o_=n6[:].rearrange("p (h d) -> p h d", h=4),
              a=zt[:].rearrange("p (h d) -> p h d", h=4), b=bv6(mu6):
              e.tensor_tensor(out=o_, in0=a, in1=b, op=ALU.subtract),
              [zt[:], mu6], [n6[:]], tag=f"d{t}zc")
        p.tt(n6[:].rearrange("p (h d) -> p h d", h=4),
             n6[:].rearrange("p (h d) -> p h d", h=4), bv6(rs6), ALU.mult,
             engine="pool", tag=f"d{t}xh")
        if cut < 7.8:
            continue
        p.tt(n6[:], n6[:], grep2[:], ALU.mult, tag=f"d{t}g")
        if cut < 7.75:
            continue
        p.tt(n6[:], n6[:], brep2[:], ALU.add, engine="pool", tag=f"d{t}b")
        if cut < 7.85:
            continue
        p.raw("dve", lambda e, o_=ob6[:], a=n6[:], b=xqt[:]:
              e.tensor_tensor(out=o_, in0=a, in1=b, op=ALU.add),
              [n6[:], xqt[:]], [ob6[:]], tag=f"d{t}o")
        p.ld(o[tsl, :], ob6[:], engine="pool", tag=f"d{t}st")
    p.pop_scope()
    p.build()
    return nc



BIG = 1e9


def pack_win(W_in, core):
    z = W_in[512 * core:512 * (core + 1)]
    x = W_in[4096 + 512 * core:4096 + 512 * (core + 1)]
    B = W_in[8192:8256]
    C = W_in[8256:8320]
    dt = W_in[8320 + 8 * core:8320 + 8 * (core + 1)]
    pad = np.zeros((24, 2048), np.float32)
    return np.ascontiguousarray(np.concatenate([z, x, B, C, dt, pad], 0).T)


def pack_convw_l1b(conv_w, conv_b, core):
    xw = conv_w[512 * core:512 * (core + 1)]
    cw = np.concatenate([xw, conv_w[4096:4224]], 0)
    xb = conv_b[512 * core:512 * (core + 1)]
    cb = np.concatenate([xb, conv_b[4096:4224]], 0)
    return np.ascontiguousarray(cw), np.ascontiguousarray(cb)


def build_l1b(S=8192, D=2048):
    nc = bass.Bass("TRN2", target_bir_lowering=False)
    ln = nc.dram_tensor("ln", [S, D], BF16, kind="ExternalInput").ap()
    win = nc.dram_tensor("win", [D, 1184], F32, kind="ExternalInput").ap()
    convw = nc.dram_tensor("convw", [640, 4], F32, kind="ExternalInput").ap()
    convb = nc.dram_tensor("convb", [640], F32, kind="ExternalInput").ap()
    dtb = nc.dram_tensor("dtb", [8], F32, kind="ExternalInput").ap()
    alog = nc.dram_tensor("alog", [8], F32, kind="ExternalInput").ap()
    dpar = nc.dram_tensor("dpar", [1, 8], F32, kind="ExternalInput").ap()
    u = nc.dram_tensor("u", [S, 512], BF16, kind="ExternalOutput").ap()
    pre_d = nc.dram_tensor("pre_d", [640, S], BF16, kind="Internal").ap()
    xc_d = nc.dram_tensor("xc_d", [640, S], BF16, kind="Internal").ap()
    zT_d = nc.dram_tensor("zT_d", [512, S], BF16, kind="Internal").ap()
    dtr_d = nc.dram_tensor("dtr_d", [8, S], F32, kind="Internal").ap()
    rows_dm = nc.dram_tensor("rows_dm", [16, S], F32, kind="Internal").ap()

    p = Prog(nc)
    nT = S // 128
    nP = S // 512
    BL = min(2048, S)
    NB = S // BL
    nC = S // 64

    glt = p.sbuf("glt", [128, nT * 8], F32)
    wco = p.sbuf("wco", [128, nT * 8], F32)
    egc = p.sbuf("egc", [128, nT * 8], F32)
    gaml = p.sbuf("gaml", [64, nC * 8], F32)
    esc = p.sbuf("esc", [8, nC], F32)
    identT = p.sbuf("identT", [128, 128], F32)
    p.memset(identT[:], 1.0, tag="idT0")
    p.affine_select(identT[:], identT[:], pattern=[[-1, 128]],
                    compare_op=ALU.is_equal, fill=0.0, base=0,
                    channel_multiplier=1, tag="idT1")
    identb = p.sbuf("identb", [128, 128], BF16)
    p.copy(identb[:], identT[:], tag="idb")
    bdU = p.sbuf("bdU", [128, 128], BF16)
    p.memset(bdU[:], 1.0, tag="bdU0")
    p.affine_select(bdU[:], bdU[:], pattern=[[1, 128]], compare_op=ALU.is_ge,
                    fill=0.0, base=0, channel_multiplier=-1, tag="bdU1")
    p.memset(bdU[0:64, 64:128], 0.0, tag="bdU2")
    bdA = p.sbuf("bdA", [128, 128], BF16)
    p.memset(bdA[:], 1.0, tag="bdA0")
    p.memset(bdA[0:64, 64:128], 0.0, tag="bdA1")
    p.memset(bdA[64:128, 0:64], 0.0, tag="bdA2")
    PNi = p.sbuf("PNi", [128, 128], BF16)
    p.memset(PNi[:], 0.0, tag="PNi0")
    p.affine_select(PNi[:], PNi[:], pattern=[[1, 128]], compare_op=ALU.is_ge,
                    fill=-BIG, base=0, channel_multiplier=-1, tag="PNi1")
    p.memset(PNi[0:64, 64:128], -BIG, tag="PNi2")
    ind8 = p.sbuf("ind8", [8, 8 * 128], F32)
    p.memset(ind8[:], 1.0, tag="i80")
    p.affine_select(ind8[:].rearrange("p (b f) -> p b f", b=8),
                    ind8[:].rearrange("p (b f) -> p b f", b=8),
                    pattern=[[-1, 8], [0, 128]], compare_op=ALU.is_equal,
                    fill=0.0, base=0, channel_multiplier=1, tag="i81")
    ind8s = p.sbuf("ind8s", [8, 8], F32)
    p.memset(ind8s[:], 1.0, tag="i8s0")
    p.affine_select(ind8s[:], ind8s[:], pattern=[[-1, 8]],
                    compare_op=ALU.is_equal, fill=0.0, base=0,
                    channel_multiplier=1, tag="i8s1")
    ones8c = p.sbuf("ones8c", [8, 64], F32)
    p.memset(ones8c[:], 1.0, tag="ones8c")
    onesr = p.sbuf("onesr", [1, 128], F32)
    p.memset(onesr[:], 1.0, tag="onesr")

    p.push_scope()
    lnp = [[p.sbuf(f"lnp{b}_{i}", [128, 512], BF16) for i in range(16)]
           for b in range(2)]
    wt = [p.sbuf(f"wt{i}", [128, 1184], BF16) for i in range(16)]
    for i in range(16):
        p.ld(wt[i][:], win[i * 128:(i + 1) * 128, :], engine="pool",
             tag=f"wld{i}")
    psE = [p.psum(f"psE{i}", [128, 512], F32) for i in range(2)]
    evb = [p.sbuf(f"evb{i}", [128, 512], BF16) for i in range(2)]
    std_ = p.sbuf("std_", [8, 512], F32)
    for n in range(nP):
        b = n % 2
        sl = slice(n * 512, (n + 1) * 512)
        for i in range(16):
            p.ldT(lnp[b][i][:], ln[sl, i * 128:(i + 1) * 128], engine="act",
                  tag=f"lnp{n}_{i}")
        for m in range(10):
            M0, M1 = m * 128, min(1184, (m + 1) * 128)
            j = m % 2
            for k in range(16):
                p.mm(psE[j][0:M1 - M0, :], wt[k][:, M0:M1], lnp[b][k][:],
                     start=(k == 0), stop=(k == 15), tag=f"pj{n}_{m}_{k}")
            if m < 4:
                p.copy(evb[j][:], psE[j][:], tag=f"pje{n}_{m}")
                p.ld(zT_d[M0:M0 + 128, sl], evb[j][:], engine="sp",
                     tag=f"pjs{n}_{m}")
            elif m < 9:
                p.copy(evb[j][:], psE[j][:], tag=f"pje{n}_{m}")
                p.ld(pre_d[M0 - 512:M0 - 384, sl], evb[j][:], engine="sp",
                     tag=f"pjs{n}_{m}")
            else:
                p.copy(std_[:], psE[j][0:8, :], tag=f"pjd{n}")
                p.ld(dtr_d[:, sl], std_[:], engine="sp", tag=f"pds{n}")
    p.pop_scope()

    p.push_scope()
    cw = p.sbuf("cw", [128, 4 * 5], F32)
    cb = p.sbuf("cb", [128, 5], F32)
    for t in range(5):
        p.ld(cw[:, t * 4:(t + 1) * 4], convw[t * 128:(t + 1) * 128, :],
             engine="pool", tag=f"cwld{t}")
    p.ld(cb[:], convb.rearrange("(a b) -> b a", b=128), engine="pool",
         tag="cbld", allow_slow_non_contiguous=True)
    cin = p.sbuf("cin", [128, S + 3], BF16)
    cacc = [p.sbuf(f"cacc{i}", [128, S], F32) for i in range(2)]
    cout = p.sbuf("cout", [128, S], BF16)
    p.memset(cin[:, 0:3], 0.0, tag="cpad")
    for t in range(5):
        p.ld(cin[:, 3:S + 3], pre_d[t * 128:(t + 1) * 128, :], tag=f"cld{t}")
        p.ts(cacc[0][:], cin[:, 0:S], cw[:, t * 4:t * 4 + 1], ALU.mult,
             tag=f"cv{t}_0")
        p.stt(cacc[1][:], cin[:, 1:S + 1], cw[:, t * 4 + 1:t * 4 + 2],
              cacc[0][:], ALU.mult, ALU.add, tag=f"cv{t}_1")
        p.stt(cacc[0][:], cin[:, 2:S + 2], cw[:, t * 4 + 2:t * 4 + 3],
              cacc[1][:], ALU.mult, ALU.add, tag=f"cv{t}_2")
        p.stt(cacc[1][:], cin[:, 3:S + 3], cw[:, t * 4 + 3:t * 4 + 4],
              cacc[0][:], ALU.mult, ALU.add, tag=f"cv{t}_3")
        p.act(cout[:], cacc[1][:], AF.Silu, bias=cb[:, t:t + 1], tag=f"cvs{t}")
        p.ld(xc_d[t * 128:(t + 1) * 128, :], cout[:], engine="sp",
             tag=f"cst{t}")
    p.pop_scope()

    gcbm = p.sbuf("gcbm", [8, BL], F32, side="right")
    zrow = p.sbuf("zrow", [8, BL], F32, side="right")
    p.push_scope()
    dtraw = p.sbuf("dtraw", [8, BL], F32)
    tr1 = p.sbuf("tr1", [8, BL], F32)
    dt_r = p.sbuf("dt_r", [8, BL], F32)
    lndt_r = p.sbuf("lndt_r", [8, BL], F32)
    adt_r = p.sbuf("adt_r", [8, BL], F32)
    rtp = p.sbuf("rtp", [16, BL], F32)
    adtb = p.sbuf("adtb", [128, 8], BF16)
    dtbt = p.sbuf("dtbt", [8, 1], F32)
    alogt = p.sbuf("alogt", [8, 1], F32)
    p.memset(zrow[:], 0.0, tag="zrow")
    p.ld(dtbt[:], dtb.rearrange("(a b) -> a b", b=1), tag="dtbld")
    p.ld(alogt[:], alog.rearrange("(a b) -> a b", b=1), tag="alogld")
    p.act(alogt[:], alogt[:], AF.Exp, tag="ea")
    p.ts(alogt[:], alogt[:], -1.0, ALU.mult, tag="nega")
    psT3 = p.psum("psT3", [128, 16], F32)
    psC3 = p.psum("psC3", [128, 16], F32)
    psR8 = p.psum("psR8", [8, 128], F32)
    for blk in range(NB):
        bsl = slice(blk * BL, (blk + 1) * BL)
        nt0 = blk * (BL // 128)
        p.ld(dtraw[:], dtr_d[:, bsl], tag=f"m3d{blk}")
        p.ts(dtraw[:], dtraw[:], dtbt[:], ALU.add, tag=f"m3a{blk}")
        p.act(tr1[:], dtraw[:], AF.Abs, tag=f"m3b{blk}")
        p.act(tr1[:], tr1[:], AF.Exp, scale=-1.0, tag=f"m3c{blk}")
        p.act(tr1[:], tr1[:], AF.Ln, bias=1.0, tag=f"m3e{blk}")
        p.ts(dt_r[:], dtraw[:], 0.0, ALU.max, tag=f"m3f{blk}")
        p.tt(dt_r[:], dt_r[:], tr1[:], ALU.add, tag=f"m3g{blk}")
        p.act(lndt_r[:], dt_r[:], AF.Ln, tag=f"m3h{blk}")
        p.ts(adt_r[:], dt_r[:], alogt[:], ALU.mult, tag=f"m3i{blk}")
        p.ld(rtp[0:8, :], adt_r[:], engine="pool", tag=f"m3p1{blk}")
        p.ld(rtp[8:16, :], lndt_r[:], engine="pool", tag=f"m3p2{blk}")
        for tt_ in range(BL // 128):
            t = nt0 + tt_
            lsl = slice(tt_ * 128, (tt_ + 1) * 128)
            tc = slice(t * 8, (t + 1) * 8)
            p.transp(psT3[:], rtp[:, lsl], identT[0:16, 0:16],
                     tag=f"m3t{blk}_{tt_}")
            p.copy(glt[:, tc], psT3[:, 8:16], tag=f"m3lt{blk}_{tt_}")
            p.copy(adtb[:], psT3[:, 0:8], tag=f"m3ab{blk}_{tt_}")
            p.mm(psC3[:, 0:8], bdU[:], adtb[:], tag=f"m3c1_{blk}_{tt_}")
            p.mm(psC3[:, 8:16], bdA[:], adtb[:], tag=f"m3c2_{blk}_{tt_}")
            p.copy(egc[:, tc], psC3[:, 0:8], tag=f"m3gc{blk}_{tt_}")
            p.raw("dve", lambda e, o_=glt[:, tc], a=glt[:, tc], b=egc[:, tc]:
                  e.tensor_tensor(out=o_, in0=a, in1=b, op=ALU.subtract),
                  [glt[:, tc], egc[:, tc]], [glt[:, tc]], tag=f"m3gl{blk}_{tt_}")
            p.raw("dve", lambda e, o_=wco[:, tc], a=psC3[:, 8:16], b=glt[:, tc]:
                  e.tensor_tensor(out=o_, in0=a, in1=b, op=ALU.add),
                  [psC3[:], glt[:, tc]], [wco[:, tc]], tag=f"m3wc{blk}_{tt_}")
            p.mm(psR8[:], adtb[:], bdU[:], tag=f"m3gt{blk}_{tt_}")
            p.copy(gcbm[:, lsl], psR8[:], tag=f"m3ge{blk}_{tt_}")
            p.mm(psR8[:], adtb[:], bdA[:], tag=f"m3cs{blk}_{tt_}")
            p.copy(esc[:, t * 2:t * 2 + 2],
                   psR8[:].rearrange("p (a c) -> p a c", a=2)[:, :, 63],
                   tag=f"m3ce{blk}_{tt_}")
        p.ld(rows_dm[0:8, bsl], gcbm[:], engine="sp", tag=f"m3r0{blk}")
        p.ld(rows_dm[8:16, bsl], zrow[:], engine="sp", tag=f"m3r1{blk}")
    p.act(egc[:], egc[:], AF.Exp, tag="egce")
    p.act(wco[:], wco[:], AF.Exp, tag="wcoe")
    escm = p.sbuf("escm", [8, nC * 8], F32)
    p.raw("dve",
          lambda e: e.tensor_tensor(
              out=escm[:].rearrange("p (c h) -> p c h", h=8),
              in0=esc[:].broadcast_to([8, nC, 8]).transpose([0, 1, 2])
              if False else esc[:].rearrange("p c -> p c").broadcast_to([8, nC, 8]),
              in1=ind8s[:].rearrange("p h -> p h").broadcast_to([8, 8, nC])
              .transpose([0, 2, 1]),
              op=ALU.mult),
          [esc[:], ind8s[:]], [escm[:]], tag="escm")
    for i in range((nC * 8) // 512):
        p.mm(psE[0][0:64, :] if False else psC3[0:64, 0:16], None, None,
             tag="") if False else None
    psG4 = p.psum("psG4", [64, 512], F32)
    for i in range(max(1, (nC * 8) // 512)):
        w_ = min(512, nC * 8 - i * 512)
        p.mm(psG4[:, 0:w_], ones8c[:], escm[:, i * 512:i * 512 + w_],
             tag=f"gm{i}")
        p.act(gaml[:, i * 512:i * 512 + w_], psG4[:, 0:w_], AF.Exp,
              tag=f"gme{i}")
    p.pop_scope()

    p.push_scope()
    xall = p.sbuf("xall", [128, 640], BF16)
    ztm = p.sbuf("ztm", [128, 512], BF16)
    bfm = p.sbuf("bfm", [64, 128], BF16)
    cfm = p.sbuf("cfm", [64, 128], BF16)
    rstg = [p.sbuf(f"rstg{i}", [16, 1024], F32) for i in range(2)]
    expw = p.sbuf("expw", [128, 128], BF16)
    Mt = p.sbuf("Mt", [128, 128], BF16)
    btl = p.sbuf("btl", [128, 512], BF16)
    NST = p.sbuf("NST", [64, 512], BF16)
    ytmp = p.sbuf("ytmp", [128, 512], F32)
    yasm = p.sbuf("yasm", [128, 512], BF16)
    Drep = p.sbuf("Drep", [128, 512], F32)
    d512 = p.sbuf("d512", [1, 512], F32)
    sz = p.sbuf("sz", [128, 512], F32)
    xd = p.sbuf("xd", [128, 512], F32)
    ub = p.sbuf("ub", [128, 512], BF16)
    psCB = p.psum("psCB", [128, 128], F32)
    psP = p.psum("psP", [128, 128], F32)
    psYd = p.psum("psYd", [128, 512], F32)
    psYo = p.psum("psYo", [128, 512], F32)
    psS = [p.psum(f"psS{i}", [64, 64], F32) for i in range(2)]
    psD = p.psum("psD", [128, 512], F32)
    drow = p.sbuf("drow", [1, 8], F32)
    p.ld(drow[:], dpar[:], tag="dld")
    p.raw("dve",
          lambda e: e.tensor_copy(
              out=d512[:].rearrange("a (h c) -> a h c", c=64),
              in_=drow[:].rearrange("a h -> a h").broadcast_to([1, 8, 64])),
          [drow[:]], [d512[:]], tag="d512")
    p.mm(psD[:], onesr[:], d512[:], tag="db1")
    p.copy(Drep[:], psD[:], tag="db2")
    p.memset(NST[:], 0.0, tag="nst0")
    RG = min(8, nT)
    for t in range(nT):
        tsl = slice(t * 128, (t + 1) * 128)
        if t % RG == 0:
            rstgb = rstg[(t // RG) % 2]
            p.ld(rstgb[:, 0:RG * 128], rows_dm[:, t * 128:(t + RG) * 128],
                 engine="pool", tag=f"m4r{t}")
        rsl = rstg[(t // RG) % 2][0:8, (t % RG) * 128:(t % RG + 1) * 128]
        p.ldT(xall[:], xc_d[:, tsl], engine="act", tag=f"m4x{t}")
        p.ldT(ztm[:], zT_d[:, tsl], engine="act", tag=f"m4z{t}")
        p.ld(bfm[:], xc_d[512:576, tsl], tag=f"m4b{t}")
        p.ld(cfm[:], xc_d[576:640, tsl], tag=f"m4c{t}")
        p.mm(psCB[:], bfm[:], cfm[:], tag=f"m4cb{t}")
        p.raw("dve",
              lambda e, o_=btl[:].rearrange("p (h n) -> p h n", h=8),
              a=xall[:, 512:576].rearrange("p n -> p n")
              .broadcast_to([128, 64, 8]).transpose([0, 2, 1]),
              b=wco[:, t * 8:(t + 1) * 8].rearrange("p h -> p h")
              .broadcast_to([128, 8, 64]):
              e.tensor_tensor(out=o_, in0=a, in1=b, op=ALU.mult),
              [xall[:], wco[:]], [btl[:]], tag=f"m4bt{t}")
        for h in range(8):
            hsl = slice(64 * h, 64 * h + 64)
            p.mm(psP[:], ind8[:, h * 128:(h + 1) * 128], rsl,
                 start=True, stop=False, tag=f"m4p{t}_{h}")
            p.mm(psP[:], identb[:], PNi[:], start=False, stop=True,
                 tag=f"m4p2{t}_{h}")
            p.act(expw[:], psP[:], AF.Exp,
                  bias=glt[:, t * 8 + h:t * 8 + h + 1], tag=f"m4e{t}_{h}")
            p.tt(Mt[:], expw[:], psCB[:], ALU.mult, tag=f"m4M{t}_{h}")
            p.mm(psYd[:, hsl], Mt[:], xall[:, hsl], start=True, stop=True,
                 tag=f"m4yd{t}_{h}")
            for half in range(2):
                c = 2 * t + half
                hs = slice(64 * half, 64 * half + 64)
                jS = psS[half % 2]
                p.mm(psYo[hs, hsl], cfm[:, hs], NST[:, hsl],
                     start=True, stop=True, tag=f"m4yo{c}_{h}")
                p.mm(jS[:], btl[hs, hsl], xall[hs, hsl], tag=f"m4st{c}_{h}")
                p.raw("dve",
                      lambda e, o_=NST[:, hsl], g_=gaml[:, c * 8 + h:c * 8 + h + 1],
                      ps=jS[:]:
                      e.scalar_tensor_tensor(out=o_, in0=o_, scalar=g_, in1=ps,
                                             op0=ALU.mult, op1=ALU.add),
                      [NST[:], gaml[:], jS[:]], [NST[:]], tag=f"m4ns{c}_{h}")
        p.raw("dve",
              lambda e, o_=ytmp[:].rearrange("p (h n) -> p h n", h=8),
              a=psYo[:].rearrange("p (h n) -> p h n", h=8),
              b=egc[:, t * 8:(t + 1) * 8].rearrange("p h -> p h")
              .broadcast_to([128, 8, 64]):
              e.tensor_tensor(out=o_, in0=a, in1=b, op=ALU.mult),
              [psYo[:], egc[:]], [ytmp[:]], tag=f"m4ev{t}")
        p.raw("dve", lambda e, o_=yasm[:], a=ytmp[:], b=psYd[:]:
              e.tensor_tensor(out=o_, in0=a, in1=b, op=ALU.add),
              [ytmp[:], psYd[:]], [yasm[:]], tag=f"m4ev2{t}")
        p.act(sz[:], ztm[:], AF.Silu, tag=f"m4si{t}")
        p.tt(xd[:], xall[:, 0:512], Drep[:], ALU.mult, engine="pool",
             tag=f"m4xd{t}")
        p.raw("dve", lambda e, o_=xd[:], a=yasm[:], b=xd[:]:
              e.tensor_tensor(out=o_, in0=a, in1=b, op=ALU.add),
              [yasm[:], xd[:]], [xd[:]], tag=f"m4ya{t}")
        p.tt(ub[:], xd[:], sz[:], ALU.mult, tag=f"m4u{t}")
        p.ld(u[tsl, :], ub[:], engine="sp", tag=f"m4us{t}")
    p.pop_scope()
    p.build()
    return nc




_PROGS = {}


def _get(key, fn, *args):
    if key not in _PROGS:
        _PROGS[key] = fn(*args)
    return _PROGS[key]


def _runp(nc, maps, outname):
    res = run_bass_kernel_spmd(nc, maps, core_ids=list(range(8)))
    return [np.asarray(r[outname]) for r in res.results]


def kernel(**inp):
    S, D, R = 8192, 2048, 1024
    x = np.ascontiguousarray(np.asarray(inp["L0_attn_a_hidden_states"])[0])

    def rows(a):
        return [np.ascontiguousarray(a[i * R:(i + 1) * R]) for i in range(8)]

    def run_ln(xc):
        nl = _get("ln", build_ln, R, D)
        lns = _runp(nl, [{"x": xr} for xr in rows(xc)], "ln")
        return np.ascontiguousarray(np.concatenate(lns, 0))

    def run_oproj(xc, o_full, wT, norm, g=None, b=None):
        C = o_full.shape[1]
        key = f"oproj{C}_{norm}"
        no = _get(key, build_oproj, R, C, D, norm)
        wTc = np.ascontiguousarray(wT)
        maps = []
        for i, xr in enumerate(rows(xc)):
            m = {"o": np.ascontiguousarray(o_full[i * R:(i + 1) * R]),
                 "wT": wTc, "x": xr}
            if norm in ("lnwb", "rmsw"):
                m["g"] = np.ascontiguousarray(g)
            if norm == "lnwb":
                m["b"] = np.ascontiguousarray(b)
            maps.append(m)
        xs = _runp(no, maps, "xo")
        return np.concatenate(xs, 0)

    ln_full = run_ln(x)
    na = _get("l0a", build_l0a, S, D)
    selm = make_selm()
    maps = []
    for c in range(8):
        maps.append({
            "ln": ln_full,
            "wproj": pack_wproj(inp["L0_attn_a_W_qkv"], inp["L0_attn_a_W_z"],
                                inp["L0_attn_a_W_b"], inp["L0_attn_a_W_a"], c),
            "convw": pack_convw(inp["L0_attn_a_conv_w"], c),
            "dtb": np.ascontiguousarray(inp["L0_attn_a_dt_bias"][4 * c:4 * c + 4]),
            "alog": np.ascontiguousarray(inp["L0_attn_a_A_log"][4 * c:4 * c + 4]),
            "normw": np.ascontiguousarray(inp["L0_attn_a_norm_weight"]),
            "selm": selm,
        })
    os_ = _runp(na, maps, "o")
    o_full = np.ascontiguousarray(np.concatenate(os_, 1))
    x = run_oproj(x, o_full, inp["L0_attn_a_W_out"].T, "none")

    def run_l0f(xc):
        lnf = run_ln(xc)
        nf = _get("l0f", build_l0ffn, R, D, 2048, 512)
        idx = np.arange(S)
        maps = []
        masks = []
        for c in range(8):
            e, half = c // 2, c % 2
            m = (idx % 4 == e) & ((idx // 4096) == half)
            masks.append(m)
            maps.append({
                "ln": np.ascontiguousarray(lnf[m]),
                "x": np.ascontiguousarray(xc[m]),
                "gup": np.ascontiguousarray(inp["L0_ffn_gate_up_proj"][e]),
                "down": np.ascontiguousarray(inp["L0_ffn_down_proj"][e]),
            })
        xs = _runp(nf, maps, "xo")
        out = np.empty_like(xc)
        for c in range(8):
            out[masks[c]] = xs[c]
        return out

    x = run_l0f(x)

    ln_full = run_ln(x)
    nt = _get("l0b", build_l0b, S, D)
    tib = (1.0 / np.arange(1, 17, dtype=np.float32)).reshape(1, 16)
    maps = []
    for c in range(8):
        maps.append({
            "ln": ln_full,
            "wproj": pack_wproj_ttt(inp["L0_attn_b_q_w"], inp["L0_attn_b_k_w"],
                                    inp["L0_attn_b_v_w"],
                                    inp["L0_attn_b_ttt_lr_weight"], c),
            "w1in": np.ascontiguousarray(
                inp["L0_attn_b_W1"][4 * c:4 * c + 4].reshape(256, 64)),
            "b1in": np.ascontiguousarray(
                inp["L0_attn_b_b1"][4 * c:4 * c + 4].reshape(1, 256)),
            "tnw": np.ascontiguousarray(
                inp["L0_attn_b_ttt_norm_weight"][4 * c:4 * c + 4].reshape(1, 256)),
            "tnb": np.ascontiguousarray(
                inp["L0_attn_b_ttt_norm_bias"][4 * c:4 * c + 4].reshape(1, 256)),
            "tib": tib,
            "tilearn": np.ascontiguousarray(
                inp["L0_attn_b_learnable_token_idx"].reshape(1, 16)),
            "lrb": np.ascontiguousarray(
                inp["L0_attn_b_ttt_lr_bias"][4 * c:4 * c + 4, 0]),
        })
    os_ = _runp(nt, maps, "o")
    o_full = np.ascontiguousarray(np.concatenate(os_, 1))
    x = run_oproj(x, o_full, inp["L0_attn_b_o_w"].T, "lnwb",
                  g=inp["L0_attn_b_post_norm_weight"],
                  b=inp["L0_attn_b_post_norm_bias"])

    x = run_l0f(x)

    nlat = _get("mlalat", build_mla_lat, R, D)
    wk_rope = inp["L1_attn_a_wk_rope"]
    wk_rope_perm = np.ascontiguousarray(
        np.concatenate([wk_rope[:, 32:64], wk_rope[:, 0:32]], 1))
    wlat = np.ascontiguousarray(np.concatenate(
        [inp["L1_attn_a_wq_down"], inp["L1_attn_a_wkv_down"], wk_rope,
         wk_rope_perm], 1))
    lats = _runp(nlat, [{"x": xr, "wlat": wlat} for xr in rows(x)], "latT")
    latT = np.ascontiguousarray(np.concatenate(lats, 1))
    nm = _get("mla", build_mla, S, 512)
    cosT, ssinT = rope_tables(S)
    wq_up, wq_rope = inp["L1_attn_a_wq_up"], inp["L1_attn_a_wq_rope"]
    wk_up, wv_up = inp["L1_attn_a_wk_up"], inp["L1_attn_a_wv_up"]
    maps = []
    for c in range(8):
        wqr_c = np.ascontiguousarray(wq_rope[:, 128 * c:128 * c + 128])
        wqrp_c = np.ascontiguousarray(np.concatenate(
            [np.concatenate([wqr_c[:, 64 * h + 32:64 * h + 64],
                             wqr_c[:, 64 * h:64 * h + 32]], 1)
             for h in range(2)], 1))
        maps.append({
            "latT": latT,
            "wqu": np.ascontiguousarray(wq_up[:, 256 * c:256 * c + 256]),
            "wqr": wqr_c,
            "wqrp": wqrp_c,
            "wku": np.ascontiguousarray(wk_up[:, 256 * c:256 * c + 256]),
            "wvu": np.ascontiguousarray(wv_up[:, 256 * c:256 * c + 256]),
            "cosT": cosT, "ssinT": ssinT,
        })
    os_ = _runp(nm, maps, "oT")
    o_full = np.ascontiguousarray(np.concatenate(os_, 0).T)
    x = run_oproj(x, o_full, inp["L1_attn_a_wo"], "none")

    def run_l1f(xc):
        nf = _get("l1f", build_l1ffn, R, D, 512)
        w = np.ascontiguousarray(inp["L1_ffn_weight"])
        sc = np.ascontiguousarray(inp["L1_ffn_scale"])
        xs = _runp(nf, [{"x": xr, "w": w, "scale": sc} for xr in rows(xc)],
                   "xo")
        return np.concatenate(xs, 0)

    x = run_l1f(x)

    ln_full = run_ln(x)
    nb = _get("l1b", build_l1b, S, D)
    maps = []
    for c in range(8):
        cw, cb = pack_convw_l1b(inp["L1_attn_b_conv_w"],
                                inp["L1_attn_b_conv_b"], c)
        maps.append({
            "ln": ln_full,
            "win": pack_win(inp["L1_attn_b_W_in"], c),
            "convw": cw, "convb": cb,
            "dtb": np.ascontiguousarray(inp["L1_attn_b_dt_bias"][8 * c:8 * c + 8]),
            "alog": np.ascontiguousarray(inp["L1_attn_b_A_log"][8 * c:8 * c + 8]),
            "dpar": np.ascontiguousarray(
                inp["L1_attn_b_D_param"][8 * c:8 * c + 8].reshape(1, 8)),
        })
    os_ = _runp(nb, maps, "u")
    o_full = np.ascontiguousarray(np.concatenate(os_, 1))
    x = run_oproj(x, o_full, inp["L1_attn_b_W_out"].T, "rmsw",
                  g=inp["L1_attn_b_norm_w"])

    x = run_l1f(x)

    return np.ascontiguousarray(x.astype(np.float32)).reshape(1, S, D)


# version 03 (final)
# speedup vs baseline: 1.0014x; 1.0014x over previous
import numpy as np
import concourse.bass as bass
import concourse.mybir as mybir
from concourse.bass_utils import run_bass_kernel_spmd



F32 = mybir.dt.float32
BF16 = mybir.dt.bfloat16
I32 = mybir.dt.int32
AF = mybir.ActivationFunctionType
ALU = mybir.AluOpType
AX = mybir.AxisListType

ENGINES = ("pe", "act", "dve", "pool", "sp")
DMA_QUEUES = {"sp": "qsp", "act": "qact", "pool": "qpool"}
ENG_ATTR = {"pe": "tensor", "act": "scalar", "dve": "vector",
            "pool": "gpsimd", "sp": "sync"}


class Op:
    __slots__ = ("engine", "fn", "deps", "is_dma", "queue", "needed", "sem",
                 "value", "tag", "amt", "idx")

    def __init__(self, engine, fn, deps, is_dma, queue, tag, idx):
        self.engine = engine
        self.fn = fn
        self.deps = deps
        self.is_dma = is_dma
        self.queue = queue
        self.needed = False
        self.sem = None
        self.value = None
        self.tag = tag
        self.idx = idx


def _names(aps):
    out = []
    for ap in aps:
        if ap is None or isinstance(ap, (int, float)):
            continue
        out.append(ap.tensor.name)
    return out


class Prog:
    def __init__(self, nc):
        self.nc = nc
        self.ops = []
        self.last_writer = {}
        self.readers_since = {}
        self._consts = {}
        self._scopes = []
        self._barriers = []

    def const_tile(self, val, dtype=F32):
        key = (float(val), dtype)
        if key not in self._consts:
            g = self.nc.sbuf_tensor(f"const_{len(self._consts)}", [128, 1],
                                    dtype, side="right")
            t = g.__enter__()
            self.memset(t[:], float(val), tag=f"const={val}")
            self._consts[key] = t[:]
        return self._consts[key]

    def sbuf(self, name, shape, dtype, side=None):
        g = self.nc.sbuf_tensor(name, list(shape), dtype, side=side)
        t = g.__enter__()
        if self._scopes:
            self._scopes[-1][0].append((g, t.name))
        return t

    def psum(self, name, shape, dtype=F32):
        g = self.nc.psum_tensor(name, list(shape), dtype)
        t = g.__enter__()
        if self._scopes:
            self._scopes[-1][0].append((g, t.name))
        return t

    def fence(self, tag="fence"):
        deps = []
        seen = {}
        for op in reversed(self.ops):
            if op.engine not in seen:
                seen[op.engine] = op
        bar = self._rec("dve", lambda e: e.engine_nop(), [], [],
                        list(seen.values()), tag)
        self._barriers.append(bar)
        return bar

    def push_scope(self):
        self._scopes.append(([], len(self.ops)))

    def pop_scope(self):
        guards, start = self._scopes.pop()
        names = {n for _, n in guards}
        barrier_deps = []
        for n in names:
            barrier_deps.extend(self.last_writer.get(n, ()))
            barrier_deps.extend(self.readers_since.get(n, ()))
            self.last_writer.pop(n, None)
            self.readers_since.pop(n, None)
        for g, _ in reversed(guards):
            g.__exit__(None, None, None)
        if barrier_deps:
            bar = self._rec("dve", lambda e: e.engine_nop(), [], [],
                            barrier_deps, "scope_barrier")
            self._barriers.append(bar)

    def _rec(self, engine, fn, reads, writes, extra, tag, is_dma=False,
             pe_acc=False):
        op = Op(engine, fn, None, is_dma, DMA_QUEUES.get(engine) if is_dma else None,
                tag, len(self.ops))
        deps = set()
        for d in extra:
            if d is not None:
                deps.add(d)
        for b in self._barriers:
            deps.add(b)
        for n in _names(reads):
            for w in self.last_writer.get(n, ()):
                deps.add(w)
            self.readers_since.setdefault(n, []).append(op)
        for n in _names(writes):
            rs = self.readers_since.get(n)
            if rs:
                for r in rs:
                    deps.add(r)
                self.last_writer[n] = [op]
                self.readers_since[n] = []
            else:
                gen = self.last_writer.setdefault(n, [])
                if len(gen) > 200:
                    for w in gen:
                        deps.add(w)
                    self.last_writer[n] = [op]
                else:
                    if not is_dma:
                        for w in gen:
                            deps.add(w)
                    else:
                        for w in gen:
                            if not w.is_dma:
                                deps.add(w)
                    gen.append(op)
            if pe_acc:
                self.readers_since[n] = [op]
        deps.discard(op)
        op.deps = list(deps)
        self.ops.append(op)
        return op

    def ld(self, out, in_, extra=(), engine="sp", tag="ld", **kw):
        return self._rec(engine, lambda e: e.dma_start(out=out, in_=in_, **kw),
                         [in_], [out], extra, tag, is_dma=True)

    def ldT(self, out, in_, extra=(), engine="sp", tag="ldT"):
        return self._rec(engine, lambda e: e.dma_start_transpose(out=out, in_=in_),
                         [in_], [out], extra, tag, is_dma=True)

    def mm(self, out, lhsT, rhs, extra=(), start=True, stop=True,
           tile_position=None, tag="mm"):
        return self._rec(
            "pe",
            lambda e: e.matmul(out, lhsT, rhs, start=start, stop=stop,
                               tile_position=tile_position),
            [lhsT, rhs], [out], extra, tag, pe_acc=not stop)

    def transp(self, out, in_, identity, extra=(), tag="tr"):
        return self._rec("pe", lambda e: e.transpose(out, in_, identity),
                         [in_, identity], [out], extra, tag)

    def act(self, out, in_, func, extra=(), bias=0.0, scale=1.0, accum_out=None,
            tag="act", engine="act"):
        if isinstance(bias, (int, float)) and float(bias) != 0.0:
            np_ = in_.partition_size() if hasattr(in_, "partition_size") else 128
            bias = self.const_tile(bias)[0:np_, :]
        return self._rec(
            engine,
            lambda e: e.activation(out=out, in_=in_, func=func, bias=bias,
                                   scale=scale, accum_out=accum_out),
            [in_, bias if not isinstance(bias, float) else None,
             scale if not isinstance(scale, float) else None],
            [out] + ([accum_out] if accum_out is not None else []),
            extra, tag)

    def tt(self, out, in0, in1, op, extra=(), engine="dve", tag="tt"):
        return self._rec(engine,
                         lambda e: e.tensor_tensor(out=out, in0=in0, in1=in1, op=op),
                         [in0, in1], [out], extra, tag)

    def ts(self, out, in0, s1, op0, s2=None, op1=None, extra=(), accum_out=None,
           engine="dve", tag="ts"):
        def fn(e):
            if op1 is None:
                return e.tensor_scalar(out=out, in0=in0, scalar1=s1, scalar2=None,
                                       op0=op0, accum_out=accum_out)
            return e.tensor_scalar(out=out, in0=in0, scalar1=s1, scalar2=s2,
                                   op0=op0, op1=op1, accum_out=accum_out)
        return self._rec(engine, fn,
                         [in0, s1 if not isinstance(s1, float) else None,
                          s2 if not isinstance(s2, float) else None],
                         [out] + ([accum_out] if accum_out is not None else []),
                         extra, tag)

    def stt(self, out, in0, scalar, in1, op0, op1, extra=(), engine="dve",
            tag="stt"):
        return self._rec(
            engine,
            lambda e: e.scalar_tensor_tensor(out=out, in0=in0, scalar=scalar,
                                             in1=in1, op0=op0, op1=op1),
            [in0, in1, scalar if not isinstance(scalar, float) else None],
            [out], extra, tag)

    def red(self, out, in_, op, extra=(), negate=False, engine="dve", tag="red"):
        return self._rec(
            engine,
            lambda e: e.tensor_reduce(out=out, in_=in_, axis=AX.X, op=op,
                                      negate=negate),
            [in_], [out], extra, tag)

    def ttr(self, out, in0, in1, op0, op1, accum_out, scale=1.0, scalar=0.0,
            extra=(), tag="ttr"):
        return self._rec(
            "dve",
            lambda e: e.tensor_tensor_reduce(out=out, in0=in0, in1=in1,
                                             scale=scale, scalar=scalar,
                                             op0=op0, op1=op1,
                                             accum_out=accum_out),
            [in0, in1, scalar if not isinstance(scalar, float) else None],
            [out, accum_out], extra, tag)

    def recip(self, out, in_, extra=(), tag="recip"):
        return self._rec("dve", lambda e: e.reciprocal(out=out, in_=in_),
                         [in_], [out], extra, tag)

    def copy(self, out, in_, extra=(), engine="dve", tag="copy"):
        if engine == "act":
            return self._rec(engine, lambda e: e.copy(out=out, in_=in_),
                             [in_], [out], extra, tag)
        return self._rec(engine, lambda e: e.tensor_copy(out=out, in_=in_),
                         [in_], [out], extra, tag)

    def memset(self, ap, val, extra=(), engine="pool", tag="memset"):
        return self._rec(engine, lambda e: e.memset(ap, val), [], [ap], extra, tag)

    def iota(self, out, pattern, base=0, channel_multiplier=0, extra=(),
             tag="iota"):
        return self._rec("pool",
                         lambda e: e.iota(out, pattern, base=base,
                                          channel_multiplier=channel_multiplier),
                         [], [out], extra, tag)

    def affine_select(self, out, in_, pattern, compare_op, fill, base=0,
                      channel_multiplier=0, extra=(), tag="affsel"):
        return self._rec(
            "pool",
            lambda e: e.affine_select(out=out, in_=in_, pattern=pattern,
                                      compare_op=compare_op, fill=fill, base=base,
                                      channel_multiplier=channel_multiplier),
            [in_], [out], extra, tag)

    def ln_bwd_dx(self, out, dy, x_hat, mean_dyx, mean_dy, scale=1.0, extra=(),
                  tag="lnbwd"):
        return self._rec(
            "dve",
            lambda e: e.ln_bwd_dx(out=out, dy=dy, x_hat=x_hat, mean_dyx=mean_dyx,
                                  mean_dy=mean_dy, scale=scale),
            [dy, x_hat, mean_dyx, mean_dy], [out], extra, tag)

    def raw(self, engine, fn, reads, writes, extra=(), tag="raw", is_dma=False):
        return self._rec(engine, fn, reads, writes, extra, tag, is_dma=is_dma)

    def collective(self, kind, op, ins, outs, replica_groups, extra=(), tag="cc"):
        def fn(e):
            return e.collective_compute(kind, op, ins=ins, outs=outs,
                                        replica_groups=replica_groups)
        return self._rec("pool", fn, ins, outs, extra, tag, is_dma=True)

    def build(self, dma_slots=(("qsp", 48), ("qact", 24), ("qpool", 24))):
        nc = self.nc
        for op in self.ops:
            for d in op.deps:
                if (d.engine == "pe" and op.engine == "pe" and not d.is_dma):
                    continue
                d.needed = True
        dma_slots = dict(dma_slots)
        counters = {n: 0 for n in ENGINES}
        qcount = {q: 0 for q in DMA_QUEUES.values()}
        used = set()
        for op in self.ops:
            if op.is_dma:
                K = dma_slots[op.queue]
                i = qcount[op.queue]
                qcount[op.queue] += 1
                slot = i % K
                cycle = i // K
                op.sem = f"{op.queue}#{slot}"
                op.value = 16 * (cycle + 1)
                op.amt = 16
                used.add(op.sem)
            elif op.needed:
                op.sem = op.engine
                counters[op.engine] += 1
                op.value = counters[op.engine]
                op.amt = 1
                used.add(op.sem)
        sems = {s: nc.semaphore(f"sem_{s}").__enter__() for s in sorted(used)}
        eng_ops = {e: [] for e in ENGINES}
        for op in self.ops:
            eng_ops[op.engine].append(op)
        final_dma = {}
        for op in self.ops:
            if op.is_dma:
                final_dma[op.sem] = max(final_dma.get(op.sem, 0), op.value)

        with nc.Block() as block:
            for ename in ENGINES:
                ops = eng_ops[ename]
                if not ops and ename != "sp":
                    continue

                def make_body(ops=ops, ename=ename):
                    def body(eng):
                        waited = {}
                        for op in ops:
                            for d in sorted(op.deps, key=lambda o: o.idx):
                                if (d.engine == "pe" and ename == "pe"
                                        and not d.is_dma):
                                    continue
                                if d.sem is None:
                                    raise RuntimeError(
                                        f"dep of {op.tag} on un-inc'd {d.tag}")
                                if waited.get(d.sem, 0) < d.value:
                                    eng.wait_ge(sems[d.sem], d.value)
                                    waited[d.sem] = d.value
                            if op.is_dma and op.value > 16:
                                pv = op.value - 16
                                if waited.get(op.sem, 0) < pv:
                                    eng.wait_ge(sems[op.sem], pv)
                                    waited[op.sem] = pv
                            inst = op.fn(eng)
                            if op.sem is not None:
                                inst.then_inc(sems[op.sem], op.amt)
                        if ename == "sp":
                            for s, v in sorted(final_dma.items()):
                                if waited.get(s, 0) < v:
                                    eng.wait_ge(sems[s], v)
                    return body

                getattr(block, ENG_ATTR[ename])(make_body())
        return nc

    def stats(self):
        from collections import Counter
        c = Counter(op.engine for op in self.ops)
        return dict(c)



LN_EPS = 1e-5
FFN_EPS = 1e-7


def emit_ln(p, xt, ln_out, scratch, sc, D, tag="ln"):
    mu, ss, var, rstd = sc[:, 0:1], sc[:, 1:2], sc[:, 2:3], sc[:, 3:4]
    p.red(mu, xt, ALU.add, tag=f"{tag}:sum")
    p.act(scratch, xt, AF.Square, accum_out=ss, tag=f"{tag}:sumsq")
    p.ts(mu, mu, 1.0 / D, ALU.mult, tag=f"{tag}:mu")
    p.tt(var, mu, mu, ALU.mult, tag=f"{tag}:mu2")
    p.stt(var, ss, 1.0 / D, var, ALU.mult, ALU.subtract, tag=f"{tag}:var")
    p.act(rstd, var, AF.Sqrt, bias=LN_EPS, tag=f"{tag}:std")
    p.recip(rstd, rstd, tag=f"{tag}:rstd")
    return p.ts(ln_out, xt, mu, ALU.subtract, rstd, ALU.mult, tag=f"{tag}:norm")


def build_l1ffn(R=1024, D=2048, TOKT=512):
    nc = bass.Bass("TRN2", target_bir_lowering=False)
    x = nc.dram_tensor("x", [R, D], F32, kind="ExternalInput").ap()
    w = nc.dram_tensor("w", [D, D], F32, kind="ExternalInput").ap()
    scale = nc.dram_tensor("scale", [D], F32, kind="ExternalInput").ap()
    xo = nc.dram_tensor("xo", [R, D], F32, kind="ExternalOutput").ap()
    ln_d = nc.dram_tensor("ln_d", [R, D], BF16, kind="Internal").ap()
    otp_d = nc.dram_tensor("otp_d", [D, R], BF16, kind="Internal").ap()

    p = Prog(nc)
    nK = D // 128
    nM = D // 128
    nT = R // 128
    nN = R // TOKT

    xt = [p.sbuf(f"xt{i}", [128, D], F32) for i in range(nT)]
    lnt = p.sbuf("lnt", [128, D], BF16)
    trash = p.sbuf("trash", [128, D], BF16)
    sc_t = [p.sbuf(f"sc{i}", [128, 8], F32) for i in range(nT)]
    wt = [p.sbuf(f"wt{i}", [128, D], BF16) for i in range(nK)]
    fmt = [p.sbuf(f"fmt{i}", [128, R], BF16) for i in range(nK)]
    sct = p.sbuf("sct", [128, nM], F32)
    ot = [p.sbuf(f"ot{i}", [128, TOKT], BF16) for i in range(2)]
    pt = p.sbuf("pt", [128, D], F32)
    ps = [p.psum(f"ps{i}", [128, TOKT], F32) for i in range(2)]

    for i in range(nK):
        p.ld(wt[i][:], w[i * 128:(i + 1) * 128, :], engine="pool", tag=f"wld{i}")
    p.ld(sct[:], scale.rearrange("(a b) -> b a", b=128), engine="pool",
         tag="scld", allow_slow_non_contiguous=True)

    for t in range(nT):
        p.ld(xt[t][:], x[t * 128:(t + 1) * 128, :], tag=f"xld{t}")
        emit_ln(p, xt[t][:], lnt[:], trash[:], sc_t[t][:], D, tag=f"ln{t}")
        p.ld(ln_d[t * 128:(t + 1) * 128, :], lnt[:], engine="sp", tag=f"lnst{t}")

    for i in range(nK):
        p.ldT(fmt[i][:], ln_d[:, i * 128:(i + 1) * 128], engine="act",
              tag=f"fmld{i}")

    for m in range(nM):
        for n in range(nN):
            j = (m * nN + n) % 2
            for k in range(nK):
                p.mm(ps[j][:], wt[k][:, m * 128:(m + 1) * 128],
                     fmt[k][:, n * TOKT:(n + 1) * TOKT],
                     start=(k == 0), stop=(k == nK - 1), tag=f"mm{m}_{n}_{k}")
            p.ts(ot[j][:], ps[j][:], sct[:, m:m + 1], ALU.mult, tag=f"ev{m}_{n}")
            p.ld(otp_d[m * 128:(m + 1) * 128, n * TOKT:(n + 1) * TOKT],
                 ot[j][:], engine="sp", tag=f"ost{m}_{n}")

    for t in range(nT):
        s = sc_t[t]
        p.ldT(trash[:], otp_d[:, t * 128:(t + 1) * 128], engine="act",
              tag=f"pld{t}")
        o_bf = trash[:]
        ssq, vn, f1, tmp = s[:, 4:5], s[:, 5:6], s[:, 6:7], s[:, 7:8]
        p.act(pt[:], o_bf, AF.Square, accum_out=ssq, tag=f"po{t}:sq")
        p.act(vn, ssq, AF.Sqrt, tag=f"po{t}:norm")
        p.ts(vn, vn, FFN_EPS, ALU.max, tag=f"po{t}:vn")
        p.act(f1, vn, AF.Tanh, tag=f"po{t}:tanh")
        p.ts(tmp, vn, FFN_EPS, ALU.add, tag=f"po{t}:vne")
        p.recip(tmp, tmp, tag=f"po{t}:rec")
        p.tt(f1, f1, tmp, ALU.mult, tag=f"po{t}:f1")
        p.tt(tmp, f1, vn, ALU.mult, tag=f"po{t}:on")
        p.ts(tmp, tmp, FFN_EPS, ALU.max, tag=f"po{t}:onm")
        p.recip(tmp, tmp, tag=f"po{t}:rec2")
        p.ts(tmp, tmp, 0.99, ALU.mult, 1.0, ALU.min, tag=f"po{t}:f2")
        p.tt(f1, f1, tmp, ALU.mult, tag=f"po{t}:f")
        p.ts(pt[:], o_bf, f1, ALU.mult, tag=f"po{t}:scale")
        p.tt(pt[:], pt[:], xt[t][:], ALU.add, tag=f"po{t}:res")
        p.ld(xo[t * 128:(t + 1) * 128, :], pt[:], engine="sp", tag=f"xost{t}")

    p.build()
    return nc


def ref_l1ffn(x, w, scale):
    mu = x.mean(-1, keepdims=True)
    var = x.var(-1, keepdims=True)
    h = (x - mu) / np.sqrt(var + LN_EPS)
    out = h @ w * scale
    vn = np.maximum(np.linalg.norm(out, axis=-1, keepdims=True), FFN_EPS)
    out = out * (np.tanh(vn) / (vn + FFN_EPS))
    on = np.maximum(np.linalg.norm(out, axis=-1, keepdims=True), FFN_EPS)
    out = out * np.minimum(0.99 / on, 1.0)
    return x + out






def build_ln(R=1024, D=2048):
    nc = bass.Bass("TRN2", target_bir_lowering=False)
    x = nc.dram_tensor("x", [R, D], F32, kind="ExternalInput").ap()
    ln = nc.dram_tensor("ln", [R, D], BF16, kind="ExternalOutput").ap()
    p = Prog(nc)
    nT = R // 128
    xt = [p.sbuf(f"xt{i}", [128, D], F32) for i in range(2)]
    lnt = [p.sbuf(f"lnt{i}", [128, D], BF16) for i in range(2)]
    trash = p.sbuf("trash", [128, D], BF16)
    sc = p.sbuf("sc", [128, 8], F32)
    for t in range(nT):
        j = t % 2
        p.ld(xt[j][:], x[t * 128:(t + 1) * 128, :], tag=f"xld{t}")
        emit_ln(p, xt[j][:], lnt[j][:], trash[:], sc[:], D, tag=f"ln{t}")
        p.ld(ln[t * 128:(t + 1) * 128, :], lnt[j][:], engine="sp", tag=f"lnst{t}")
    p.build()
    return nc


def build_l0ffn(R=1024, D=2048, I=2048, TOKT=512):
    nc = bass.Bass("TRN2", target_bir_lowering=False)
    x = nc.dram_tensor("x", [R, D], F32, kind="ExternalInput").ap()
    gup = nc.dram_tensor("gup", [D, 2 * I], F32, kind="ExternalInput").ap()
    down = nc.dram_tensor("down", [I, D], F32, kind="ExternalInput").ap()
    xo = nc.dram_tensor("xo", [R, D], F32, kind="ExternalOutput").ap()
    otp_d = nc.dram_tensor("otp_d", [D, R], BF16, kind="Internal").ap()
    ln = nc.dram_tensor("ln_d", [R, D], BF16, kind="Internal").ap()

    p = Prog(nc)
    nK = D // 128
    nI = I // 128
    nT = R // 128
    nN = R // TOKT

    p.push_scope()
    lxt = [p.sbuf(f"lxt{i}", [128, D], F32) for i in range(2)]
    llt = [p.sbuf(f"llt{i}", [128, D], BF16) for i in range(2)]
    ltr = p.sbuf("ltr", [128, D], BF16)
    lsc = p.sbuf("lsc", [128, 8], F32)
    for t in range(nT):
        j = t % 2
        p.ld(lxt[j][:], x[t * 128:(t + 1) * 128, :], tag=f"lxld{t}")
        emit_ln(p, lxt[j][:], llt[j][:], ltr[:], lsc[:], D, tag=f"lln{t}")
        p.ld(ln[t * 128:(t + 1) * 128, :], llt[j][:], engine="sp",
             tag=f"llst{t}")
    p.pop_scope()

    fmt = [p.sbuf(f"fmt{i}", [128, R], BF16) for i in range(nK)]
    for i in range(nK):
        p.ldT(fmt[i][:], ln[:, i * 128:(i + 1) * 128], engine="act", tag=f"fmld{i}")

    gwp = [p.sbuf(f"gwp{b}", [128, nK * 256], BF16) for b in range(2)]
    ps = [p.psum(f"ps{i}", [128, TOKT], F32) for i in range(4)]
    sg = [p.sbuf(f"sg{i}", [128, TOKT], F32) for i in range(2)]
    inter = [p.sbuf(f"it{i}", [128, R], BF16) for i in range(nI)]

    for m in range(nI):
        gb = gwp[m % 2]
        for k in range(nK):
            p.ld(gb[:, k * 256:k * 256 + 128],
                 gup[k * 128:(k + 1) * 128, m * 128:(m + 1) * 128],
                 engine="pool", tag=f"gwl{m}_{k}")
            p.ld(gb[:, k * 256 + 128:k * 256 + 256],
                 gup[k * 128:(k + 1) * 128, I + m * 128:I + (m + 1) * 128],
                 engine="pool", tag=f"uwl{m}_{k}")
        for n in range(nN):
            jg = (2 * (m * nN + n)) % 4
            ju = jg + 1
            for k in range(nK):
                p.mm(ps[jg][:], gb[:, k * 256:k * 256 + 128],
                     fmt[k][:, n * TOKT:(n + 1) * TOKT],
                     start=(k == 0), stop=(k == nK - 1), tag=f"gmm{m}_{n}_{k}")
                p.mm(ps[ju][:], gb[:, k * 256 + 128:k * 256 + 256],
                     fmt[k][:, n * TOKT:(n + 1) * TOKT],
                     start=(k == 0), stop=(k == nK - 1), tag=f"umm{m}_{n}_{k}")
            p.act(sg[n % 2][:], ps[jg][:], AF.Silu, tag=f"silu{m}_{n}")
            p.tt(inter[m][:, n * TOKT:(n + 1) * TOKT], sg[n % 2][:],
                 ps[ju][:], ALU.mult, tag=f"gx{m}_{n}")

    dwp = [p.sbuf(f"dwp{b}", [128, nI * 128], BF16) for b in range(2)]
    ot = [p.sbuf(f"ot{i}", [128, TOKT], BF16) for i in range(2)]
    nM = D // 128
    for m in range(nM):
        db = dwp[m % 2]
        for k in range(nI):
            p.ld(db[:, k * 128:(k + 1) * 128],
                 down[k * 128:(k + 1) * 128, m * 128:(m + 1) * 128],
                 engine="pool", tag=f"dwl{m}_{k}")
        for n in range(nN):
            j = (m * nN + n) % 2
            for k in range(nI):
                p.mm(ps[j][:], db[:, k * 128:(k + 1) * 128],
                     inter[k][:, n * TOKT:(n + 1) * TOKT],
                     start=(k == 0), stop=(k == nI - 1), tag=f"dmm{m}_{n}_{k}")
            p.copy(ot[j][:], ps[j][:], tag=f"dev{m}_{n}")
            p.ld(otp_d[m * 128:(m + 1) * 128, n * TOKT:(n + 1) * TOKT],
                 ot[j][:], engine="sp", tag=f"ost{m}_{n}")

    xt = p.sbuf("xt", [128, D], F32)
    pt = p.sbuf("pt", [128, D], F32)
    ob = p.sbuf("ob", [128, D], BF16)
    for t in range(nT):
        p.ldT(ob[:], otp_d[:, t * 128:(t + 1) * 128], engine="act", tag=f"pld{t}")
        p.ld(xt[:], x[t * 128:(t + 1) * 128, :], tag=f"xld{t}")
        p.tt(pt[:], ob[:], xt[:], ALU.add, tag=f"res{t}")
        p.ld(xo[t * 128:(t + 1) * 128, :], pt[:], engine="sp", tag=f"xost{t}")

    p.build()
    return nc


def ref_l0ffn(ln, x, gup, down):
    gu = ln.astype(np.float32) @ gup
    I = gup.shape[1] // 2
    gate, up = gu[:, :I], gu[:, I:]
    inter = gate / (1 + np.exp(-gate)) * up
    return x + inter @ down





NH_CORE = 2
HD, RD = 128, 64
QR, KVR = 768, 512
LT = 1408


def rope_tables(S, dtype=np.float32):
    d = RD
    inv = 1.0 / 10000.0 ** (np.arange(0, d, 2, dtype=np.float64) / d)
    pos = np.arange(S, dtype=np.float64)
    fr = np.outer(pos, inv)
    cos = np.concatenate([np.cos(fr), np.cos(fr)], -1)
    sin = np.sin(fr)
    ssin = np.concatenate([-sin, sin], -1)
    return cos.T.astype(dtype).copy(), ssin.T.astype(dtype).copy()


def build_mla_lat(R=1024, D=2048):
    nc = bass.Bass("TRN2", target_bir_lowering=False)
    x = nc.dram_tensor("x", [R, D], F32, kind="ExternalInput").ap()
    wlat = nc.dram_tensor("wlat", [D, LT], F32, kind="ExternalInput").ap()
    latT = nc.dram_tensor("latT", [LT, R], BF16, kind="ExternalOutput").ap()
    ln_d = nc.dram_tensor("ln_d", [R, D], BF16, kind="Internal").ap()
    p = Prog(nc)
    nK, nT, TOKT = D // 128, R // 128, 512
    nN = R // TOKT
    nM = LT // 128
    xt = [p.sbuf(f"xt{i}", [128, D], F32) for i in range(2)]
    lnt = [p.sbuf(f"lnt{i}", [128, D], BF16) for i in range(2)]
    trash = p.sbuf("trash", [128, D], BF16)
    sc = p.sbuf("sc", [128, 8], F32)
    for t in range(nT):
        j = t % 2
        p.ld(xt[j][:], x[t * 128:(t + 1) * 128, :], tag=f"xld{t}")
        emit_ln(p, xt[j][:], lnt[j][:], trash[:], sc[:], D, tag=f"ln{t}")
        p.ld(ln_d[t * 128:(t + 1) * 128, :], lnt[j][:], engine="sp", tag=f"lnst{t}")
    fmt = [p.sbuf(f"fmt{i}", [128, R], BF16) for i in range(nK)]
    for i in range(nK):
        p.ldT(fmt[i][:], ln_d[:, i * 128:(i + 1) * 128], engine="act",
              tag=f"fmld{i}")
    wt = [p.sbuf(f"wt{i}", [128, LT], BF16) for i in range(nK)]
    for i in range(nK):
        p.ld(wt[i][:], wlat[i * 128:(i + 1) * 128, :], engine="pool",
             tag=f"wld{i}")
    ps = [p.psum(f"ps{i}", [128, TOKT], F32) for i in range(2)]
    ot = [p.sbuf(f"ot{i}", [128, TOKT], BF16) for i in range(2)]
    for m in range(nM):
        for n in range(nN):
            j = (m * nN + n) % 2
            for k in range(nK):
                p.mm(ps[j][:], wt[k][:, m * 128:(m + 1) * 128],
                     fmt[k][:, n * TOKT:(n + 1) * TOKT],
                     start=(k == 0), stop=(k == nK - 1), tag=f"mm{m}_{n}_{k}")
            p.copy(ot[j][:], ps[j][:], tag=f"ev{m}_{n}")
            p.ld(latT[m * 128:(m + 1) * 128, n * TOKT:(n + 1) * TOKT],
                 ot[j][:], engine="sp", tag=f"ost{m}_{n}")
    p.build()
    return nc


def build_mla(S=8192, QT=512):
    nc = bass.Bass("TRN2", target_bir_lowering=False)
    latT = nc.dram_tensor("latT", [LT, S], BF16, kind="ExternalInput").ap()
    wqu = nc.dram_tensor("wqu", [QR, NH_CORE * HD], F32, kind="ExternalInput").ap()
    wqr = nc.dram_tensor("wqr", [QR, NH_CORE * RD], F32, kind="ExternalInput").ap()
    wqrp = nc.dram_tensor("wqrp", [QR, NH_CORE * RD], F32, kind="ExternalInput").ap()
    wku = nc.dram_tensor("wku", [KVR, NH_CORE * HD], F32, kind="ExternalInput").ap()
    wvu = nc.dram_tensor("wvu", [KVR, NH_CORE * HD], F32, kind="ExternalInput").ap()
    cosT_d = nc.dram_tensor("cosT", [RD, S], F32, kind="ExternalInput").ap()
    ssinT_d = nc.dram_tensor("ssinT", [RD, S], F32, kind="ExternalInput").ap()
    oT = nc.dram_tensor("oT", [NH_CORE * HD, S], BF16, kind="ExternalOutput").ap()
    vb_d = nc.dram_tensor("vb_d", [HD, S], BF16, kind="Internal").ap()
    qn_d = nc.dram_tensor("qn_d", [1, S], F32, kind="Internal").ap()

    p = Prog(nc)
    SCALE = 1.0 / float(np.sqrt(HD + RD))
    nS = S // 128
    nQ = S // QT
    kq = QT // 128
    nP = S // 512
    RC = min(1024, S)

    nQR, nKV = QR // 128, KVR // 128
    wqu_t = [p.sbuf(f"wqu{i}", [128, NH_CORE * HD], BF16) for i in range(nQR)]
    wqr_t = [p.sbuf(f"wqr{i}", [128, NH_CORE * RD], BF16) for i in range(nQR)]
    wqrp_t = [p.sbuf(f"wqrp{i}", [128, NH_CORE * RD], BF16) for i in range(nQR)]
    wku_t = [p.sbuf(f"wku{i}", [128, NH_CORE * HD], BF16) for i in range(nKV)]
    wvu_t = [p.sbuf(f"wvu{i}", [128, NH_CORE * HD], BF16) for i in range(nKV)]
    for i in range(nQR):
        p.ld(wqu_t[i][:], wqu[i * 128:(i + 1) * 128, :], engine="pool", tag=f"wq{i}")
        p.ld(wqr_t[i][:], wqr[i * 128:(i + 1) * 128, :], engine="pool", tag=f"wr{i}")
        p.ld(wqrp_t[i][:], wqrp[i * 128:(i + 1) * 128, :], engine="pool", tag=f"wp{i}")
    for i in range(nKV):
        p.ld(wku_t[i][:], wku[i * 128:(i + 1) * 128, :], engine="pool", tag=f"wk{i}")
        p.ld(wvu_t[i][:], wvu[i * 128:(i + 1) * 128, :], engine="pool", tag=f"wv{i}")

    masks = [p.sbuf(f"mask{d}", [128, QT], BF16) for d in range(kq)]
    for d in range(kq):
        p.memset(masks[d][:], 1.0, tag=f"mask{d}:set")
        p.affine_select(masks[d][:], masks[d][:], pattern=[[1, QT]],
                        compare_op=ALU.is_ge, fill=0.0, base=-128 * d,
                        channel_multiplier=-1, tag=f"mask{d}:sel")

    qc1 = p.sbuf("qc1", [128, S], BF16)
    qc2 = p.sbuf("qc2", [128, S], BF16)
    kc1 = p.sbuf("kc1", [128, S], BF16)
    kc2 = p.sbuf("kc2", [128, S], BF16)
    vtk = p.sbuf("vtk", [128, S], BF16)
    sq = p.sbuf("sq", [128, S], BF16)
    rpre = p.sbuf("rpre", [64, S], BF16)
    rrot = p.sbuf("rrot", [64, S], BF16)
    latp = [[p.sbuf(f"latp{b}_{i}", [128, 512], BF16) for i in range(nQR)]
            for b in range(2)]
    kvp = [[p.sbuf(f"kvp{b}_{i}", [128, 512], BF16) for i in range(nKV)]
           for b in range(2)]
    cosc = p.sbuf("cosc", [RD, RC], F32)
    ssinc = p.sbuf("ssinc", [RD, RC], F32)
    rfp = p.sbuf("rfp", [RD, RC], F32)
    rfp2 = p.sbuf("rfp2", [RD, RC], F32)
    rowbuf = p.sbuf("rowbuf", [1, 16], F32)
    qn = p.sbuf("qn", [1, 512], F32)
    qnb = p.sbuf("qnb", [1, 512], BF16)
    ones_k = p.sbuf("ones_k", [128, 1], BF16)
    ones_r = p.sbuf("ones_r", [1, 128], F32)
    pT = [p.sbuf(f"pT{i}", [128, QT], BF16) for i in range(2)]
    obuf = [p.sbuf(f"obuf{i}", [128, QT], BF16) for i in range(2)]
    rs = p.sbuf("rs", [1, QT], F32)
    rb = p.sbuf("rb", [128, QT], F32)

    p.memset(ones_k[:], 1.0, tag="ones_k")
    p.memset(ones_r[:], 1.0, tag="ones_r")

    psA = p.psum("psA", [128, 512], F32)
    psB = p.psum("psB", [128, 512], F32)
    psS = [p.psum(f"psS{i}", [128, QT], F32) for i in range(2)]
    psO = p.psum("psO", [128, QT], F32)
    psR = p.psum("psR", [1, QT], F32)
    psN = p.psum("psN", [1, 512], F32)

    def rope_combine(dst, pre, rot, tag):
        for c in range(S // RC):
            sl = slice(c * RC, (c + 1) * RC)
            p.ld(cosc[:], cosT_d[:, sl], tag=f"{tag}cos{c}")
            p.ld(ssinc[:], ssinT_d[:, sl], tag=f"{tag}sin{c}")
            p.tt(rfp[:], pre[:, sl], cosc[:], ALU.mult, tag=f"{tag}a{c}")
            p.tt(rfp2[:], rot[:, sl], ssinc[:], ALU.mult, tag=f"{tag}b{c}")
            p.tt(dst[0:64, sl], rfp[:], rfp2[:], ALU.add, tag=f"{tag}c{c}")

    p.ld(rpre[:], latT[1280:1344, :], tag="krpld")
    p.ld(rrot[:], latT[1344:1408, :], tag="krrld")
    rope_combine(kc2, rpre[:], rrot[:], "kR")
    p.memset(kc2[64:65, 0:S], 1.0, tag="kc2one")

    for h in range(NH_CORE):
        hHD = slice(h * HD, (h + 1) * HD)
        hRD = slice(h * RD, (h + 1) * RD)
        for n in range(nP):
            b = n % 2
            sl = slice(n * 512, (n + 1) * 512)
            for i in range(nKV):
                p.ld(kvp[b][i][:], latT[QR + i * 128:QR + (i + 1) * 128, sl],
                     tag=f"h{h}kvp{n}_{i}")
            for k in range(nKV):
                p.mm(psA[:], wku_t[k][:, hHD], kvp[b][k][:],
                     start=(k == 0), stop=(k == nKV - 1), tag=f"h{h}kc{n}_{k}")
            p.copy(kc1[:, sl], psA[:], tag=f"h{h}kcev{n}")
            for k in range(nKV):
                p.mm(psB[:], wvu_t[k][:, hHD], kvp[b][k][:],
                     start=(k == 0), stop=(k == nKV - 1), tag=f"h{h}v{n}_{k}")
            p.copy(sq[:, sl], psB[:], tag=f"h{h}vev{n}")
            p.ld(vb_d[:, sl], sq[:, sl], engine="sp", tag=f"h{h}vst{n}")
        for t in range(nS):
            p.ldT(vtk[:, t * 128:(t + 1) * 128], vb_d[:, t * 128:(t + 1) * 128],
                  engine="act", tag=f"h{h}vld{t}")
        for n in range(nP):
            b = n % 2
            sl = slice(n * 512, (n + 1) * 512)
            for i in range(nQR):
                p.ld(latp[b][i][:], latT[i * 128:(i + 1) * 128, sl],
                     tag=f"h{h}lp{n}_{i}")
            for k in range(nQR):
                p.mm(psA[:], wqu_t[k][:, hHD], latp[b][k][:],
                     start=(k == 0), stop=(k == nQR - 1), tag=f"h{h}qc{n}_{k}")
            p.copy(qc1[:, sl], psA[:], tag=f"h{h}qcev{n}")
            for k in range(nQR):
                p.mm(psB[0:64, :], wqr_t[k][:, hRD], latp[b][k][:],
                     start=(k == 0), stop=(k == nQR - 1), tag=f"h{h}qr{n}_{k}")
            for k in range(nQR):
                p.mm(psB[64:128, :], wqrp_t[k][:, hRD], latp[b][k][:],
                     start=(k == 0), stop=(k == nQR - 1), tag=f"h{h}qp{n}_{k}")
            p.copy(rpre[:, sl], psB[0:64, :], tag=f"h{h}qrev{n}")
            p.copy(rrot[:, sl], psB[64:128, :], tag=f"h{h}qpev{n}")
        rope_combine(qc2, rpre[:], rrot[:], f"h{h}qR")
        p.act(sq[:, 0:S], qc1[:], AF.Square, tag=f"h{h}sqq1")
        for n in range(nP):
            sl = slice(n * 512, (n + 1) * 512)
            p.mm(psN[:], ones_k[:], sq[:, sl], start=True, stop=False,
                 tag=f"h{h}qn1_{n}")
            p.act(sq[0:64, sl], qc2[0:64, sl], AF.Square, tag=f"h{h}sqq2_{n}")
            p.mm(psN[:], ones_k[0:64, :], sq[0:64, sl], start=False, stop=True,
                 tag=f"h{h}qn2_{n}")
            p.act(qn[:], psN[:], AF.Sqrt, tag=f"h{h}qnrm{n}")
            p.ld(qn_d[0:1, sl], qn[:], engine="sp", tag=f"h{h}qns{n}")
        p.act(sq[:, 0:S], kc1[:], AF.Square, tag=f"h{h}sqk1")
        for n in range(nP):
            sl = slice(n * 512, (n + 1) * 512)
            p.mm(psN[:], ones_k[:], sq[:, sl], start=True, stop=False,
                 tag=f"h{h}kn1_{n}")
            p.act(sq[0:64, sl], kc2[0:64, sl], AF.Square, tag=f"h{h}sqk2_{n}")
            p.mm(psN[:], ones_k[0:64, :], sq[0:64, sl], start=False, stop=True,
                 tag=f"h{h}kn2_{n}")
            if n == 0:
                p.red(rowbuf[:, 0:1], psN[:], ALU.max, tag=f"h{h}km{n}")
            else:
                p.red(rowbuf[:, 1:2], psN[:], ALU.max, tag=f"h{h}km{n}")
                p.tt(rowbuf[:, 0:1], rowbuf[:, 0:1], rowbuf[:, 1:2], ALU.max,
                     tag=f"h{h}kmx{n}")
        p.act(rowbuf[:, 2:3], rowbuf[:, 0:1], AF.Sqrt, tag=f"h{h}mk")
        for n in range(nP):
            sl = slice(n * 512, (n + 1) * 512)
            p.ld(qn[:], qn_d[0:1, sl], tag=f"h{h}qnl{n}")
            p.ts(qn[:], qn[:], rowbuf[:, 2:3], ALU.mult, -1.0, ALU.mult,
                 tag=f"h{h}brow{n}")
            p.copy(qnb[:], qn[:], tag=f"h{h}brow16{n}")
            p.ld(qc2[64:65, sl], qnb[:], engine="sp", tag=f"h{h}bmv{n}")

        for qt in range(nQ):
            nk = kq * qt + kq
            qsl = slice(qt * QT, (qt + 1) * QT)
            for kt in range(nk):
                j = kt % 2
                ksl = slice(kt * 128, (kt + 1) * 128)
                p.mm(psS[j][:], kc1[:, ksl], qc1[:, qsl], start=True, stop=False,
                     tag=f"h{h}s{qt}_{kt}a")
                p.mm(psS[j][:], kc2[0:65, ksl], qc2[0:65, qsl], start=False,
                     stop=True, tag=f"h{h}s{qt}_{kt}b")
                p.act(pT[j][:], psS[j][:], AF.Exp, scale=SCALE,
                      tag=f"h{h}e{qt}_{kt}")
                d = kt - kq * qt
                if d >= 0:
                    p.tt(pT[j][:], pT[j][:], masks[d][:], ALU.mult,
                         tag=f"h{h}m{qt}_{kt}")
                p.mm(psR[:], ones_k[:], pT[j][:], start=(kt == 0),
                     stop=(kt == nk - 1), tag=f"h{h}r{qt}_{kt}")
                p.mm(psO[:], vtk[:, ksl], pT[j][:], start=(kt == 0),
                     stop=(kt == nk - 1), tag=f"h{h}o{qt}_{kt}")
            jo = qt % 2
            p.copy(rs[:], psR[:], tag=f"h{h}rse{qt}")
            p.recip(rs[:], rs[:], tag=f"h{h}rr{qt}")
            p.mm(psS[0][:], ones_r[:], rs[:], start=True, stop=True,
                 tag=f"h{h}rb{qt}")
            p.copy(rb[:], psS[0][:], tag=f"h{h}rbe{qt}")
            p.tt(obuf[jo][:], psO[:], rb[:], ALU.mult, tag=f"h{h}on{qt}")
            p.ld(oT[h * HD:(h + 1) * HD, qsl], obuf[jo][:], engine="sp",
                 tag=f"h{h}ost{qt}")

    p.build()
    return nc


def ref_mla_full(x, wq_down, wq_up, wq_rope, wkv_down, wk_up, wv_up, wk_rope,
                 NH=16):
    import math
    S, D = x.shape
    mu = x.mean(-1, keepdims=True)
    h = (x - mu) / np.sqrt(x.var(-1, keepdims=True) + 1e-5)
    ql = h @ wq_down
    qc = (ql @ wq_up).reshape(S, NH, HD)
    qr = (ql @ wq_rope).reshape(S, NH, RD)
    cos, ssin = rope_tables(S)

    def rope(v):
        d = RD
        inv = 1.0 / 10000.0 ** (np.arange(0, d, 2) / d)
        fr = np.outer(np.arange(S), inv)
        c = np.concatenate([np.cos(fr), np.cos(fr)], -1)
        s = np.concatenate([np.sin(fr), np.sin(fr)], -1)
        rot = np.concatenate([-v[..., d // 2:], v[..., :d // 2]], -1)
        if v.ndim == 3:
            return v * c[:, None, :] + rot * s[:, None, :]
        return v * c + rot * s

    qr = rope(qr)
    kvl = h @ wkv_down
    kc = (kvl @ wk_up).reshape(S, NH, HD)
    v = (kvl @ wv_up).reshape(S, NH, HD)
    kr = rope(h @ wk_rope)
    kr = np.broadcast_to(kr[:, None, :], (S, NH, RD))
    q = np.concatenate([qc, qr], -1)
    k = np.concatenate([kc, kr], -1)
    scale = 1.0 / math.sqrt(HD + RD)
    out = np.zeros((S, NH, HD), np.float32)
    for hh in range(NH):
        sc = q[:, hh] @ k[:, hh].T * scale
        mask = np.triu(np.ones((S, S), bool), 1)
        sc = np.where(mask, -np.inf, sc)
        sc = sc - sc.max(-1, keepdims=True)
        e = np.exp(sc)
        a = e / e.sum(-1, keepdims=True)
        out[:, hh] = a @ v[:, hh]
    return out.reshape(S, NH * HD)






def build_oproj(R=1024, C=4096, D=2048, norm="none", TOKT=512, ln_eps=1e-6,
                rms_eps=1e-5):
    nc = bass.Bass("TRN2", target_bir_lowering=False)
    o = nc.dram_tensor("o", [R, C], BF16, kind="ExternalInput").ap()
    wT = nc.dram_tensor("wT", [C, D], F32, kind="ExternalInput").ap()
    x = nc.dram_tensor("x", [R, D], F32, kind="ExternalInput").ap()
    if norm in ("lnwb", "rmsw"):
        g = nc.dram_tensor("g", [C], F32, kind="ExternalInput").ap()
    if norm == "lnwb":
        bvec = nc.dram_tensor("b", [C], F32, kind="ExternalInput").ap()
    xo = nc.dram_tensor("xo", [R, D], F32, kind="ExternalOutput").ap()
    on_d = nc.dram_tensor("on_d", [R, C], BF16, kind="Internal").ap()

    p = Prog(nc)
    nT = R // 128
    nC = C // 128
    nN = R // TOKT
    nD = D // 512

    psA = p.psum("psA", [128, 512], F32)
    if norm in ("lnwb", "rmsw"):
        grow = p.sbuf("grow", [1, C], F32)
        grep = p.sbuf("grep", [128, C], F32)
        ones1 = p.sbuf("ones1", [1, 128], F32)
        p.memset(ones1[:], 1.0, tag="ones1")
        p.ld(grow[:], g.rearrange("(a c) -> a c", a=1), tag="gld")
        for i in range(C // 512):
            p.mm(psA[:], ones1[:], grow[:, i * 512:(i + 1) * 512], tag=f"gb{i}")
            p.copy(grep[:, i * 512:(i + 1) * 512], psA[:], tag=f"gbe{i}")
    if norm == "lnwb":
        brow = p.sbuf("brow", [1, C], F32)
        brep = p.sbuf("brep", [128, C], F32)
        p.ld(brow[:], bvec.rearrange("(a c) -> a c", a=1), tag="bld")
        for i in range(C // 512):
            p.mm(psA[:], ones1[:], brow[:, i * 512:(i + 1) * 512], tag=f"bb{i}")
            p.copy(brep[:, i * 512:(i + 1) * 512], psA[:], tag=f"bbe{i}")

    if norm != "none":
        ot_ = p.sbuf("ot_", [128, C], BF16)
        nt_ = p.sbuf("nt_", [128, C], F32)
        nb_ = p.sbuf("nb_", [128, C], BF16)
        scs = p.sbuf("scs", [128, 8], F32)
        for t in range(nT):
            p.ld(ot_[:], o[t * 128:(t + 1) * 128, :], tag=f"old{t}")
            mu, ss, var, rstd = scs[:, 0:1], scs[:, 1:2], scs[:, 2:3], scs[:, 3:4]
            if norm == "lnwb":
                p.red(mu, ot_[:], ALU.add, tag=f"n{t}:sum")
                p.act(nt_[:], ot_[:], AF.Square, accum_out=ss, tag=f"n{t}:sq")
                p.ts(mu, mu, 1.0 / C, ALU.mult, tag=f"n{t}:mu")
                p.tt(var, mu, mu, ALU.mult, tag=f"n{t}:mu2")
                p.stt(var, ss, 1.0 / C, var, ALU.mult, ALU.subtract, tag=f"n{t}:var")
                p.act(rstd, var, AF.Sqrt, bias=ln_eps, tag=f"n{t}:std")
                p.recip(rstd, rstd, tag=f"n{t}:rstd")
                p.ts(nt_[:], ot_[:], mu, ALU.subtract, rstd, ALU.mult,
                     tag=f"n{t}:xh")
                p.tt(nt_[:], nt_[:], grep[:], ALU.mult, tag=f"n{t}:g")
                p.tt(nb_[:], nt_[:], brep[:], ALU.add, tag=f"n{t}:b")
            else:
                p.act(nt_[:], ot_[:], AF.Square, accum_out=ss, tag=f"n{t}:sq")
                p.ts(ss, ss, 1.0 / C, ALU.mult, tag=f"n{t}:ms")
                p.act(rstd, ss, AF.Sqrt, bias=rms_eps, tag=f"n{t}:std")
                p.recip(rstd, rstd, tag=f"n{t}:rstd")
                p.ts(nt_[:], ot_[:], rstd, ALU.mult, tag=f"n{t}:xh")
                p.tt(nb_[:], nt_[:], grep[:], ALU.mult, tag=f"n{t}:g")
            p.ld(on_d[t * 128:(t + 1) * 128, :], nb_[:], engine="sp",
                 tag=f"nst{t}")
        osrc = on_d
    else:
        osrc = o

    ofm = [p.sbuf(f"ofm{i}", [128, R], BF16) for i in range(nC)]
    for i in range(nC):
        p.ldT(ofm[i][:], osrc[:, i * 128:(i + 1) * 128], engine="act",
              tag=f"ofm{i}")

    wtg = [p.sbuf(f"wtg{b}", [128, nC * 512], BF16) for b in range(2)]
    xt = [p.sbuf(f"xt{i}", [128, 512], F32) for i in range(2)]
    acc = [p.sbuf(f"acc{i}", [128, 512], F32) for i in range(2)]
    ps = [p.psum(f"ps{i}", [128, 512], F32) for i in range(2)]

    for n in range(nD):
        b = n % 2
        nsl = slice(n * 512, (n + 1) * 512)
        for k in range(nC):
            p.ld(wtg[b][:, k * 512:(k + 1) * 512], wT[k * 128:(k + 1) * 128, nsl],
                 engine="pool", tag=f"wld{n}_{k}")
        for t in range(nT):
            j = t % 2
            p.ld(xt[j][:], x[t * 128:(t + 1) * 128, nsl], tag=f"xld{n}_{t}")
            for k in range(nC):
                p.mm(ps[j][:], ofm[k][:, t * 128:(t + 1) * 128],
                     wtg[b][:, k * 512:(k + 1) * 512],
                     start=(k == 0), stop=(k == nC - 1), tag=f"mm{n}_{t}_{k}")
            p.tt(acc[j][:], ps[j][:], xt[j][:], ALU.add, tag=f"ev{n}_{t}")
            p.ld(xo[t * 128:(t + 1) * 128, nsl], acc[j][:], engine="sp",
                 tag=f"xst{n}_{t}")

    p.build()
    return nc



BIG = 1e9
LOG128 = float(np.log(128.0))


def make_selm():
    selm = np.zeros((16, 12 * 128), np.float32)
    for h in range(4):
        qh = h // 2
        c = lambda v: slice((v * 4 + h) * 128, (v * 4 + h + 1) * 128)
        selm[h, c(0)] = -1.0
        selm[6 + qh, c(0)] = -0.5
        selm[h, c(1)] = 1.0
        selm[6 + qh, c(1)] = -0.5
        selm[8 + h, c(1)] = 1.0
        selm[h, c(2)] = 1.0
        selm[4 + qh, c(2)] = -0.5
        selm[12, c(2)] = -0.5 * LOG128
    return selm


def pack_wproj(W_qkv, W_z, W_b, W_a, core):
    q = W_qkv[256 * core:256 * (core + 1)]
    k = W_qkv[2048 + 256 * core:2048 + 256 * (core + 1)]
    v = W_qkv[4096 + 512 * core:4096 + 512 * (core + 1)]
    z = W_z[512 * core:512 * (core + 1)]
    b = W_b[4 * core:4 * (core + 1)]
    a = W_a[4 * core:4 * (core + 1)]
    pad = np.zeros((28, 2048), np.float32)
    Wall = np.concatenate([q, k, v, z, b, pad, a, pad], 0)
    return np.ascontiguousarray(Wall.T)


def pack_convw(conv_w, core):
    cw = conv_w[:, 0, :] if conv_w.ndim == 3 else conv_w
    q = cw[256 * core:256 * (core + 1)]
    k = cw[2048 + 256 * core:2048 + 256 * (core + 1)]
    v = cw[4096 + 512 * core:4096 + 512 * (core + 1)]
    return np.ascontiguousarray(np.concatenate([q, k, v], 0))


def build_l0a(S=8192, D=2048):
    nc = bass.Bass("TRN2", target_bir_lowering=False)
    ln = nc.dram_tensor("ln", [S, D], BF16, kind="ExternalInput").ap()
    wproj = nc.dram_tensor("wproj", [D, 1600], F32, kind="ExternalInput").ap()
    convw = nc.dram_tensor("convw", [1024, 4], F32, kind="ExternalInput").ap()
    dtb = nc.dram_tensor("dtb", [4], F32, kind="ExternalInput").ap()
    alog = nc.dram_tensor("alog", [4], F32, kind="ExternalInput").ap()
    normw = nc.dram_tensor("normw", [128], F32, kind="ExternalInput").ap()
    selm_d = nc.dram_tensor("selm", [16, 12 * 128], F32, kind="ExternalInput").ap()
    o = nc.dram_tensor("o", [S, 512], BF16, kind="ExternalOutput").ap()
    pre_d = nc.dram_tensor("pre_d", [1024, S], BF16, kind="Internal").ap()
    qkvc_d = nc.dram_tensor("qkvc_d", [1024, S], BF16, kind="Internal").ap()
    zT_d = nc.dram_tensor("zT_d", [512, S], BF16, kind="Internal").ap()
    core_d = nc.dram_tensor("core_d", [S, 512], BF16, kind="Internal").ap()
    b_d = nc.dram_tensor("b_d", [4, S], F32, kind="Internal").ap()
    a_d = nc.dram_tensor("a_d", [4, S], F32, kind="Internal").ap()
    rows_d = nc.dram_tensor("rows_d", [16, S], F32, kind="Internal").ap()

    p = Prog(nc)
    nT = S // 128
    nP = S // 512
    BL = min(2048, S)
    NB = S // BL

    tmv = p.sbuf("tmv", [128, nT * 16], F32)
    i1k = p.sbuf("i1k", [128, nT * 4], F32)
    i1q = p.sbuf("i1q", [128, nT * 4], F32)
    j1t = p.sbuf("j1t", [128, nT * 4], F32)
    nci = p.sbuf("nci", [128, nT * 4], F32)
    ceq = p.sbuf("ceq", [128, nT * 4], F32)
    ckd = p.sbuf("ckd", [128, nT * 4], F32)
    ecsr = p.sbuf("ecsr", [4, nT * 2], F32)
    selm = p.sbuf("selm_s", [16, 12 * 128], F32)
    p.ld(selm[:], selm_d[:], engine="pool", tag="selmld")
    identT = p.sbuf("identT", [128, 128], F32)
    p.memset(identT[:], 1.0, tag="idT0")
    p.affine_select(identT[:], identT[:], pattern=[[-1, 128]],
                    compare_op=ALU.is_equal, fill=0.0, base=0,
                    channel_multiplier=1, tag="idT1")
    identb = p.sbuf("identb", [128, 128], BF16)
    p.copy(identb[:], identT[:], tag="idb")
    bdU = p.sbuf("bdU", [128, 128], BF16)
    p.memset(bdU[:], 1.0, tag="bdU0")
    p.affine_select(bdU[:], bdU[:], pattern=[[1, 128]], compare_op=ALU.is_ge,
                    fill=0.0, base=0, channel_multiplier=-1, tag="bdU1")
    p.memset(bdU[0:64, 64:128], 0.0, tag="bdU2")
    bdA = p.sbuf("bdA", [128, 128], BF16)
    p.memset(bdA[:], 1.0, tag="bdA0")
    p.memset(bdA[0:64, 64:128], 0.0, tag="bdA1")
    p.memset(bdA[64:128, 0:64], 0.0, tag="bdA2")
    PMc = p.sbuf("PMc", [128, 128], BF16)
    p.memset(PMc[:], 0.0, tag="PM0")
    p.affine_select(PMc[:], PMc[:], pattern=[[-1, 128]], compare_op=ALU.is_gt,
                    fill=-BIG, base=0, channel_multiplier=1, tag="PM1")
    p.memset(PMc[64:128, 0:64], -BIG, tag="PM2")
    PNs = p.sbuf("PNs", [128, 128], BF16)
    p.memset(PNs[:], 0.0, tag="PNs0")
    p.affine_select(PNs[:], PNs[:], pattern=[[1, 128]], compare_op=ALU.is_gt,
                    fill=-BIG, base=0, channel_multiplier=-1, tag="PNs1")
    p.memset(PNs[0:64, 64:128], -BIG, tag="PNs2")
    PNi = p.sbuf("PNi", [128, 128], BF16)
    p.memset(PNi[:], 0.0, tag="PNi0")
    p.affine_select(PNi[:], PNi[:], pattern=[[1, 128]], compare_op=ALU.is_ge,
                    fill=-BIG, base=0, channel_multiplier=-1, tag="PNi1")
    p.memset(PNi[0:64, 64:128], -BIG, tag="PNi2")
    ones128 = p.sbuf("ones128", [128, 1], BF16)
    p.memset(ones128[:], 1.0, tag="ones128")
    onesr = p.sbuf("onesr", [1, 128], F32)
    p.memset(onesr[:], 1.0, tag="onesr")
    ind4 = p.sbuf("ind4", [4, 512], F32)
    p.memset(ind4[:], 1.0, tag="ind40")
    p.affine_select(ind4[:].rearrange("p (b f) -> p b f", b=4),
                    ind4[:].rearrange("p (b f) -> p b f", b=4),
                    pattern=[[-1, 4], [0, 128]], compare_op=ALU.is_equal,
                    fill=0.0, base=0, channel_multiplier=1, tag="ind41")
    ind4h = [ind4[:, h * 128:(h + 1) * 128] for h in range(4)]

    p.push_scope()
    lnp = [[p.sbuf(f"lnp{b}_{i}", [128, 512], BF16) for i in range(16)]
           for b in range(2)]
    wt = [p.sbuf(f"wt{i}", [128, 1600], BF16) for i in range(16)]
    for i in range(16):
        p.ld(wt[i][:], wproj[i * 128:(i + 1) * 128, :], engine="pool",
             tag=f"wld{i}")
    psE = [p.psum(f"psE{i}", [128, 512], F32) for i in range(2)]
    evb = [p.sbuf(f"evb{i}", [128, 512], BF16) for i in range(2)]
    stb = p.sbuf("stb", [4, 512], F32)
    sta = p.sbuf("sta", [36, 512], F32)
    for n in range(nP):
        b = n % 2
        sl = slice(n * 512, (n + 1) * 512)
        for i in range(16):
            p.ldT(lnp[b][i][:], ln[sl, i * 128:(i + 1) * 128], engine="act",
                  tag=f"lnp{n}_{i}")
        for m in range(13):
            M0, M1 = m * 128, min(1600, (m + 1) * 128)
            j = m % 2
            for k in range(16):
                p.mm(psE[j][0:M1 - M0, :], wt[k][:, M0:M1], lnp[b][k][:],
                     start=(k == 0), stop=(k == 15), tag=f"pj{n}_{m}_{k}")
            if m < 12:
                p.copy(evb[j][:], psE[j][:], tag=f"pje{n}_{m}")
                dst = pre_d if m < 8 else zT_d
                r0 = M0 if m < 8 else M0 - 1024
                p.ld(dst[r0:r0 + 128, sl], evb[j][:], engine="sp",
                     tag=f"pjs{n}_{m}")
            else:
                p.copy(stb[:], psE[j][0:4, :], tag=f"pjb{n}")
                p.copy(sta[32:36, :], psE[j][32:36, :], tag=f"pja{n}")
                p.ld(b_d[:, sl], stb[:], engine="sp", tag=f"pbs{n}")
                p.ld(a_d[:, sl], sta[32:36, :], engine="sp", tag=f"pas{n}")
    p.pop_scope()

    p.push_scope()
    cw = p.sbuf("cw", [128, 4 * 8], F32)
    for t in range(8):
        p.ld(cw[:, t * 4:(t + 1) * 4], convw[t * 128:(t + 1) * 128, :],
             engine="pool", tag=f"cwld{t}")
    cin = p.sbuf("cin", [128, S + 3], BF16)
    cacc = [p.sbuf(f"cacc{i}", [128, S], F32) for i in range(2)]
    cout = p.sbuf("cout", [128, S], BF16)
    p.memset(cin[:, 0:3], 0.0, tag="cpad")
    for t in range(8):
        p.ld(cin[:, 3:S + 3], pre_d[t * 128:(t + 1) * 128, :], tag=f"cld{t}")
        p.ts(cacc[0][:], cin[:, 0:S], cw[:, t * 4:t * 4 + 1], ALU.mult,
             tag=f"cv{t}_0")
        p.stt(cacc[1][:], cin[:, 1:S + 1], cw[:, t * 4 + 1:t * 4 + 2],
              cacc[0][:], ALU.mult, ALU.add, tag=f"cv{t}_1")
        p.stt(cacc[0][:], cin[:, 2:S + 2], cw[:, t * 4 + 2:t * 4 + 3],
              cacc[1][:], ALU.mult, ALU.add, tag=f"cv{t}_2")
        p.stt(cacc[1][:], cin[:, 3:S + 3], cw[:, t * 4 + 3:t * 4 + 4],
              cacc[0][:], ALU.mult, ALU.add, tag=f"cv{t}_3")
        p.act(cout[:], cacc[1][:], AF.Silu, tag=f"cvs{t}")
        p.ld(qkvc_d[t * 128:(t + 1) * 128, :], cout[:], engine="sp",
             tag=f"cst{t}")
    p.pop_scope()

    gcb = p.sbuf("gcb", [4, BL], F32, side="right")
    lnb = p.sbuf("lnb", [4, BL], F32, side="right")
    lnsq = [p.sbuf(f"lnsq{r}", [1, BL], F32, side="right") for r in range(4)]
    onesbl = p.sbuf("onesbl", [1, BL], F32, side="right")
    g_r = p.sbuf("g_r", [4, BL], F32, side="right")
    bco = p.sbuf("bco", [4, BL], F32, side="right")
    p.push_scope()
    braw = p.sbuf("braw", [4, BL], F32)
    araw = p.sbuf("araw", [4, BL], F32)
    tr1 = p.sbuf("tr1", [4, BL], F32)
    p.memset(onesbl[:], 1.0, tag="onesbl")
    rtp = p.sbuf("rtp", [16, BL], F32)
    sqs = p.sbuf("sqs", [128, 512], BF16)
    psN1 = p.psum("psN1", [1, 512], F32)
    psT3 = p.psum("psT3", [128, 16], F32)
    psC3 = p.psum("psC3", [128, 8], F32)
    psR4 = p.psum("psR4", [4, 128], F32)
    dtbt = p.sbuf("dtbt", [4, 1], F32)
    alogt = p.sbuf("alogt", [4, 1], F32)
    p.ld(dtbt[:], dtb.rearrange("(a b) -> a b", b=1), tag="dtbld")
    p.ld(alogt[:], alog.rearrange("(a b) -> a b", b=1), tag="alogld")
    p.act(alogt[:], alogt[:], AF.Exp, tag="sg:ea")
    p.ts(alogt[:], alogt[:], -1.0, ALU.mult, tag="sg:nega")

    def softplus_rows(dst, src, tag):
        p.act(tr1[:], src, AF.Abs, tag=f"{tag}:abs")
        p.act(tr1[:], tr1[:], AF.Exp, scale=-1.0, tag=f"{tag}:exp")
        p.act(tr1[:], tr1[:], AF.Ln, bias=1.0, tag=f"{tag}:ln")
        p.ts(dst, src, 0.0, ALU.max, tag=f"{tag}:max")
        p.tt(dst, dst, tr1[:], ALU.add, tag=f"{tag}:add")

    for blk in range(NB):
        bsl = slice(blk * BL, (blk + 1) * BL)
        nt0 = blk * (BL // 128)
        p.ld(braw[:], b_d[:, bsl], tag=f"b3b{blk}")
        p.ld(araw[:], a_d[:, bsl], tag=f"b3a{blk}")
        p.ts(g_r[:], braw[:], -1.0, ALU.mult, tag=f"b3neg{blk}")
        softplus_rows(lnb[:], g_r[:], f"b3sb{blk}")
        p.ts(lnb[:], lnb[:], -1.0, ALU.mult, tag=f"b3lnb{blk}")
        p.act(bco[:], braw[:], AF.Sigmoid, tag=f"b3sig{blk}")
        p.ts(braw[:], araw[:], dtbt[:], ALU.add, tag=f"b3adt{blk}")
        softplus_rows(g_r[:], braw[:], f"b3sg{blk}")
        p.ts(g_r[:], g_r[:], alogt[:], ALU.mult, tag=f"b3g{blk}")
        for r in range(4):
            src_row = r * 128 if r < 2 else 256 + (r - 2) * 128
            for n2 in range(BL // 512):
                gsl = slice(blk * BL + n2 * 512, blk * BL + (n2 + 1) * 512)
                p.ld(sqs[:], qkvc_d[src_row:src_row + 128, gsl],
                     tag=f"b3sq{blk}_{r}_{n2}")
                p.act(sqs[:], sqs[:], AF.Square, tag=f"b3sa{blk}_{r}_{n2}")
                p.mm(psN1[:], ones128[:], sqs[:], tag=f"b3sm{blk}_{r}_{n2}")
                p.act(lnsq[r][0:1, n2 * 512:(n2 + 1) * 512], psN1[:], AF.Ln,
                      bias=1e-6, tag=f"b3sl{blk}_{r}_{n2}")
        for r in range(4):
            p.ld(rtp[r:r + 1, :], lnsq[r][0:1, :], engine="pool",
                 tag=f"b3p{blk}_{r}")
        p.ld(rtp[4:8, :], lnb[:], engine="pool", tag=f"b3pl{blk}")
        p.ld(rtp[8:12, :], g_r[:], engine="pool", tag=f"b3pg{blk}")
        p.ld(rtp[12:16, :], bco[:], engine="pool", tag=f"b3pb{blk}")
        for tt_ in range(BL // 128):
            t = nt0 + tt_
            lsl = slice(tt_ * 128, (tt_ + 1) * 128)
            p.transp(psT3[:], rtp[:, lsl], identT[0:16, 0:16],
                     tag=f"b3t{blk}_{tt_}")
            p.copy(tmv[:, t * 16:(t + 1) * 16], psT3[:], tag=f"b3te{blk}_{tt_}")
        for tt_ in range(BL // 128):
            t = nt0 + tt_
            lsl = slice(tt_ * 128, (tt_ + 1) * 128)
            gslice = tmv[:, t * 16 + 8:t * 16 + 12]
            p.copy(sqs[:, 0:4], gslice, tag=f"b3gb{blk}_{tt_}")
            p.mm(psC3[:, 0:4], bdU[:], sqs[:, 0:4], tag=f"b3c1_{blk}_{tt_}")
            p.mm(psC3[:, 4:8], bdA[:], sqs[:, 0:4], tag=f"b3c2_{blk}_{tt_}")
            p.copy(i1k[:, t * 4:(t + 1) * 4], psC3[:, 0:4], tag=f"b3ce{blk}_{tt_}")
            p.copy(ckd[:, t * 4:(t + 1) * 4], psC3[:, 4:8], tag=f"b3cf{blk}_{tt_}")
            p.mm(psR4[:], sqs[:, 0:4], bdU[:], tag=f"b3gt{blk}_{tt_}")
            p.copy(gcb[:, lsl], psR4[:], tag=f"b3gte{blk}_{tt_}")
            p.mm(psR4[:], sqs[:, 0:4], bdA[:], tag=f"b3cs{blk}_{tt_}")
            p.copy(ecsr[:, t * 2:t * 2 + 2],
                   psR4[:].rearrange("p (a c) -> p a c", a=2)[:, :, 63],
                   tag=f"b3cse{blk}_{tt_}")
        p.fence(tag=f"b3f{blk}")
        p.ld(rows_d[0:4, bsl], gcb[:], engine="sp", tag=f"b3r0{blk}")
        for r in range(4):
            p.ld(rows_d[4 + r:5 + r, bsl], lnsq[r][0:1, :], engine="sp",
                 tag=f"b3r1{blk}_{r}")
        p.ld(rows_d[8:12, bsl], lnb[:], engine="sp", tag=f"b3r2{blk}")
        for rz in range(12, 16):
            p.ld(rows_d[rz:rz + 1, bsl], onesbl[:], engine="sp",
                 tag=f"b3r3{blk}_{rz}")
    p.act(ecsr[:], ecsr[:], AF.Exp, tag="ecse")
    tm16 = tmv[:].rearrange("p (t k) -> p t k", k=16)
    gc4 = i1k[:].rearrange("p (t k) -> p t k", k=4)
    cs4 = ckd[:].rearrange("p (t k) -> p t k", k=4)
    lnb_t = tm16[:, :, 4:8]
    ssqq3 = tm16[:, :, 0:2]
    ssqk3 = tm16[:, :, 2:4]

    def rsl4(ap, rr):
        return ap.rearrange("p (t k) -> p t k", k=4)[:, :, rr::2]

    for rr in range(2):
        p.stt(rsl4(j1t[:], rr), ssqk3, -0.5, rsl4(i1k[:], rr), ALU.mult,
              ALU.subtract, tag=f"cj1_{rr}")
        p.stt(rsl4(i1q[:], rr), ssqq3, -0.5, rsl4(i1k[:], rr), ALU.mult,
              ALU.add, tag=f"ciq1_{rr}")
    p.ts(i1q[:], i1q[:], -0.5 * LOG128, ALU.add, tag="ciq2")
    p.tt(ckd[:].rearrange("p (t k) -> p t k", k=4), cs4, gc4,
         ALU.subtract, tag="ckd1")
    for rr in range(2):
        p.stt(rsl4(ckd[:], rr), ssqk3, -0.5, rsl4(ckd[:], rr), ALU.mult,
              ALU.add, tag=f"ckd2_{rr}")
    p.tt(gc4, gc4, lnb_t, ALU.add, tag="ci1k1")
    for rr in range(2):
        p.stt(rsl4(i1k[:], rr), ssqk3, -0.5, rsl4(i1k[:], rr), ALU.mult,
              ALU.add, tag=f"ci1k2_{rr}")
    p.act(nci[:], i1k[:], AF.Exp, tag="cnci")
    p.ts(nci[:], nci[:], -1.0, ALU.mult, tag="cnci2")
    p.act(ceq[:], i1q[:], AF.Exp, tag="cceq")
    p.act(ckd[:], ckd[:], AF.Exp, tag="cckd")
    p.pop_scope()

    p.push_scope()
    kfm = p.sbuf("kfm", [128, S], BF16)
    qfm = p.sbuf("qfm", [128, S], BF16)
    ktm = p.sbuf("ktm", [128, S], BF16)
    vtm = p.sbuf("vtm", [128, S], BF16)
    Zb = p.sbuf("Zb", [128, S], BF16)
    atb = p.sbuf("atb", [128, S], BF16)
    rstg = [p.sbuf(f"rstg{i}", [16, 1024], F32) for i in range(2)]
    expw = p.sbuf("expw", [128, 128], BF16)
    Mt = p.sbuf("Mt", [128, 128], BF16)
    Nt = p.sbuf("Nt", [128, 128], BF16)
    pw = [p.sbuf(f"pw{i}", [128, 128], BF16) for i in range(2)]
    Tb = [p.sbuf(f"Tb{i}", [128, 128], BF16) for i in range(2)]
    Ssb = p.sbuf("Ssb", [128, 128], BF16)
    vnb = p.sbuf("vnb", [128, 128], BF16)
    kds = p.sbuf("kds", [128, 128], BF16)
    oas = p.sbuf("oas", [128, 128], BF16)
    gcol = p.sbuf("gcol", [128, 1], F32)
    otmp = p.sbuf("otmp", [128, 128], F32)
    psG = p.psum("psG", [128, 128], F32)
    psP = p.psum("psP", [128, 128], F32)
    psP2 = p.psum("psP2", [128, 128], F32)
    psW = p.psum("psW", [128, 128], F32)
    psSt = p.psum("psSt", [128, 128], F32)
    psQ = p.psum("psQ", [128, 128], F32)
    psA2 = p.psum("psA2", [128, 128], F32)
    psg1 = p.psum("psg1", [128, 1], F32)
    tm16b = tmv[:].rearrange("p (t k) -> p t k", k=16)

    for h in range(4):
        qk = h // 2
        qrow, krow, vrow = 128 * qk, 256 + 128 * qk, 512 + 128 * h
        for n in range(nP):
            sl = slice(n * 512, (n + 1) * 512)
            p.ld(kfm[:, sl], qkvc_d[krow:krow + 128, sl], tag=f"h{h}kf{n}")
            p.ld(qfm[:, sl], qkvc_d[qrow:qrow + 128, sl], tag=f"h{h}qf{n}")
        for t in range(nT):
            tsl = slice(t * 128, (t + 1) * 128)
            p.ldT(ktm[:, tsl], qkvc_d[krow:krow + 128, tsl], engine="act",
                  tag=f"h{h}kt{t}")
            p.ldT(vtm[:, tsl], qkvc_d[vrow:vrow + 128, tsl], engine="act",
                  tag=f"h{h}vt{t}")
        selM = selm[:, (0 * 4 + h) * 128:(0 * 4 + h + 1) * 128]
        selN = selm[:, (1 * 4 + h) * 128:(1 * 4 + h + 1) * 128]
        selA = selm[:, (2 * 4 + h) * 128:(2 * 4 + h + 1) * 128]
        for t in range(nT):
            tsl = slice(t * 128, (t + 1) * 128)
            RG = min(8, nT)
            if t % RG == 0:
                rb = rstg[(t // RG) % 2]
                p.ld(rb[:, 0:RG * 128], rows_d[:, t * 128:(t + RG) * 128],
                     engine="pool", tag=f"h{h}rl{t}")
            rsl = rstg[(t // RG) % 2][:, (t % RG) * 128:(t % RG + 1) * 128]
            ck = i1k[:, t * 4 + h:t * 4 + h + 1]
            cj = j1t[:, t * 4 + h:t * 4 + h + 1]
            cb = tm16b[:, t, 12 + h:13 + h]
            p.mm(psG[:], kfm[:, tsl], kfm[:, tsl], tag=f"h{h}G{t}")
            p.mm(psP[:], selM, rsl, start=True, stop=False, tag=f"h{h}pm{t}")
            p.mm(psP[:], identb[:], PMc[:], start=False, stop=True,
                 tag=f"h{h}pm2{t}")
            p.act(expw[:], psP[:], AF.Exp, bias=ck, tag=f"h{h}em{t}")
            p.tt(Mt[:], psG[:], expw[:], ALU.mult, tag=f"h{h}M{t}")
            p.mm(psP2[:], selN, rsl, start=True, stop=False, tag=f"h{h}pn{t}")
            p.mm(psP2[:], identb[:], PNs[:], start=False, stop=True,
                 tag=f"h{h}pn2{t}")
            p.act(expw[:], psP2[:], AF.Exp, bias=cj, tag=f"h{h}en{t}")
            p.tt(Nt[:], psG[:], expw[:], ALU.mult, tag=f"h{h}N{t}")
            p.ts(Tb[0][:], vtm[:, tsl], cb, ALU.mult, tag=f"h{h}T0{t}")
            curM, curN = Mt, Nt
            src = 0
            for lev in range(5):
                p.mm(psP[:], curM[:], curN[:], tag=f"h{h}pwn{t}_{lev}")
                p.mm(psP2[:], curN[:], curM[:], tag=f"h{h}pwm{t}_{lev}")
                pm, pn = pw[0], pw[1]
                p.copy(pm[:], psP2[:], tag=f"h{h}pwem{t}_{lev}")
                p.copy(pn[:], psP[:], engine="act", tag=f"h{h}pwen{t}_{lev}")
                p.mm(psP[:], pn[:], Tb[src][:], tag=f"h{h}ap{t}_{lev}")
                p.tt(Tb[1 - src][:], psP[:], Tb[src][:], ALU.add,
                     tag=f"h{h}ape{t}_{lev}")
                src = 1 - src
                curM, curN = pm, pn
            p.mm(psP[:], Nt[:], Tb[src][:], tag=f"h{h}apf{t}")
            p.raw("dve",
                  lambda e, o_=Zb[:, tsl], a=Tb[src][:], b=psP[:]:
                  e.tensor_tensor(out=o_, in0=a, in1=b, op=ALU.subtract),
                  [psP[:], Tb[src][:]], [Zb[:, tsl]], tag=f"h{h}Z{t}")
            p.mm(psG[:], kfm[:, tsl], qfm[:, tsl], tag=f"h{h}Gq{t}")
            p.mm(psP2[:], selA, rsl, start=True, stop=False, tag=f"h{h}pa{t}")
            p.mm(psP2[:], identb[:], PNi[:], start=False, stop=True,
                 tag=f"h{h}pa2{t}")
            p.act(expw[:], psP2[:], AF.Exp, bias=cj, tag=f"h{h}ea{t}")
            p.tt(atb[:, tsl], psG[:], expw[:], ALU.mult, tag=f"h{h}A{t}")
        p.memset(Ssb[:], 0.0, tag=f"h{h}S0")
        for c in range(nT * 2):
            t, half = c // 2, c % 2
            csl = slice(c * 64, (c + 1) * 64)
            hsl = slice(half * 64, half * 64 + 64)
            p.mm(psW[hsl, :], kfm[:, csl], Ssb[:], tag=f"h{h}W{c}")
            ncol = nci[hsl, t * 4 + h:t * 4 + h + 1]
            p.raw("dve",
                  lambda e, o_=vnb[hsl, :], w=psW[hsl, :], s=ncol,
                  z=Zb[hsl, t * 128:(t + 1) * 128]:
                  e.scalar_tensor_tensor(out=o_, in0=w, scalar=s, in1=z,
                                         op0=ALU.mult, op1=ALU.add),
                  [psW[:], ncol, Zb[hsl, t * 128:(t + 1) * 128]], [vnb[:]],
                  tag=f"h{h}vn{c}")
            p.mm(psQ[hsl, :], qfm[:, csl], Ssb[:], tag=f"h{h}q{c}")
            p.mm(psA2[hsl, :],
                 atb[hsl, t * 128 + half * 64:t * 128 + half * 64 + 64],
                 vnb[hsl, :], tag=f"h{h}av{c}")
            qcol = ceq[hsl, t * 4 + h:t * 4 + h + 1]
            p.act(otmp[hsl, :], psQ[hsl, :], AF.Copy, scale=qcol,
                  engine="act", tag=f"h{h}oe{c}")
            p.tt(oas[hsl, :], otmp[hsl, :], psA2[hsl, :], ALU.add,
                 tag=f"h{h}oa{c}")
            kcol = ckd[hsl, t * 4 + h:t * 4 + h + 1]
            p.ts(kds[hsl, :], ktm[hsl, t * 128:(t + 1) * 128], kcol, ALU.mult,
                 tag=f"h{h}kd{c}")
            p.mm(psSt[:], kds[hsl, :], vnb[hsl, :], tag=f"h{h}st{c}")
            p.mm(psg1[:], ind4h[h], ecsr[:, c:c + 1], tag=f"h{h}g{c}")
            p.copy(gcol[:], psg1[:], tag=f"h{h}ge{c}")
            p.raw("dve",
                  lambda e, o_=Ssb[:], s=gcol[:], ps=psSt[:]:
                  e.scalar_tensor_tensor(out=o_, in0=o_, scalar=s, in1=ps,
                                         op0=ALU.mult, op1=ALU.add),
                  [Ssb[:], gcol[:], psSt[:]], [Ssb[:]], tag=f"h{h}su{c}")
            if half == 1:
                p.ld(core_d[t * 128:(t + 1) * 128, h * 128:(h + 1) * 128],
                     oas[:], engine="sp", tag=f"h{h}os{t}")
    p.pop_scope()

    p.push_scope()
    ztm = p.sbuf("ztm", [128, 512], BF16)
    ctm = p.sbuf("ctm", [128, 512], BF16)
    wrep = p.sbuf("wrep", [128, 512], F32)
    nrow = p.sbuf("nrow", [1, 512], F32)
    sred = p.sbuf("sred", [128, 4], F32)
    g1 = p.sbuf("g1", [128, 512], F32)
    g2 = p.sbuf("g2", [128, 512], F32)
    ob = p.sbuf("ob", [128, 512], BF16)
    psB5 = p.psum("psB5", [128, 512], F32)
    p.ld(nrow[:, 0:128], normw.rearrange("(a b) -> a b", a=1), tag="nwld")
    p.copy(nrow[:, 128:256], nrow[:, 0:128], tag="nw2")
    p.copy(nrow[:, 256:512], nrow[:, 0:256], tag="nw3")
    p.mm(psB5[:], onesr[:], nrow[:], tag="nwb")
    p.copy(wrep[:], psB5[:], tag="nwbe")
    for t in range(nT):
        tsl = slice(t * 128, (t + 1) * 128)
        p.ld(ctm[:], core_d[tsl, :], tag=f"g5c{t}")
        for i in range(4):
            p.ldT(ztm[:, i * 128:(i + 1) * 128],
                  zT_d[i * 128:(i + 1) * 128, tsl], engine="act",
                  tag=f"g5z{t}_{i}")
        p.act(g1[:], ctm[:], AF.Square, tag=f"g5sa{t}")
        p.red(sred[:], g1[:].rearrange("p (h d) -> p h d", h=4), ALU.add,
              tag=f"g5sq{t}")
        p.ts(sred[:], sred[:], 1.0 / 128, ALU.mult, tag=f"g5m{t}")
        p.act(sred[:], sred[:], AF.Sqrt, bias=1e-6, tag=f"g5s{t}")
        p.recip(sred[:], sred[:], tag=f"g5rc{t}")
        srb = sred[:].broadcast_to([128, 4, 128])
        c3 = ctm[:].rearrange("p (h d) -> p h d", h=4)
        p.tt(g1[:].rearrange("p (h d) -> p h d", h=4), c3, srb, ALU.mult,
             tag=f"g5n{t}")
        p.act(g2[:], ztm[:], AF.Silu, tag=f"g5si{t}")
        p.tt(g2[:], g2[:], wrep[:], ALU.mult, engine="pool", tag=f"g5w{t}")
        p.tt(ob[:], g1[:], g2[:], ALU.mult, tag=f"g5o{t}")
        p.ld(o[tsl, :], ob[:], engine="sp", tag=f"g5st{t}")
    p.pop_scope()
    p.build()
    return nc



REC = 840


def pack_wproj_ttt(q_w, k_w, v_w, lr_w, core):
    q = q_w[256 * core:256 * (core + 1)]
    k = k_w[256 * core:256 * (core + 1)]
    v = v_w[256 * core:256 * (core + 1)]
    lr = lr_w[4 * core:4 * (core + 1), 0, :]
    pad = np.zeros((28, 2048), np.float32)
    return np.ascontiguousarray(np.concatenate([q, k, v, lr, pad], 0).T)


def build_l0b(S=8192, D=2048, cut=9):
    nc = bass.Bass("TRN2", target_bir_lowering=False)
    ln = nc.dram_tensor("ln", [S, D], BF16, kind="ExternalInput").ap()
    wproj = nc.dram_tensor("wproj", [D, 800], F32, kind="ExternalInput").ap()
    w1in = nc.dram_tensor("w1in", [256, 64], F32, kind="ExternalInput").ap()
    b1in = nc.dram_tensor("b1in", [1, 256], F32, kind="ExternalInput").ap()
    tnw = nc.dram_tensor("tnw", [1, 256], F32, kind="ExternalInput").ap()
    tnb = nc.dram_tensor("tnb", [1, 256], F32, kind="ExternalInput").ap()
    tib = nc.dram_tensor("tib", [1, 16], F32, kind="ExternalInput").ap()
    tilearn = nc.dram_tensor("tilearn", [1, 16], F32, kind="ExternalInput").ap()
    lrb = nc.dram_tensor("lrb", [4], F32, kind="ExternalInput").ap()
    o = nc.dram_tensor("o", [S, 256], BF16, kind="ExternalOutput").ap()
    qf_d = nc.dram_tensor("qf_d", [256, S], BF16, kind="Internal").ap()
    kf_d = nc.dram_tensor("kf_d", [256, S], BF16, kind="Internal").ap()
    vf_d = nc.dram_tensor("vf_d", [256, S], BF16, kind="Internal").ap()
    st_d = nc.dram_tensor("st_d", [S // 16, 16, REC], BF16, kind="Internal").ap()
    zb_d = nc.dram_tensor("zb_d", [S // 16, 16, 256], BF16, kind="Internal").ap()
    xqa_d = nc.dram_tensor("xqa_d", [S, 256], BF16, kind="Internal").ap()

    p = Prog(nc)
    nT = S // 128
    nP = S // 512
    NS = S // 16

    onesr = p.sbuf("onesr", [1, 128], F32)
    p.memset(onesr[:], 1.0, tag="onesr")
    onescb = p.sbuf("onescb", [16, 1], BF16)
    p.memset(onescb[:], 1.0, tag="onescb")
    ones1b = p.sbuf("ones1b", [1, 16], BF16)
    p.memset(ones1b[:], 1.0, tag="ones1b")

    p.push_scope()
    lnp = [[p.sbuf(f"lnp{b}_{i}", [128, 512], BF16) for i in range(16)]
           for b in range(2)]
    wt = [p.sbuf(f"wt{i}", [128, 800], BF16) for i in range(16)]
    for i in range(16):
        p.ld(wt[i][:], wproj[i * 128:(i + 1) * 128, :], engine="pool",
             tag=f"wld{i}")
    psE = [p.psum(f"psE{i}", [128, 512], F32) for i in range(2)]
    evb = [p.sbuf(f"evb{i}", [128, 512], BF16) for i in range(2)]
    lrr = p.sbuf("lrr", [4, S], F32, side="right")
    for n in range(nP):
        b = n % 2
        sl = slice(n * 512, (n + 1) * 512)
        for i in range(16):
            p.ldT(lnp[b][i][:], ln[sl, i * 128:(i + 1) * 128], engine="act",
                  tag=f"lnp{n}_{i}")
        for m in range(7):
            M0, M1 = m * 128, min(800, (m + 1) * 128)
            j = m % 2
            for k in range(16):
                p.mm(psE[j][0:M1 - M0, :], wt[k][:, M0:M1], lnp[b][k][:],
                     start=(k == 0), stop=(k == 15), tag=f"pj{n}_{m}_{k}")
            if m < 6:
                p.copy(evb[j][:], psE[j][:], tag=f"pje{n}_{m}")
                dst = [qf_d, qf_d, kf_d, kf_d, vf_d, vf_d][m]
                r0 = (m % 2) * 128
                p.ld(dst[r0:r0 + 128, sl], evb[j][:], engine="sp",
                     tag=f"pjs{n}_{m}")
            else:
                p.copy(lrr[:, sl], psE[j][0:4, :], tag=f"pjl{n}")
    p.pop_scope()

    grow = p.sbuf("grow", [1, 256], F32)
    browf = p.sbuf("browf", [1, 256], F32)
    trow = p.sbuf("trow", [1, 16], F32)
    trow512 = p.sbuf("trow512", [1, 512], F32)
    ti15 = p.sbuf("ti15", [16, 1], F32)
    G2R16 = p.sbuf("G2R16", [16, 256], F32)
    gam16 = p.sbuf("gam16", [16, 256], F32)
    bet16 = p.sbuf("bet16", [16, 256], F32)
    TIMbig = p.sbuf("TIMbig", [16, 512], BF16)
    psW = p.psum("psW", [128, 512], F32)
    p.ld(grow[:], tnw[:], tag="gld")
    p.ld(browf[:], tnb[:], tag="bld")
    p.mm(psW[0:16, 0:256], onesr[:, 0:16], grow[:], tag="gb1")
    p.copy(gam16[:], psW[0:16, 0:256], tag="gb2")
    p.act(G2R16[:], gam16[:], AF.Square, tag="gb3")
    p.mm(psW[0:16, 0:256], onesr[:, 0:16], browf[:], tag="bb1")
    p.copy(bet16[:], psW[0:16, 0:256], tag="bb2")
    tl2 = p.sbuf("tl2", [1, 16], F32)
    p.ld(trow[:], tib[:], tag="tld")
    p.ld(tl2[:], tilearn[:], tag="tl2")
    p.tt(trow[:], trow[:], tl2[:], ALU.add, tag="tadd")
    p.act(trow[:], trow[:], AF.Relu, tag="trelu")
    p.raw("dve",
          lambda e: e.tensor_copy(
              out=trow512[:].rearrange("a (s c) -> a s c", c=16),
              in_=trow[:].rearrange("a c -> a c").broadcast_to([1, 16, 32])
              .transpose([0, 2, 1])),
          [trow[:]], [trow512[:]], tag="t512")
    p.mm(psW[0:16, 0:512], onesr[:, 0:16], trow512[:], tag="tb1")
    p.memset(TIMbig[:], -1.0, tag="tim0")
    p.affine_select(TIMbig[:].rearrange("p (s c) -> p s c", c=16),
                    TIMbig[:].rearrange("p (s c) -> p s c", c=16),
                    pattern=[[0, 32], [1, 16]], compare_op=ALU.is_ge,
                    fill=0.0, base=0, channel_multiplier=-1, tag="tim1")
    p.raw("dve", lambda e: e.tensor_tensor(out=TIMbig[:], in0=TIMbig[:],
                                           in1=psW[0:16, 0:512], op=ALU.mult),
          [TIMbig[:], psW[:]], [TIMbig[:]], tag="tim2")
    p.mm(psW[0:16, 0:1], onesr[:, 0:16], trow[:, 15:16], tag="ti151")
    p.copy(ti15[:], psW[0:16, 0:1], tag="ti152")
    lrbt = p.sbuf("lrbt", [4, 1], F32)
    p.ld(lrbt[:], lrb.rearrange("(a b) -> a b", b=1), tag="lrbld")
    p.ts(lrr[:], lrr[:], lrbt[:], ALU.add, tag="slr0")
    p.act(lrr[:], lrr[:], AF.Sigmoid, tag="slr1")
    p.ts(lrr[:], lrr[:], 1.0 / 64, ALU.mult, tag="slr2")

    p.push_scope()
    xq_t = p.sbuf("xq_t", [128, 256], BF16)
    xk_t = p.sbuf("xk_t", [128, 256], BF16)
    xv_t = p.sbuf("xv_t", [128, 256], BF16)
    tg_t = p.sbuf("tg_t", [128, 256], BF16)
    t1_t = p.sbuf("t1_t", [128, 256], F32)
    brep = p.sbuf("brep", [128, 256], F32)
    grep = p.sbuf("grep", [128, 256], F32)
    slr_b = p.sbuf("slr_b", [128, 8], BF16)
    psT = p.psum("psT", [128, 16], F32)
    identT = p.sbuf("identT", [16, 16], F32)
    p.memset(identT[:], 1.0, tag="idT0")
    p.affine_select(identT[:], identT[:], pattern=[[-1, 16]],
                    compare_op=ALU.is_equal, fill=0.0, base=0,
                    channel_multiplier=1, tag="idT1")
    p.mm(psW[:, 0:256], onesr[:], browf[:], tag="brep1")
    p.copy(brep[:], psW[:, 0:256], tag="brep2")
    p.mm(psW[:, 0:256], onesr[:], grow[:], tag="grep1")
    p.copy(grep[:], psW[:, 0:256], tag="grep2")
    for t in range(nT):
        tsl = slice(t * 128, (t + 1) * 128)
        for pr in range(2):
            psl = slice(pr * 128, (pr + 1) * 128)
            p.ldT(xq_t[:, psl], qf_d[psl, tsl], engine="act", tag=f"q2t{t}_{pr}")
            p.ldT(xk_t[:, psl], kf_d[psl, tsl], engine="act", tag=f"k2t{t}_{pr}")
            p.ldT(xv_t[:, psl], vf_d[psl, tsl], engine="act", tag=f"v2t{t}_{pr}")
        p.raw("dve", lambda e, o_=t1_t[:], a=xk_t[:], b=xv_t[:]:
              e.tensor_tensor(out=o_, in0=a, in1=b, op=ALU.subtract),
              [xk_t[:], xv_t[:]], [t1_t[:]], tag=f"tg1{t}")
        p.tt(t1_t[:], t1_t[:], brep[:], ALU.add, engine="pool", tag=f"tg2{t}")
        p.tt(tg_t[:], t1_t[:], grep[:], ALU.mult, tag=f"tg3{t}")
        p.transp(psT[:, 0:4], lrr[:, tsl], identT[0:4, 0:4], tag=f"sl1{t}")
        if t == 0:
            p.memset(slr_b[:, 4:8], 0.0, tag="slrz")
        p.copy(slr_b[:, 0:4], psT[0:128, 0:4], tag=f"sl2{t}")
        t0 = t * 8
        p.ld(st_d[t0:t0 + 8, :, 0:256].rearrange("s p c -> (s p) c"),
             xq_t[:], engine="sp", tag=f"sw1{t}")
        p.ld(st_d[t0:t0 + 8, :, 256:512].rearrange("s p c -> (s p) c"),
             xk_t[:], engine="sp", tag=f"sw2{t}")
        p.ld(st_d[t0:t0 + 8, :, 512:768].rearrange("s p c -> (s p) c"),
             tg_t[:], engine="sp", tag=f"sw3{t}")
        p.ld(st_d[t0:t0 + 8, :, 832:840].rearrange("s p c -> (s p) c"),
             slr_b[:], engine="pool", tag=f"sw4{t}")
        p.ld(xqa_d[tsl, :], xq_t[:], engine="pool", tag=f"sw5{t}")
    p.pop_scope()

    p.push_scope()
    kh = [p.sbuf(f"kh{h}", [64, 128], BF16) for h in range(4)]
    qh = [p.sbuf(f"qh{h}", [64, 128], BF16) for h in range(4)]
    g2f = p.sbuf("g2f", [16, 512], F32)
    g2s = p.sbuf("g2s", [16, 512], BF16)
    psA = p.psum("psA", [16, 512], F32)
    for grp in range(NS // 8):
        ssl = slice(grp * 128, (grp + 1) * 128)
        for h in range(4):
            p.ld(kh[h][:], kf_d[64 * h:64 * h + 64, ssl], tag=f"gk{grp}_{h}")
            p.ld(qh[h][:], qf_d[64 * h:64 * h + 64, ssl], tag=f"gq{grp}_{h}")
        for sl_ in range(8):
            for h in range(4):
                tsl2 = slice(sl_ * 16, sl_ * 16 + 16)
                p.mm(psA[0:16, 16 * (4 * sl_ + h):16 * (4 * sl_ + h) + 16],
                     kh[h][:, tsl2], qh[h][:, tsl2],
                     tag=f"a1_{grp}_{sl_}_{h}")
        p.ts(g2f[:], psA[:], 1.0, ALU.add, tag=f"g2a{grp}")
        p.tt(g2s[:], g2f[:], TIMbig[:], ALU.mult, tag=f"g2b{grp}")
        p.ld(st_d[grp * 8:grp * 8 + 8, :, 768:832].rearrange("s p c -> p s c"),
             g2s[:].rearrange("p (s c) -> p s c", s=8), engine="sp",
             tag=f"g2w{grp}")
    p.pop_scope()

    p.push_scope()
    rc = [p.sbuf(f"rc{i}", [16, REC], BF16) for i in range(2)]
    fmq = [p.sbuf(f"fmq{i}", [128, 512], BF16) for i in range(4)]
    fmk = [p.sbuf(f"fmk{i}", [128, 512], BF16) for i in range(4)]
    W1p = [p.sbuf(f"W1p{i}", [128, 128], BF16) for i in range(2)]
    b1rowp = [p.sbuf(f"b1rowp{i}", [1, 128], F32) for i in range(2)]
    b1rbp = [p.sbuf(f"b1rbp{i}", [1, 128], BF16) for i in range(2)]
    sqsp = [p.sbuf(f"sqsp{i}", [16, 128], F32) for i in range(2)]
    zcp = [p.sbuf(f"zcp{i}", [16, 128], F32) for i in range(2)]
    xhp = [p.sbuf(f"xhp{i}", [16, 128], F32) for i in range(2)]
    gxhp = [p.sbuf(f"gxhp{i}", [16, 128], F32) for i in range(2)]
    prodp = [p.sbuf(f"prodp{i}", [16, 128], F32) for i in range(2)]
    tvecp = [p.sbuf(f"tvecp{i}", [16, 128], F32) for i in range(2)]
    grdp = [p.sbuf(f"grdp{i}", [16, 128], F32) for i in range(2)]
    grad2p = [p.sbuf(f"grad2p{i}", [16, 128], BF16) for i in range(2)]
    gradLp = [p.sbuf(f"gradLp{i}", [16, 128], BF16) for i in range(2)]
    zbbp = [p.sbuf(f"zbbp{i}", [16, 128], BF16) for i in range(2)]
    psZrp = [p.sbuf(f"psZrp{i}", [16, 128], F32) for i in range(2)]
    sredp = [p.sbuf(f"sredp{i}", [16, 6], F32) for i in range(2)]
    psZp = [p.psum(f"psZp{i}", [16, 128], F32) for i in range(2)]
    psBp = [p.psum(f"psBp{i}", [33, 128], F32) for i in range(2)]
    psUp = [p.psum(f"psUp{i}", [128, 128], F32) for i in range(2)]
    for pr in range(2):
        p.memset(W1p[pr][:], 0.0, tag=f"w1z{pr}")
        for j in range(2):
            h = 2 * pr + j
            p.ld(W1p[pr][64 * j:64 * j + 64, 64 * j:64 * j + 64],
                 w1in[64 * h:64 * (h + 1), :], engine="pool", tag=f"w1i{h}")
        p.ld(b1rowp[pr][:], b1in[:, 128 * pr:128 * pr + 128], tag=f"b1i{pr}")
        p.copy(b1rbp[pr][:], b1rowp[pr][:], tag=f"b1ib{pr}")

    def bv2(ap):
        return ap.broadcast_to([16, 2, 64])

    def h3(ap):
        return ap.rearrange("p (h d) -> p h d", h=2)

    for s in range(NS):
        rb = rc[s % 2]
        p.ld(rb[:], st_d[s, :, :], tag=f"s{s}rc")
        if s % 32 == 0:
            g = (s // 32) % 2
            ssl = slice(s * 16, s * 16 + 512)
            p.ld(fmq[2 * g][:], qf_d[0:128, ssl], engine="pool", tag=f"s{s}fq0")
            p.ld(fmq[2 * g + 1][:], qf_d[128:256, ssl], engine="pool",
                 tag=f"s{s}fq1")
            p.ld(fmk[2 * g][:], kf_d[0:128, ssl], engine="pool", tag=f"s{s}fk0")
            p.ld(fmk[2 * g + 1][:], kf_d[128:256, ssl], engine="pool",
                 tag=f"s{s}fk1")
        g = (s // 32) % 2
        lsl = slice((s % 32) * 16, (s % 32) * 16 + 16)
        C0 = [0, 128]
        for pr in range(2):
            p.mm(psZp[pr][0:16, :], fmk[2 * g + pr][:, lsl], W1p[pr][:],
                 start=True, stop=False, tag=f"s{s}z{pr}")
            p.mm(psZp[pr][0:16, :], ones1b[:], b1rbp[pr][:],
                 start=False, stop=True, tag=f"s{s}zb{pr}")
        for pr in range(2):
            mu = sredp[pr][:, 0:2]
            p.red(mu, h3(psZp[pr][0:16, :]), ALU.add, tag=f"s{s}mu{pr}")
        for pr in range(2):
            p.copy(psZrp[pr][:], psZp[pr][0:16, :], tag=f"s{s}ev{pr}")
        for pr in range(2):
            p.tt(sqsp[pr][:], psZrp[pr][:], psZrp[pr][:], ALU.mult,
                 tag=f"s{s}sq{pr}")
        for pr in range(2):
            p.red(sredp[pr][:, 2:4], h3(sqsp[pr][:]), ALU.add,
                  tag=f"s{s}ss{pr}")
        for pr in range(2):
            mu = sredp[pr][:, 0:2]
            p.ts(mu, mu, 1.0 / 64, ALU.mult, tag=f"s{s}mu2{pr}")
        for pr in range(2):
            mu = sredp[pr][:, 0:2]
            p.tt(sredp[pr][:, 4:6], mu, mu, ALU.mult, tag=f"s{s}m2{pr}")
        for pr in range(2):
            ssq, m2 = sredp[pr][:, 2:4], sredp[pr][:, 4:6]
            p.raw("dve", lambda e, a=ssq, b=m2, v=ssq:
                  e.scalar_tensor_tensor(out=v, in0=a, scalar=1.0 / 64, in1=b,
                                         op0=ALU.mult, op1=ALU.subtract),
                  [ssq, m2], [ssq], tag=f"s{s}var{pr}")
        for pr in range(2):
            var, rstd = sredp[pr][:, 2:4], sredp[pr][:, 4:6]
            p.act(rstd, var, AF.Sqrt, bias=1e-6, tag=f"s{s}sd{pr}")
        for pr in range(2):
            rstd = sredp[pr][:, 4:6]
            p.recip(rstd, rstd, tag=f"s{s}rs{pr}")
        for pr in range(2):
            mu = sredp[pr][:, 0:2]
            p.raw("dve", lambda e, z=zcp[pr][:], ps=psZp[pr][0:16, :],
                  m=bv2(mu):
                  e.tensor_tensor(out=z, in0=ps, in1=m, op=ALU.subtract),
                  [psZp[pr][:], mu], [zcp[pr][:]], tag=f"s{s}zc{pr}")
        for pr in range(2):
            rstd = sredp[pr][:, 4:6]
            p.tt(h3(xhp[pr][:]), h3(zcp[pr][:]), bv2(rstd), ALU.mult,
                 engine="pool", tag=f"s{s}xh{pr}")
        for pr in range(2):
            p.tt(gxhp[pr][:], xhp[pr][:], G2R16[:, C0[pr]:C0[pr] + 128],
                 ALU.mult, tag=f"s{s}g1{pr}")
        for pr in range(2):
            p.tt(gxhp[pr][:], gxhp[pr][:],
                 rb[:, 512 + C0[pr]:512 + C0[pr] + 128], ALU.add,
                 engine="pool", tag=f"s{s}g2{pr}")
        for pr in range(2):
            p.red(sredp[pr][:, 0:2], h3(gxhp[pr][:]), ALU.add,
                  tag=f"s{s}sg{pr}")
        for pr in range(2):
            p.tt(prodp[pr][:], gxhp[pr][:], xhp[pr][:], ALU.mult,
                 engine="pool", tag=f"s{s}pr{pr}")
        for pr in range(2):
            p.red(sredp[pr][:, 2:4], h3(prodp[pr][:]), ALU.add,
                  tag=f"s{s}sgx{pr}")
        for pr in range(2):
            p.ts(sredp[pr][:, 0:4], sredp[pr][:, 0:4], 1.0 / 64, ALU.mult,
                 tag=f"s{s}m14{pr}")
        for pr in range(2):
            p.tt(h3(tvecp[pr][:]), h3(xhp[pr][:]), bv2(sredp[pr][:, 2:4]),
                 ALU.mult, tag=f"s{s}t1{pr}")
        for pr in range(2):
            p.tt(tvecp[pr][:], gxhp[pr][:], tvecp[pr][:], ALU.subtract,
                 engine="pool", tag=f"s{s}t2{pr}")
        for pr in range(2):
            p.raw("dve", lambda e, o_=h3(tvecp[pr][:]), a=h3(tvecp[pr][:]),
                  b=bv2(sredp[pr][:, 0:2]):
                  e.tensor_tensor(out=o_, in0=a, in1=b, op=ALU.subtract),
                  [tvecp[pr][:], sredp[pr][:, 0:2]], [tvecp[pr][:]],
                  tag=f"s{s}t3{pr}")
        for pr in range(2):
            p.tt(h3(grdp[pr][:]), h3(tvecp[pr][:]), bv2(sredp[pr][:, 4:6]),
                 ALU.mult, engine="pool", tag=f"s{s}gr{pr}")
        for pr in range(2):
            p.tt(h3(grad2p[pr][:]), h3(grdp[pr][:]),
                 rb[:, 832 + 2 * pr:832 + 2 * pr + 2].broadcast_to([16, 2, 64]),
                 ALU.mult, tag=f"s{s}g2b{pr}")
        for pr in range(2):
            p.ts(gradLp[pr][:], grad2p[pr][:], ti15[:], ALU.mult,
                 tag=f"s{s}gL{pr}")
        for pr in range(2):
            p.mm(psBp[pr][0:16, :], fmq[2 * g + pr][:, lsl], W1p[pr][:],
                 start=True, stop=False, tag=f"s{s}q{pr}")
        for pr in range(2):
            for j in range(2):
                h = 2 * pr + j
                p.mm(psBp[pr][0:16, 64 * j:64 * j + 64],
                     rb[:, 768 + 16 * h:768 + 16 * h + 16],
                     grad2p[pr][:, 64 * j:64 * j + 64],
                     start=False, stop=False, tag=f"s{s}G{h}")
        for pr in range(2):
            p.mm(psBp[pr][0:16, :], ones1b[:], b1rbp[pr][:],
                 start=False, stop=True, tag=f"s{s}bb{pr}")
            p.mm(psBp[pr][32:33, :], onescb[:], gradLp[pr][:],
                 start=True, stop=True, tag=f"s{s}b1{pr}")
        for pr in range(2):
            p.copy(zbbp[pr][:], psBp[pr][0:16, :], tag=f"s{s}ze{pr}")
        for pr in range(2):
            p.ld(zb_d[s, :, C0[pr]:C0[pr] + 128], zbbp[pr][:], engine="act",
                 tag=f"s{s}zw{pr}")
        for pr in range(2):
            for j in range(2):
                h = 2 * pr + j
                qs = slice(64 * j, 64 * j + 64)
                p.mm(psUp[pr][qs, qs],
                     rb[:, 256 + 64 * h:256 + 64 * h + 64],
                     gradLp[pr][:, 64 * j:64 * j + 64],
                     start=True, stop=True, tag=f"s{s}w{h}")
        for pr in range(2):
            for j in range(2):
                qs = slice(64 * j, 64 * j + 64)
                p.raw("dve", lambda e, w=W1p[pr][qs, qs], ps=psUp[pr][qs, qs]:
                      e.tensor_tensor(out=w, in0=w, in1=ps, op=ALU.subtract),
                      [W1p[pr][:], psUp[pr][:]], [W1p[pr][:]],
                      tag=f"s{s}wu{pr}_{j}")
        for pr in range(2):
            p.raw("dve", lambda e, b=b1rowp[pr][:], ps=psBp[pr][32:33, :]:
                  e.tensor_tensor(out=b, in0=b, in1=ps, op=ALU.subtract),
                  [b1rowp[pr][:], psBp[pr][:]], [b1rowp[pr][:]],
                  tag=f"s{s}bu{pr}")
        for pr in range(2):
            p.copy(b1rbp[pr][:], b1rowp[pr][:], engine="pool",
                   tag=f"s{s}bc{pr}")
    p.pop_scope()

    p.push_scope()
    zt = p.sbuf("zt", [128, 256], BF16)
    xqt = p.sbuf("xqt", [128, 256], BF16)
    sq6 = p.sbuf("sq6", [128, 256], F32)
    n6 = p.sbuf("n6", [128, 256], F32)
    ob6 = p.sbuf("ob6", [128, 256], BF16)
    sr6 = p.sbuf("sr6", [128, 12], F32)
    grep2 = p.sbuf("grep2", [128, 256], F32)
    brep2 = p.sbuf("brep2", [128, 256], F32)
    p.mm(psW[:, 0:256], onesr[:], grow[:], tag="g6a")
    p.copy(grep2[:], psW[:, 0:256], tag="g6b")
    p.mm(psW[:, 0:256], onesr[:], browf[:], tag="g6c")
    p.copy(brep2[:], psW[:, 0:256], tag="g6d")
    p.memset(ob6[:], 0.0, tag="ob6z")
    for t in range(nT):
        tsl = slice(t * 128, (t + 1) * 128)
        p.ld(o[tsl, :], ob6[:], engine="pool", tag=f"dd{t}st")
        p.ld(zt[:], zb_d.rearrange("s p c -> (s p) c")[tsl, :], tag=f"d{t}z")
        p.ld(xqt[:], xqa_d[tsl, :], tag=f"d{t}x")
        mu6, ss6, rs6 = sr6[:, 0:4], sr6[:, 4:8], sr6[:, 8:12]
        if cut < 7.2:
            continue
        p.red(mu6, zt[:].rearrange("p (h d) -> p h d", h=4), ALU.add,
              tag=f"d{t}mu")
        p.act(sq6[:], zt[:], AF.Square, tag=f"d{t}sq")
        p.red(ss6, sq6[:].rearrange("p (h d) -> p h d", h=4), ALU.add,
              tag=f"d{t}ss")
        p.ts(mu6, mu6, 1.0 / 64, ALU.mult, tag=f"d{t}m")
        p.tt(sr6[:, 8:12], mu6, mu6, ALU.mult, tag=f"d{t}m2")
        p.raw("dve", lambda e, a=ss6, b=sr6[:, 8:12], v=ss6:
              e.scalar_tensor_tensor(out=v, in0=a, scalar=1.0 / 64, in1=b,
                                     op0=ALU.mult, op1=ALU.subtract),
              [ss6, sr6[:, 8:12]], [ss6], tag=f"d{t}v")
        if cut < 7.4:
            continue
        p.act(rs6, ss6, AF.Sqrt, bias=1e-6, tag=f"d{t}sd")
        p.recip(rs6, rs6, tag=f"d{t}rs")
        if cut < 7.6:
            continue
        bv6 = lambda ap: ap.broadcast_to([128, 4, 64])
        p.raw("dve", lambda e, o_=n6[:].rearrange("p (h d) -> p h d", h=4),
              a=zt[:].rearrange("p (h d) -> p h d", h=4), b=bv6(mu6):
              e.tensor_tensor(out=o_, in0=a, in1=b, op=ALU.subtract),
              [zt[:], mu6], [n6[:]], tag=f"d{t}zc")
        p.tt(n6[:].rearrange("p (h d) -> p h d", h=4),
             n6[:].rearrange("p (h d) -> p h d", h=4), bv6(rs6), ALU.mult,
             engine="pool", tag=f"d{t}xh")
        if cut < 7.8:
            continue
        p.tt(n6[:], n6[:], grep2[:], ALU.mult, tag=f"d{t}g")
        if cut < 7.75:
            continue
        p.tt(n6[:], n6[:], brep2[:], ALU.add, engine="pool", tag=f"d{t}b")
        if cut < 7.85:
            continue
        p.raw("dve", lambda e, o_=ob6[:], a=n6[:], b=xqt[:]:
              e.tensor_tensor(out=o_, in0=a, in1=b, op=ALU.add),
              [n6[:], xqt[:]], [ob6[:]], tag=f"d{t}o")
        p.ld(o[tsl, :], ob6[:], engine="pool", tag=f"d{t}st")
    p.pop_scope()
    p.build()
    return nc



BIG = 1e9


def pack_win(W_in, core):
    z = W_in[512 * core:512 * (core + 1)]
    x = W_in[4096 + 512 * core:4096 + 512 * (core + 1)]
    B = W_in[8192:8256]
    C = W_in[8256:8320]
    dt = W_in[8320 + 8 * core:8320 + 8 * (core + 1)]
    pad = np.zeros((24, 2048), np.float32)
    return np.ascontiguousarray(np.concatenate([z, x, B, C, dt, pad], 0).T)


def pack_convw_l1b(conv_w, conv_b, core):
    xw = conv_w[512 * core:512 * (core + 1)]
    cw = np.concatenate([xw, conv_w[4096:4224]], 0)
    xb = conv_b[512 * core:512 * (core + 1)]
    cb = np.concatenate([xb, conv_b[4096:4224]], 0)
    return np.ascontiguousarray(cw), np.ascontiguousarray(cb)


def build_l1b(S=8192, D=2048):
    nc = bass.Bass("TRN2", target_bir_lowering=False)
    ln = nc.dram_tensor("ln", [S, D], BF16, kind="ExternalInput").ap()
    win = nc.dram_tensor("win", [D, 1184], F32, kind="ExternalInput").ap()
    convw = nc.dram_tensor("convw", [640, 4], F32, kind="ExternalInput").ap()
    convb = nc.dram_tensor("convb", [640], F32, kind="ExternalInput").ap()
    dtb = nc.dram_tensor("dtb", [8], F32, kind="ExternalInput").ap()
    alog = nc.dram_tensor("alog", [8], F32, kind="ExternalInput").ap()
    dpar = nc.dram_tensor("dpar", [1, 8], F32, kind="ExternalInput").ap()
    u = nc.dram_tensor("u", [S, 512], BF16, kind="ExternalOutput").ap()
    pre_d = nc.dram_tensor("pre_d", [640, S], BF16, kind="Internal").ap()
    xc_d = nc.dram_tensor("xc_d", [640, S], BF16, kind="Internal").ap()
    zT_d = nc.dram_tensor("zT_d", [512, S], BF16, kind="Internal").ap()
    dtr_d = nc.dram_tensor("dtr_d", [8, S], F32, kind="Internal").ap()
    rows_dm = nc.dram_tensor("rows_dm", [16, S], F32, kind="Internal").ap()

    p = Prog(nc)
    nT = S // 128
    nP = S // 512
    BL = min(2048, S)
    NB = S // BL
    nC = S // 64

    glt = p.sbuf("glt", [128, nT * 8], F32)
    wco = p.sbuf("wco", [128, nT * 8], F32)
    egc = p.sbuf("egc", [128, nT * 8], F32)
    gaml = p.sbuf("gaml", [64, nC * 8], F32)
    esc = p.sbuf("esc", [8, nC], F32)
    identT = p.sbuf("identT", [128, 128], F32)
    p.memset(identT[:], 1.0, tag="idT0")
    p.affine_select(identT[:], identT[:], pattern=[[-1, 128]],
                    compare_op=ALU.is_equal, fill=0.0, base=0,
                    channel_multiplier=1, tag="idT1")
    identb = p.sbuf("identb", [128, 128], BF16)
    p.copy(identb[:], identT[:], tag="idb")
    bdU = p.sbuf("bdU", [128, 128], BF16)
    p.memset(bdU[:], 1.0, tag="bdU0")
    p.affine_select(bdU[:], bdU[:], pattern=[[1, 128]], compare_op=ALU.is_ge,
                    fill=0.0, base=0, channel_multiplier=-1, tag="bdU1")
    p.memset(bdU[0:64, 64:128], 0.0, tag="bdU2")
    bdA = p.sbuf("bdA", [128, 128], BF16)
    p.memset(bdA[:], 1.0, tag="bdA0")
    p.memset(bdA[0:64, 64:128], 0.0, tag="bdA1")
    p.memset(bdA[64:128, 0:64], 0.0, tag="bdA2")
    PNi = p.sbuf("PNi", [128, 128], BF16)
    p.memset(PNi[:], 0.0, tag="PNi0")
    p.affine_select(PNi[:], PNi[:], pattern=[[1, 128]], compare_op=ALU.is_ge,
                    fill=-BIG, base=0, channel_multiplier=-1, tag="PNi1")
    p.memset(PNi[0:64, 64:128], -BIG, tag="PNi2")
    ind8 = p.sbuf("ind8", [8, 8 * 128], F32)
    p.memset(ind8[:], 1.0, tag="i80")
    p.affine_select(ind8[:].rearrange("p (b f) -> p b f", b=8),
                    ind8[:].rearrange("p (b f) -> p b f", b=8),
                    pattern=[[-1, 8], [0, 128]], compare_op=ALU.is_equal,
                    fill=0.0, base=0, channel_multiplier=1, tag="i81")
    ind8s = p.sbuf("ind8s", [8, 8], F32)
    p.memset(ind8s[:], 1.0, tag="i8s0")
    p.affine_select(ind8s[:], ind8s[:], pattern=[[-1, 8]],
                    compare_op=ALU.is_equal, fill=0.0, base=0,
                    channel_multiplier=1, tag="i8s1")
    ones8c = p.sbuf("ones8c", [8, 64], F32)
    p.memset(ones8c[:], 1.0, tag="ones8c")
    onesr = p.sbuf("onesr", [1, 128], F32)
    p.memset(onesr[:], 1.0, tag="onesr")

    p.push_scope()
    lnp = [[p.sbuf(f"lnp{b}_{i}", [128, 512], BF16) for i in range(16)]
           for b in range(2)]
    wt = [p.sbuf(f"wt{i}", [128, 1184], BF16) for i in range(16)]
    for i in range(16):
        p.ld(wt[i][:], win[i * 128:(i + 1) * 128, :], engine="pool",
             tag=f"wld{i}")
    psE = [p.psum(f"psE{i}", [128, 512], F32) for i in range(2)]
    evb = [p.sbuf(f"evb{i}", [128, 512], BF16) for i in range(2)]
    std_ = p.sbuf("std_", [8, 512], F32)
    for n in range(nP):
        b = n % 2
        sl = slice(n * 512, (n + 1) * 512)
        for i in range(16):
            p.ldT(lnp[b][i][:], ln[sl, i * 128:(i + 1) * 128], engine="act",
                  tag=f"lnp{n}_{i}")
        for m in range(10):
            M0, M1 = m * 128, min(1184, (m + 1) * 128)
            j = m % 2
            for k in range(16):
                p.mm(psE[j][0:M1 - M0, :], wt[k][:, M0:M1], lnp[b][k][:],
                     start=(k == 0), stop=(k == 15), tag=f"pj{n}_{m}_{k}")
            if m < 4:
                p.copy(evb[j][:], psE[j][:], tag=f"pje{n}_{m}")
                p.ld(zT_d[M0:M0 + 128, sl], evb[j][:], engine="sp",
                     tag=f"pjs{n}_{m}")
            elif m < 9:
                p.copy(evb[j][:], psE[j][:], tag=f"pje{n}_{m}")
                p.ld(pre_d[M0 - 512:M0 - 384, sl], evb[j][:], engine="sp",
                     tag=f"pjs{n}_{m}")
            else:
                p.copy(std_[:], psE[j][0:8, :], tag=f"pjd{n}")
                p.ld(dtr_d[:, sl], std_[:], engine="sp", tag=f"pds{n}")
    p.pop_scope()

    p.push_scope()
    cw = p.sbuf("cw", [128, 4 * 5], F32)
    cb = p.sbuf("cb", [128, 5], F32)
    for t in range(5):
        p.ld(cw[:, t * 4:(t + 1) * 4], convw[t * 128:(t + 1) * 128, :],
             engine="pool", tag=f"cwld{t}")
    p.ld(cb[:], convb.rearrange("(a b) -> b a", b=128), engine="pool",
         tag="cbld", allow_slow_non_contiguous=True)
    cin = p.sbuf("cin", [128, S + 3], BF16)
    cacc = [p.sbuf(f"cacc{i}", [128, S], F32) for i in range(2)]
    cout = p.sbuf("cout", [128, S], BF16)
    p.memset(cin[:, 0:3], 0.0, tag="cpad")
    for t in range(5):
        p.ld(cin[:, 3:S + 3], pre_d[t * 128:(t + 1) * 128, :], tag=f"cld{t}")
        p.ts(cacc[0][:], cin[:, 0:S], cw[:, t * 4:t * 4 + 1], ALU.mult,
             tag=f"cv{t}_0")
        p.stt(cacc[1][:], cin[:, 1:S + 1], cw[:, t * 4 + 1:t * 4 + 2],
              cacc[0][:], ALU.mult, ALU.add, tag=f"cv{t}_1")
        p.stt(cacc[0][:], cin[:, 2:S + 2], cw[:, t * 4 + 2:t * 4 + 3],
              cacc[1][:], ALU.mult, ALU.add, tag=f"cv{t}_2")
        p.stt(cacc[1][:], cin[:, 3:S + 3], cw[:, t * 4 + 3:t * 4 + 4],
              cacc[0][:], ALU.mult, ALU.add, tag=f"cv{t}_3")
        p.act(cout[:], cacc[1][:], AF.Silu, bias=cb[:, t:t + 1], tag=f"cvs{t}")
        p.ld(xc_d[t * 128:(t + 1) * 128, :], cout[:], engine="sp",
             tag=f"cst{t}")
    p.pop_scope()

    gcbm = p.sbuf("gcbm", [8, BL], F32, side="right")
    zrow = p.sbuf("zrow", [8, BL], F32, side="right")
    p.push_scope()
    dtraw = p.sbuf("dtraw", [8, BL], F32)
    tr1 = p.sbuf("tr1", [8, BL], F32)
    dt_r = p.sbuf("dt_r", [8, BL], F32)
    lndt_r = p.sbuf("lndt_r", [8, BL], F32)
    adt_r = p.sbuf("adt_r", [8, BL], F32)
    rtp = p.sbuf("rtp", [16, BL], F32)
    adtb = p.sbuf("adtb", [128, 8], BF16)
    dtbt = p.sbuf("dtbt", [8, 1], F32)
    alogt = p.sbuf("alogt", [8, 1], F32)
    p.memset(zrow[:], 0.0, tag="zrow")
    p.ld(dtbt[:], dtb.rearrange("(a b) -> a b", b=1), tag="dtbld")
    p.ld(alogt[:], alog.rearrange("(a b) -> a b", b=1), tag="alogld")
    p.act(alogt[:], alogt[:], AF.Exp, tag="ea")
    p.ts(alogt[:], alogt[:], -1.0, ALU.mult, tag="nega")
    psT3 = p.psum("psT3", [128, 16], F32)
    psC3 = p.psum("psC3", [128, 16], F32)
    psR8 = p.psum("psR8", [8, 128], F32)
    for blk in range(NB):
        bsl = slice(blk * BL, (blk + 1) * BL)
        nt0 = blk * (BL // 128)
        p.ld(dtraw[:], dtr_d[:, bsl], tag=f"m3d{blk}")
        p.ts(dtraw[:], dtraw[:], dtbt[:], ALU.add, tag=f"m3a{blk}")
        p.act(tr1[:], dtraw[:], AF.Abs, tag=f"m3b{blk}")
        p.act(tr1[:], tr1[:], AF.Exp, scale=-1.0, tag=f"m3c{blk}")
        p.act(tr1[:], tr1[:], AF.Ln, bias=1.0, tag=f"m3e{blk}")
        p.ts(dt_r[:], dtraw[:], 0.0, ALU.max, tag=f"m3f{blk}")
        p.tt(dt_r[:], dt_r[:], tr1[:], ALU.add, tag=f"m3g{blk}")
        p.act(lndt_r[:], dt_r[:], AF.Ln, tag=f"m3h{blk}")
        p.ts(adt_r[:], dt_r[:], alogt[:], ALU.mult, tag=f"m3i{blk}")
        p.ld(rtp[0:8, :], adt_r[:], engine="pool", tag=f"m3p1{blk}")
        p.ld(rtp[8:16, :], lndt_r[:], engine="pool", tag=f"m3p2{blk}")
        for tt_ in range(BL // 128):
            t = nt0 + tt_
            lsl = slice(tt_ * 128, (tt_ + 1) * 128)
            tc = slice(t * 8, (t + 1) * 8)
            p.transp(psT3[:], rtp[:, lsl], identT[0:16, 0:16],
                     tag=f"m3t{blk}_{tt_}")
            p.copy(glt[:, tc], psT3[:, 8:16], tag=f"m3lt{blk}_{tt_}")
            p.copy(adtb[:], psT3[:, 0:8], tag=f"m3ab{blk}_{tt_}")
            p.mm(psC3[:, 0:8], bdU[:], adtb[:], tag=f"m3c1_{blk}_{tt_}")
            p.mm(psC3[:, 8:16], bdA[:], adtb[:], tag=f"m3c2_{blk}_{tt_}")
            p.copy(egc[:, tc], psC3[:, 0:8], tag=f"m3gc{blk}_{tt_}")
            p.raw("dve", lambda e, o_=glt[:, tc], a=glt[:, tc], b=egc[:, tc]:
                  e.tensor_tensor(out=o_, in0=a, in1=b, op=ALU.subtract),
                  [glt[:, tc], egc[:, tc]], [glt[:, tc]], tag=f"m3gl{blk}_{tt_}")
            p.raw("dve", lambda e, o_=wco[:, tc], a=psC3[:, 8:16], b=glt[:, tc]:
                  e.tensor_tensor(out=o_, in0=a, in1=b, op=ALU.add),
                  [psC3[:], glt[:, tc]], [wco[:, tc]], tag=f"m3wc{blk}_{tt_}")
            p.mm(psR8[:], adtb[:], bdU[:], tag=f"m3gt{blk}_{tt_}")
            p.copy(gcbm[:, lsl], psR8[:], tag=f"m3ge{blk}_{tt_}")
            p.mm(psR8[:], adtb[:], bdA[:], tag=f"m3cs{blk}_{tt_}")
            p.copy(esc[:, t * 2:t * 2 + 2],
                   psR8[:].rearrange("p (a c) -> p a c", a=2)[:, :, 63],
                   tag=f"m3ce{blk}_{tt_}")
        p.ld(rows_dm[0:8, bsl], gcbm[:], engine="sp", tag=f"m3r0{blk}")
        p.ld(rows_dm[8:16, bsl], zrow[:], engine="sp", tag=f"m3r1{blk}")
    p.act(egc[:], egc[:], AF.Exp, tag="egce")
    p.act(wco[:], wco[:], AF.Exp, tag="wcoe")
    escm = p.sbuf("escm", [8, nC * 8], F32)
    p.raw("dve",
          lambda e: e.tensor_tensor(
              out=escm[:].rearrange("p (c h) -> p c h", h=8),
              in0=esc[:].broadcast_to([8, nC, 8]).transpose([0, 1, 2])
              if False else esc[:].rearrange("p c -> p c").broadcast_to([8, nC, 8]),
              in1=ind8s[:].rearrange("p h -> p h").broadcast_to([8, 8, nC])
              .transpose([0, 2, 1]),
              op=ALU.mult),
          [esc[:], ind8s[:]], [escm[:]], tag="escm")
    for i in range((nC * 8) // 512):
        p.mm(psE[0][0:64, :] if False else psC3[0:64, 0:16], None, None,
             tag="") if False else None
    psG4 = p.psum("psG4", [64, 512], F32)
    for i in range(max(1, (nC * 8) // 512)):
        w_ = min(512, nC * 8 - i * 512)
        p.mm(psG4[:, 0:w_], ones8c[:], escm[:, i * 512:i * 512 + w_],
             tag=f"gm{i}")
        p.act(gaml[:, i * 512:i * 512 + w_], psG4[:, 0:w_], AF.Exp,
              tag=f"gme{i}")
    p.pop_scope()

    p.push_scope()
    xall = p.sbuf("xall", [128, 640], BF16)
    ztm = p.sbuf("ztm", [128, 512], BF16)
    bfm = p.sbuf("bfm", [64, 128], BF16)
    cfm = p.sbuf("cfm", [64, 128], BF16)
    rstg = [p.sbuf(f"rstg{i}", [16, 1024], F32) for i in range(2)]
    expw = p.sbuf("expw", [128, 128], BF16)
    Mt = p.sbuf("Mt", [128, 128], BF16)
    btl = p.sbuf("btl", [128, 512], BF16)
    NST = p.sbuf("NST", [64, 512], BF16)
    ytmp = p.sbuf("ytmp", [128, 512], F32)
    yasm = p.sbuf("yasm", [128, 512], BF16)
    Drep = p.sbuf("Drep", [128, 512], F32)
    d512 = p.sbuf("d512", [1, 512], F32)
    sz = p.sbuf("sz", [128, 512], F32)
    xd = p.sbuf("xd", [128, 512], F32)
    ub = p.sbuf("ub", [128, 512], BF16)
    psCB = p.psum("psCB", [128, 128], F32)
    psP = p.psum("psP", [128, 128], F32)
    psYd = p.psum("psYd", [128, 512], F32)
    psYo = p.psum("psYo", [128, 512], F32)
    psS = [p.psum(f"psS{i}", [64, 64], F32) for i in range(2)]
    psD = p.psum("psD", [128, 512], F32)
    drow = p.sbuf("drow", [1, 8], F32)
    p.ld(drow[:], dpar[:], tag="dld")
    p.raw("dve",
          lambda e: e.tensor_copy(
              out=d512[:].rearrange("a (h c) -> a h c", c=64),
              in_=drow[:].rearrange("a h -> a h").broadcast_to([1, 8, 64])),
          [drow[:]], [d512[:]], tag="d512")
    p.mm(psD[:], onesr[:], d512[:], tag="db1")
    p.copy(Drep[:], psD[:], tag="db2")
    p.memset(NST[:], 0.0, tag="nst0")
    RG = min(8, nT)
    for t in range(nT):
        tsl = slice(t * 128, (t + 1) * 128)
        if t % RG == 0:
            rstgb = rstg[(t // RG) % 2]
            p.ld(rstgb[:, 0:RG * 128], rows_dm[:, t * 128:(t + RG) * 128],
                 engine="pool", tag=f"m4r{t}")
        rsl = rstg[(t // RG) % 2][0:8, (t % RG) * 128:(t % RG + 1) * 128]
        p.ldT(xall[:], xc_d[:, tsl], engine="act", tag=f"m4x{t}")
        p.ldT(ztm[:], zT_d[:, tsl], engine="act", tag=f"m4z{t}")
        p.ld(bfm[:], xc_d[512:576, tsl], tag=f"m4b{t}")
        p.ld(cfm[:], xc_d[576:640, tsl], tag=f"m4c{t}")
        p.mm(psCB[:], bfm[:], cfm[:], tag=f"m4cb{t}")
        p.raw("dve",
              lambda e, o_=btl[:].rearrange("p (h n) -> p h n", h=8),
              a=xall[:, 512:576].rearrange("p n -> p n")
              .broadcast_to([128, 64, 8]).transpose([0, 2, 1]),
              b=wco[:, t * 8:(t + 1) * 8].rearrange("p h -> p h")
              .broadcast_to([128, 8, 64]):
              e.tensor_tensor(out=o_, in0=a, in1=b, op=ALU.mult),
              [xall[:], wco[:]], [btl[:]], tag=f"m4bt{t}")
        for h in range(8):
            hsl = slice(64 * h, 64 * h + 64)
            p.mm(psP[:], ind8[:, h * 128:(h + 1) * 128], rsl,
                 start=True, stop=False, tag=f"m4p{t}_{h}")
            p.mm(psP[:], identb[:], PNi[:], start=False, stop=True,
                 tag=f"m4p2{t}_{h}")
            p.act(expw[:], psP[:], AF.Exp,
                  bias=glt[:, t * 8 + h:t * 8 + h + 1], tag=f"m4e{t}_{h}")
            p.tt(Mt[:], expw[:], psCB[:], ALU.mult, tag=f"m4M{t}_{h}")
            p.mm(psYd[:, hsl], Mt[:], xall[:, hsl], start=True, stop=True,
                 tag=f"m4yd{t}_{h}")
            for half in range(2):
                c = 2 * t + half
                hs = slice(64 * half, 64 * half + 64)
                jS = psS[half % 2]
                p.mm(psYo[hs, hsl], cfm[:, hs], NST[:, hsl],
                     start=True, stop=True, tag=f"m4yo{c}_{h}")
                p.mm(jS[:], btl[hs, hsl], xall[hs, hsl], tag=f"m4st{c}_{h}")
                p.raw("dve",
                      lambda e, o_=NST[:, hsl], g_=gaml[:, c * 8 + h:c * 8 + h + 1],
                      ps=jS[:]:
                      e.scalar_tensor_tensor(out=o_, in0=o_, scalar=g_, in1=ps,
                                             op0=ALU.mult, op1=ALU.add),
                      [NST[:], gaml[:], jS[:]], [NST[:]], tag=f"m4ns{c}_{h}")
        p.raw("dve",
              lambda e, o_=ytmp[:].rearrange("p (h n) -> p h n", h=8),
              a=psYo[:].rearrange("p (h n) -> p h n", h=8),
              b=egc[:, t * 8:(t + 1) * 8].rearrange("p h -> p h")
              .broadcast_to([128, 8, 64]):
              e.tensor_tensor(out=o_, in0=a, in1=b, op=ALU.mult),
              [psYo[:], egc[:]], [ytmp[:]], tag=f"m4ev{t}")
        p.raw("dve", lambda e, o_=yasm[:], a=ytmp[:], b=psYd[:]:
              e.tensor_tensor(out=o_, in0=a, in1=b, op=ALU.add),
              [ytmp[:], psYd[:]], [yasm[:]], tag=f"m4ev2{t}")
        p.act(sz[:], ztm[:], AF.Silu, tag=f"m4si{t}")
        p.tt(xd[:], xall[:, 0:512], Drep[:], ALU.mult, engine="pool",
             tag=f"m4xd{t}")
        p.raw("dve", lambda e, o_=xd[:], a=yasm[:], b=xd[:]:
              e.tensor_tensor(out=o_, in0=a, in1=b, op=ALU.add),
              [yasm[:], xd[:]], [xd[:]], tag=f"m4ya{t}")
        p.tt(ub[:], xd[:], sz[:], ALU.mult, tag=f"m4u{t}")
        p.ld(u[tsl, :], ub[:], engine="sp", tag=f"m4us{t}")
    p.pop_scope()
    p.build()
    return nc




_PROGS = {}


def _get(key, fn, *args):
    if key not in _PROGS:
        _PROGS[key] = fn(*args)
    return _PROGS[key]


def _runp(nc, maps, outname):
    res = run_bass_kernel_spmd(nc, maps, core_ids=list(range(8)))
    return [np.asarray(r[outname]) for r in res.results]


def kernel(**inp):
    S, D, R = 8192, 2048, 1024
    x = np.ascontiguousarray(np.asarray(inp["L0_attn_a_hidden_states"])[0])

    def rows(a):
        return [np.ascontiguousarray(a[i * R:(i + 1) * R]) for i in range(8)]

    def run_ln(xc):
        nl = _get("ln", build_ln, R, D)
        lns = _runp(nl, [{"x": xr} for xr in rows(xc)], "ln")
        return np.ascontiguousarray(np.concatenate(lns, 0))

    def run_oproj(xc, o_full, wT, norm, g=None, b=None):
        C = o_full.shape[1]
        key = f"oproj{C}_{norm}"
        no = _get(key, build_oproj, R, C, D, norm)
        wTc = np.ascontiguousarray(wT)
        maps = []
        for i, xr in enumerate(rows(xc)):
            m = {"o": np.ascontiguousarray(o_full[i * R:(i + 1) * R]),
                 "wT": wTc, "x": xr}
            if norm in ("lnwb", "rmsw"):
                m["g"] = np.ascontiguousarray(g)
            if norm == "lnwb":
                m["b"] = np.ascontiguousarray(b)
            maps.append(m)
        xs = _runp(no, maps, "xo")
        return np.concatenate(xs, 0)

    ln_full = run_ln(x)
    na = _get("l0a", build_l0a, S, D)
    selm = make_selm()
    maps = []
    for c in range(8):
        maps.append({
            "ln": ln_full,
            "wproj": pack_wproj(inp["L0_attn_a_W_qkv"], inp["L0_attn_a_W_z"],
                                inp["L0_attn_a_W_b"], inp["L0_attn_a_W_a"], c),
            "convw": pack_convw(inp["L0_attn_a_conv_w"], c),
            "dtb": np.ascontiguousarray(inp["L0_attn_a_dt_bias"][4 * c:4 * c + 4]),
            "alog": np.ascontiguousarray(inp["L0_attn_a_A_log"][4 * c:4 * c + 4]),
            "normw": np.ascontiguousarray(inp["L0_attn_a_norm_weight"]),
            "selm": selm,
        })
    os_ = _runp(na, maps, "o")
    o_full = np.ascontiguousarray(np.concatenate(os_, 1))
    x = run_oproj(x, o_full, inp["L0_attn_a_W_out"].T, "none")

    def run_l0f(xc):
        nf = _get("l0f", build_l0ffn, R, D, 2048, 512)
        idx = np.arange(S)
        maps = []
        masks = []
        for c in range(8):
            e, half = c // 2, c % 2
            m = (idx % 4 == e) & ((idx // 4096) == half)
            masks.append(m)
            maps.append({
                "x": np.ascontiguousarray(xc[m]),
                "gup": np.ascontiguousarray(inp["L0_ffn_gate_up_proj"][e]),
                "down": np.ascontiguousarray(inp["L0_ffn_down_proj"][e]),
            })
        xs = _runp(nf, maps, "xo")
        out = np.empty_like(xc)
        for c in range(8):
            out[masks[c]] = xs[c]
        return out

    x = run_l0f(x)

    ln_full = run_ln(x)
    nt = _get("l0b", build_l0b, S, D)
    tib = (1.0 / np.arange(1, 17, dtype=np.float32)).reshape(1, 16)
    maps = []
    for c in range(8):
        maps.append({
            "ln": ln_full,
            "wproj": pack_wproj_ttt(inp["L0_attn_b_q_w"], inp["L0_attn_b_k_w"],
                                    inp["L0_attn_b_v_w"],
                                    inp["L0_attn_b_ttt_lr_weight"], c),
            "w1in": np.ascontiguousarray(
                inp["L0_attn_b_W1"][4 * c:4 * c + 4].reshape(256, 64)),
            "b1in": np.ascontiguousarray(
                inp["L0_attn_b_b1"][4 * c:4 * c + 4].reshape(1, 256)),
            "tnw": np.ascontiguousarray(
                inp["L0_attn_b_ttt_norm_weight"][4 * c:4 * c + 4].reshape(1, 256)),
            "tnb": np.ascontiguousarray(
                inp["L0_attn_b_ttt_norm_bias"][4 * c:4 * c + 4].reshape(1, 256)),
            "tib": tib,
            "tilearn": np.ascontiguousarray(
                inp["L0_attn_b_learnable_token_idx"].reshape(1, 16)),
            "lrb": np.ascontiguousarray(
                inp["L0_attn_b_ttt_lr_bias"][4 * c:4 * c + 4, 0]),
        })
    os_ = _runp(nt, maps, "o")
    o_full = np.ascontiguousarray(np.concatenate(os_, 1))
    x = run_oproj(x, o_full, inp["L0_attn_b_o_w"].T, "lnwb",
                  g=inp["L0_attn_b_post_norm_weight"],
                  b=inp["L0_attn_b_post_norm_bias"])

    x = run_l0f(x)

    nlat = _get("mlalat", build_mla_lat, R, D)
    wk_rope = inp["L1_attn_a_wk_rope"]
    wk_rope_perm = np.ascontiguousarray(
        np.concatenate([wk_rope[:, 32:64], wk_rope[:, 0:32]], 1))
    wlat = np.ascontiguousarray(np.concatenate(
        [inp["L1_attn_a_wq_down"], inp["L1_attn_a_wkv_down"], wk_rope,
         wk_rope_perm], 1))
    lats = _runp(nlat, [{"x": xr, "wlat": wlat} for xr in rows(x)], "latT")
    latT = np.ascontiguousarray(np.concatenate(lats, 1))
    nm = _get("mla", build_mla, S, 512)
    cosT, ssinT = rope_tables(S)
    wq_up, wq_rope = inp["L1_attn_a_wq_up"], inp["L1_attn_a_wq_rope"]
    wk_up, wv_up = inp["L1_attn_a_wk_up"], inp["L1_attn_a_wv_up"]
    maps = []
    for c in range(8):
        wqr_c = np.ascontiguousarray(wq_rope[:, 128 * c:128 * c + 128])
        wqrp_c = np.ascontiguousarray(np.concatenate(
            [np.concatenate([wqr_c[:, 64 * h + 32:64 * h + 64],
                             wqr_c[:, 64 * h:64 * h + 32]], 1)
             for h in range(2)], 1))
        maps.append({
            "latT": latT,
            "wqu": np.ascontiguousarray(wq_up[:, 256 * c:256 * c + 256]),
            "wqr": wqr_c,
            "wqrp": wqrp_c,
            "wku": np.ascontiguousarray(wk_up[:, 256 * c:256 * c + 256]),
            "wvu": np.ascontiguousarray(wv_up[:, 256 * c:256 * c + 256]),
            "cosT": cosT, "ssinT": ssinT,
        })
    os_ = _runp(nm, maps, "oT")
    o_full = np.ascontiguousarray(np.concatenate(os_, 0).T)
    x = run_oproj(x, o_full, inp["L1_attn_a_wo"], "none")

    def run_l1f(xc):
        nf = _get("l1f", build_l1ffn, R, D, 512)
        w = np.ascontiguousarray(inp["L1_ffn_weight"])
        sc = np.ascontiguousarray(inp["L1_ffn_scale"])
        xs = _runp(nf, [{"x": xr, "w": w, "scale": sc} for xr in rows(xc)],
                   "xo")
        return np.concatenate(xs, 0)

    x = run_l1f(x)

    ln_full = run_ln(x)
    nb = _get("l1b", build_l1b, S, D)
    maps = []
    for c in range(8):
        cw, cb = pack_convw_l1b(inp["L1_attn_b_conv_w"],
                                inp["L1_attn_b_conv_b"], c)
        maps.append({
            "ln": ln_full,
            "win": pack_win(inp["L1_attn_b_W_in"], c),
            "convw": cw, "convb": cb,
            "dtb": np.ascontiguousarray(inp["L1_attn_b_dt_bias"][8 * c:8 * c + 8]),
            "alog": np.ascontiguousarray(inp["L1_attn_b_A_log"][8 * c:8 * c + 8]),
            "dpar": np.ascontiguousarray(
                inp["L1_attn_b_D_param"][8 * c:8 * c + 8].reshape(1, 8)),
        })
    os_ = _runp(nb, maps, "u")
    o_full = np.ascontiguousarray(np.concatenate(os_, 1))
    x = run_oproj(x, o_full, inp["L1_attn_b_W_out"].T, "rmsw",
                  g=inp["L1_attn_b_norm_w"])

    x = run_l1f(x)

    return np.ascontiguousarray(x.astype(np.float32)).reshape(1, S, D)
